# version 03
# baseline (speedup 1.0000x reference)
import math
from contextlib import ExitStack
import numpy as np
import ml_dtypes
import concourse.bass as bass
import concourse.mybir as mybir
from concourse.bass_utils import run_bass_kernel_spmd

F32 = mybir.dt.float32
BF16 = mybir.dt.bfloat16
I32 = mybir.dt.int32
AF = mybir.ActivationFunctionType
ALU = mybir.AluOpType
AX = mybir.AxisListType

D = 2048
H = 16
TOWN = 1024
TSEQ = 2048
NT = TOWN // 128
NTS = TSEQ // 128
NI = D // 128

TWO_PI = 6.283185307179586
INV_2PI = 1.0 / TWO_PI
C_HI = 6.28125
C_LO = 0.00193530717958647692
PI_C = float(np.nextafter(np.float32(np.pi), np.float32(0.0)))
HALF_PI = float(np.float32(np.pi / 2))
ATTN_SCALE = 1.0 / math.sqrt(128.0)


_TRKN = [0]


class Trk:
    def __init__(self, nc, name):
        _TRKN[0] += 1
        self.h = nc.alloc_semaphore("%s_%d" % (name, _TRKN[0]))
        self.n = 0

    def inc(self, instr, k=1):
        instr.then_inc(self.h, k)
        self.n += k
        return self.n

    def wait(self, eng, val=None):
        eng.wait_ge(self.h, self.n if val is None else val)


class Ctx:
    def __init__(self, nc):
        self.nc = nc
        self.pe = Trk(nc, "t_pe")
        self.act = Trk(nc, "t_act")
        self.dve = Trk(nc, "t_dve")
        self.pool = Trk(nc, "t_pool")
        self.ld = Trk(nc, "t_ld")
        self.ld2 = Trk(nc, "t_ld2")
        self.ld3 = Trk(nc, "t_ld3")

        self.stack = None
        self.prefix = ""

    def sb(self, name, shape, dt):
        name = self.prefix + name
        if self.stack is not None:
            return self.stack.enter_context(self.nc.sbuf_tensor(name, shape, dt))
        return self.nc.alloc_sbuf_tensor(name, shape, dt)

    def ps(self, name, shape, dt):
        name = self.prefix + name
        if self.stack is not None:
            return self.stack.enter_context(self.nc.psum_tensor(name, shape, dt))
        return self.nc.alloc_psum_tensor(name, shape, dt)


def emit_common_consts(ctx, blk_new):
    nc = ctx.nc
    ident = ctx.sb("c_ident", [128, 128], BF16)
    ones_row = ctx.sb("c_ones_row", [1, 512], BF16)
    ctx.eps_col = ctx.sb("c_eps", [128, 1], F32)
    with blk_new() as blk:
        @blk.gpsimd
        def _(g):
            ctx.pool.inc(g.memset(ident[:, :], 1.0))
            g.memset(ones_row[:, :], 1.0)
            g.memset(ctx.eps_col[:, :], 1e-5)
            ctx.pool.wait(g)
            g.affine_select(ident[:, :], ident[:, :], [[1, 128]], ALU.is_equal, 0.0,
                            base=0, channel_multiplier=-1)
    return ident, ones_row


def emit_rope_tables(ctx, blk_new, pfr, scrA, scrB, pos_ap, theta_hi_ap, theta_lo_ap,
                     ncols, name, swap=False):
    nc = ctx.nc
    cs_t = ctx.sb(name + "_cs", [128, ncols], F32)
    s_pos = scrB[0:1, 0:ncols]
    s_th = ctx.sb(name + "_th", [1, 128], F32)
    s_tl = ctx.sb(name + "_tl", [1, 128], F32)
    s_u = scrA[:, 0:512]
    s_n = ctx.sb(name + "_n", [128, 512], I32)
    s_nf = scrA[:, 512:1024]
    s_m = scrA[:, 1024:1536]
    s_w = scrA[:, 1536:2048]
    cos_half = slice(64, 128) if swap else slice(0, 64)
    ldm = Trk(ctx.nc, name + "_ld")
    with blk_new() as blk:
        @blk.sync
        def _(s):
            ldm.inc(s.dma_start(s_pos, pos_ap), 16)
            ldm.inc(s.dma_start(s_th[:, :], theta_hi_ap), 16)
            ldm.inc(s.dma_start(s_tl[:, :], theta_lo_ap), 16)
            ldm.wait(s)
    nch = ncols // 512
    for c in range(nch):
        cs = slice(c * 512, (c + 1) * 512)
        with blk_new() as blk:
            @blk.tensor
            def _(t):
                ldm.wait(t)
                t.matmul(pfr[:, :], s_th[:, :], scrB[0:1, c * 512:(c + 1) * 512],
                         start=True, stop=False)
                ctx.pe.inc(t.matmul(pfr[:, :], s_tl[:, :], scrB[0:1, c * 512:(c + 1) * 512],
                                    start=False, stop=True))
            @blk.vector
            def _(v):
                ctx.pe.wait(v)
                def step(ins):
                    ctx.dve.inc(ins)
                    ctx.dve.wait(v)
                step(v.tensor_scalar(s_u, pfr[:, :], INV_2PI, None, ALU.mult))
                step(v.tensor_copy(s_n[:, :], s_u))
                step(v.tensor_copy(s_nf, s_n[:, :]))
                step(v.scalar_tensor_tensor(s_m, s_nf, -C_HI, pfr[:, :], ALU.mult, ALU.add))
                step(v.scalar_tensor_tensor(s_m, s_nf, -C_LO, s_m, ALU.mult, ALU.add))
                step(v.tensor_scalar(s_w, s_m, -PI_C, TWO_PI, ALU.is_lt, ALU.mult))
                step(v.tensor_add(s_m, s_m, s_w))
                step(v.tensor_scalar(s_w, s_m, PI_C, TWO_PI, ALU.is_gt, ALU.mult))
                step(v.tensor_sub(s_m, s_m, s_w))
                step(v.tensor_scalar(s_m, s_m, -PI_C, PI_C, ALU.max, ALU.min))
                mh = s_m[cos_half, :]
                wh = s_w[cos_half, :]
                step(v.tensor_scalar(mh, mh, HALF_PI, None, ALU.add))
                step(v.tensor_scalar(wh, mh, PI_C, TWO_PI, ALU.is_gt, ALU.mult))
                step(v.tensor_sub(mh, mh, wh))
                step(v.tensor_scalar(mh, mh, -PI_C, PI_C, ALU.max, ALU.min))
            @blk.scalar
            def _(sc):
                ctx.dve.wait(sc)
                ctx.act.inc(sc.activation(cs_t[:, cs], s_m, AF.Sin))
    return cs_t


def emit_ln_fm(ctx, blk, src_f32, xln_fm, tcol0, pbank_t, scr_f32, xln_tm, ident,
               fm_cols, ld):
    if not hasattr(ctx, "_ln_st"):
        ctx._ln_st = ctx.sb("ln_stats", [128, 8], F32)
    st = ctx._ln_st
    rs, mean, sumsq, var, rstd, nb = (st[:, 0:1], st[:, 1:2], st[:, 2:3], st[:, 3:4],
                                      st[:, 4:5], st[:, 5:6])

    @blk.scalar
    def _(sc):
        ld.wait(sc)
        ctx.dve.wait(sc)
        ctx.act.wait(sc)
        ctx.act.inc(sc.activation(scr_f32[:, :], src_f32[:, :], AF.Square,
                                  accum_out=sumsq))

    @blk.vector
    def _(v):
        ld.wait(v)
        ctx.dve.inc(v.reduce_sum(rs, src_f32[:, :], axis=AX.X))
        ctx.dve.wait(v)
        ctx.dve.inc(v.tensor_scalar(mean[:, :], rs, 1.0 / D, None, ALU.mult))
        ctx.act.wait(v)
        ctx.dve.wait(v)
        ctx.dve.inc(v.tensor_tensor(var[:, :], mean, mean, ALU.mult))
        ctx.dve.wait(v)
        ctx.dve.inc(v.scalar_tensor_tensor(var[:, :], sumsq, 1.0 / D, var, ALU.mult, ALU.subtract))

    @blk.scalar
    def _(sc):
        ctx.dve.wait(sc)
        ctx.act.inc(sc.activation(var[:, :], var[:, :], AF.Sqrt, bias=ctx.eps_col[:, :]))

    @blk.vector
    def _(v):
        ctx.act.wait(v)
        ctx.dve.inc(v.reciprocal(rstd[:, :], var[:, :]))
        ctx.dve.wait(v)
        ctx.dve.inc(v.scalar_tensor_tensor(nb[:, :], mean, -1.0, rstd, ALU.mult, ALU.mult))

    @blk.scalar
    def _(sc):
        ctx.dve.wait(sc)
        ctx.act.inc(sc.activation(xln_tm[:, :], src_f32[:, :], AF.Identity,
                                  bias=nb, scale=rstd))

    n_feat_tiles = D // 128
    for half in range(n_feat_tiles // 8):
        @blk.tensor
        def _(t, half=half):
            ctx.act.wait(t)
            ctx.dve.wait(t)
            for j in range(8):
                jj = half * 8 + j
                ins = t.transpose(pbank_t[:, j * 128:(j + 1) * 128],
                                  xln_tm[:, jj * 128:(jj + 1) * 128], ident[:, :])
            ctx.pe.inc(ins)

        @blk.vector
        def _(v, half=half):
            ctx.pe.wait(v)
            dst = xln_fm[:, :].rearrange("p (i c) -> p i c", c=fm_cols)
            ctx.dve.inc(v.tensor_copy(
                dst[:, half * 8:(half + 1) * 8, tcol0:tcol0 + 128],
                pbank_t[:, :].rearrange("p (i c) -> p i c", c=128)))


def emit_l0_attn(ctx, blk_new, io):
    nc = ctx.nc
    pk = [ctx.ps("a_pk%d" % i, [128, 512], F32) for i in range(2)]
    pctx = [ctx.ps("a_pctx%d" % i, [128, 512], F32) for i in range(4)]
    ptr = [ctx.ps("a_ptr%d" % i, [128, 1024], BF16) for i in range(2)]

    ident, ones_row = emit_common_consts(ctx, blk_new)
    ld_x = Trk(nc, "a_ld_x")
    ld_w = Trk(nc, "a_ld_w")
    ld_wo = Trk(nc, "a_ld_wo")
    ld_b = Trk(nc, "a_ld_b")
    ld_xr = [Trk(nc, "a_ld_xr%d" % i) for i in range(2)]

    xln = ctx.sb("a_xln", [128, NI * TSEQ], BF16)
    wraw = [ctx.sb("a_wraw%d" % i, [128, D], F32) for i in range(2)]
    scr_f32 = wraw[1]
    xstage = wraw[0]
    xln_tm = ctx.sb("a_xlntm", [128, D], BF16)
    wbf = ctx.sb("a_wbf", [128, D], BF16)
    wpool = ctx.sb("a_wpool", [128, 4 * D], BF16)
    kfm = ctx.sb("a_kfm", [128, TSEQ], BF16)
    qfm = ctx.sb("a_qfm", [128, TOWN], BF16)
    vfm = ctx.sb("a_vfm", [128, TSEQ], BF16)
    vone = ctx.sb("a_vone", [128, NTS * 129], BF16)
    ptil = [ctx.sb("a_ptil%d" % i, [128, 512], BF16) for i in range(2)]
    ropeT = ctx.sb("a_ropeT", [128, 512], BF16)
    csub = ctx.sb("a_csub", [128, 64], BF16)
    ctx_tm = ctx.sb("a_ctxtm", [128, 512], BF16)
    ctx_fm = ctx.sb("a_ctxfm", [128, H * TOWN], BF16)
    recips = ctx.sb("a_recips", [128, 4], F32)
    bqc = ctx.sb("a_bqc", [128, H], F32)
    bkc = ctx.sb("a_bkc", [128, H], F32)
    bvc = ctx.sb("a_bvc", [128, H], F32)
    bo_row = ctx.sb("a_borow", [1, D], BF16)
    xres = [ctx.sb("a_xres%d" % i, [128, 512], F32) for i in range(2)]
    ysb = [ctx.sb("a_ysb%d" % i, [128, 512], F32) for i in range(2)]

    cs_t = emit_rope_tables(ctx, blk_new, pctx[0], wraw[0], wraw[1],
                            io["pos_all"], io["th_hi"], io["th_lo"], TSEQ, "rt0")

    xln_v = xln[:, :].rearrange("p (i c) -> p i c", c=TSEQ)
    vone_v = vone[:, :].rearrange("p (t c) -> p t c", c=129)
    wqT_v = wpool[:, 0 * D:1 * D].rearrange("p (j c) -> p j c", c=128)
    wkT_v = wpool[:, 1 * D:2 * D].rearrange("p (j c) -> p j c", c=128)
    wvT_v = wpool[:, 2 * D:3 * D].rearrange("p (j c) -> p j c", c=128)
    woT_v = wpool[:, 0:4 * D].rearrange("p (j c) -> p j c", c=512)
    wT_v = {"k": wkT_v, "q": wqT_v, "v": wvT_v}

    with blk_new() as blk:
        @blk.sync
        def _(s):
            ld_b.inc(s.dma_start(bqc[:, :], io["bq_c"]), 16)
            ld_b.inc(s.dma_start(bkc[:, :], io["bk_c"]), 16)
            ld_b.inc(s.dma_start(bvc[:, :], io["bv_c"]), 16)
            ld_b.inc(s.dma_start(csub[:, :], io["csub"]), 16)
            ld_b.wait(s)
        @blk.gpsimd
        def _(g):
            ctx.ld3.inc(g.dma_start(bo_row[:, :], io["bo"].rearrange("(a d) -> a d", a=1)), 16)
            ctx.ld3.wait(g)
        @blk.vector
        def _(v):
            ctx.dve.inc(v.memset(vone_v[:, :, 128:129], 1.0))


    for part in range(2):
        src_ap = io["x_own"] if part == 0 else io["x_oth"]
        for t in range(NT):
            with blk_new() as blk:
                @blk.sync
                def _(s, t=t, src_ap=src_ap):
                    s.wait_ge(ctx.act.h, ctx.act.n)
                    s.wait_ge(ctx.dve.h, ctx.dve.n)
                    s.wait_ge(ctx.pe.h, ctx.pe.n)
                    ld_x.inc(s.dma_start(xstage[:, :],
                                         src_ap[t * 128:(t + 1) * 128, :]), 16)
                    ld_x.wait(s)
                emit_ln_fm(ctx, blk, xstage, xln, part * TOWN + t * 128, ptr[0],
                           scr_f32, xln_tm, ident, TSEQ, ld_x)

    def transpose_w(blk, src_bf, dstT_v, nj, col_off=0):
        for bi, j0 in enumerate(range(0, nj, 8)):
            j1 = min(nj, j0 + 8)
            @blk.tensor
            def _(t, j0=j0, j1=j1, bi=bi):
                ctx.pool.wait(t)
                ctx.dve.wait(t)
                for j in range(j0, j1):
                    ins = t.transpose(ptr[bi % 2][:, (j - j0) * 128:(j - j0 + 1) * 128],
                                      src_bf[:, j * 128:(j + 1) * 128], ident[:, :])
                ctx.pe.inc(ins)
            @blk.vector
            def _(v, j0=j0, j1=j1, bi=bi):
                ctx.pe.wait(v)
                ctx.dve.inc(v.tensor_copy(
                    dstT_v[:, j0:j1, col_off:col_off + 128],
                    ptr[bi % 2][:, 0:(j1 - j0) * 128].rearrange("p (j c) -> p j c", c=128)))

    def proj_chunk(t, which, cs, bank):
        for j in range(NI):
            ins = t.matmul(pk[bank][:, :], wT_v[which][:, j, :], xln_v[:, j, cs],
                           start=(j == 0), stop=(j == NI - 1))
        return ctx.pe.inc(ins)

    for h in range(H):
        with blk_new() as blk:
            @blk.sync
            def _(s):
                s.wait_ge(ctx.pool.h, ctx.pool.n)
                s.wait_ge(ctx.act.h, ctx.act.n)
                s.wait_ge(ctx.dve.h, ctx.dve.n)
                ld_w.inc(s.dma_start(wraw[0][:, :], io["wq_p"][h * 128:(h + 1) * 128, :]), 16)
                ld_w.inc(s.dma_start(wraw[1][:, :], io["wk_p"][h * 128:(h + 1) * 128, :]), 16)
            @blk.gpsimd
            def _(g):
                ld_w.wait(g)
                ctx.pool.inc(g.tensor_copy(wbf[:, :], wraw[0][:, :]))
            transpose_w(blk, wbf, wqT_v, NI)
            @blk.gpsimd
            def _(g):
                ld_w.wait(g)
                ctx.dve.wait(g)
                ctx.pool.inc(g.tensor_copy(wbf[:, :], wraw[1][:, :]))
            transpose_w(blk, wbf, wkT_v, NI)
            @blk.sync
            def _(s):
                s.wait_ge(ctx.pool.h, ctx.pool.n)
                ld_w.inc(s.dma_start(wraw[0][:, :], io["wv"][h * 128:(h + 1) * 128, :]), 16)
            @blk.gpsimd
            def _(g):
                ld_w.wait(g)
                ctx.dve.wait(g)
                ctx.pool.inc(g.tensor_copy(wbf[:, :], wraw[0][:, :]))
            transpose_w(blk, wbf, wvT_v, NI)

        with blk_new() as blk:
            def rope_emit(blk, dst_fm, bcol, cs, bank):
                @blk.vector
                def _(v):
                    ctx.pe.wait(v)
                    ctx.act.wait(v)
                    ctx.dve.inc(v.scalar_tensor_tensor(ropeT[:, :], pk[bank][:, :],
                                                       bcol, cs_t[:, cs],
                                                       ALU.add, ALU.mult))
                @blk.tensor
                def _(t):
                    ctx.dve.wait(t)
                    if bank == 1:
                        ctx.act.wait(t)
                    ctx.pe.inc(t.matmul(pctx[bank][0:64, 0:512], csub[:, :], ropeT[:, :],
                                        start=True, stop=True))
                @blk.scalar
                def _(sc):
                    ctx.pe.wait(sc)
                    ctx.act.inc(sc.activation(dst_fm[64:128, cs], pk[bank][64:128, :],
                                              AF.Identity, bias=bcol[64:128, :]))
                    ctx.act.inc(sc.activation(dst_fm[0:64, cs], pctx[bank][0:64, 0:512],
                                              AF.Identity))

            for c in range(4):
                cs = slice(c * 512, (c + 1) * 512)
                @blk.tensor
                def _(t, c=c, cs=cs):
                    ctx.dve.wait(t)
                    ctx.act.wait(t)
                    proj_chunk(t, "k", cs, c % 2)
                rope_emit(blk, kfm, bkc[:, h:h + 1], cs, c % 2)
            for c in range(2):
                cs = slice(c * 512, (c + 1) * 512)
                @blk.tensor
                def _(t, c=c, cs=cs):
                    ctx.dve.wait(t)
                    ctx.act.wait(t)
                    proj_chunk(t, "q", cs, c % 2)
                rope_emit(blk, qfm, bqc[:, h:h + 1], cs, c % 2)
            for c in range(4):
                cs = slice(c * 512, (c + 1) * 512)
                @blk.tensor
                def _(t, c=c, cs=cs):
                    ctx.dve.wait(t)
                    ctx.act.wait(t)
                    proj_chunk(t, "v", cs, c % 2)
                @blk.scalar
                def _(sc, c=c, cs=cs):
                    ctx.pe.wait(sc)
                    ctx.act.inc(sc.activation(vfm[:, cs], pk[c % 2][:, :], AF.Identity,
                                              bias=bvc[:, h:h + 1]))
            for half in range(2):
                @blk.tensor
                def _(t, half=half):
                    ctx.act.wait(t)
                    ctx.dve.wait(t)
                    for j in range(8):
                        jj = half * 8 + j
                        ins = t.transpose(ptr[half % 2][:, j * 128:(j + 1) * 128],
                                          vfm[:, jj * 128:(jj + 1) * 128], ident[:, :])
                    ctx.pe.inc(ins)
                @blk.vector
                def _(v, half=half):
                    ctx.pe.wait(v)
                    ctx.dve.inc(v.tensor_copy(
                        vone_v[:, half * 8:(half + 1) * 8, 0:128],
                        ptr[half % 2][:, :].rearrange("p (j c) -> p j c", c=128)))

        with blk_new() as blk:
            for qc in range(2):
                qs = slice(qc * 512, (qc + 1) * 512)
                s_marks, e_marks, pv_marks = {}, {}, {}

                def emit_pv(t, tk):
                    pb = ptil[tk % 2]
                    for sub in range(4):
                        ins = t.matmul(pctx[sub][:, 0:129], pb[:, sub * 128:(sub + 1) * 128],
                                       vone_v[:, tk, :],
                                       start=(tk == 0), stop=(tk == NTS - 1))
                    pv_marks[tk] = ctx.pe.inc(ins)

                for tk in range(NTS):
                    @blk.tensor
                    def _(t, tk=tk, qs=qs, qc=qc):
                        if tk >= 2:
                            t.wait_ge(ctx.act.h, e_marks[tk - 2])
                        if tk == 0:
                            ctx.act.wait(t)
                            ctx.dve.wait(t)
                        ins = t.matmul(pk[tk % 2][:, :], kfm[:, tk * 128:(tk + 1) * 128],
                                       qfm[:, qs], start=True, stop=True)
                        s_marks[tk] = ctx.pe.inc(ins)
                        if tk >= 1:
                            t.wait_ge(ctx.act.h, e_marks[tk - 1])
                            emit_pv(t, tk - 1)
                    @blk.scalar
                    def _(sc, tk=tk):
                        sc.wait_ge(ctx.pe.h, s_marks[tk])
                        if tk >= 2:
                            sc.wait_ge(ctx.pe.h, pv_marks[tk - 2])
                        ins = sc.activation(ptil[tk % 2][:, :], pk[tk % 2][:, :], AF.Exp,
                                            scale=ATTN_SCALE)
                        e_marks[tk] = ctx.act.inc(ins)
                @blk.tensor
                def _(t, qc=qc):
                    t.wait_ge(ctx.act.h, e_marks[NTS - 1])
                    emit_pv(t, NTS - 1)
                @blk.vector
                def _(v, qc=qc):
                    v.wait_ge(ctx.pe.h, pv_marks[NTS - 1])
                    ctx.dve.wait(v)
                    for sub in range(4):
                        ctx.dve.inc(v.reciprocal(recips[:, sub:sub + 1],
                                                 pctx[sub][:, 128:129]))
                    ctx.dve.wait(v)
                    for sub in range(4):
                        ctx.dve.inc(v.tensor_scalar(ctx_tm[:, sub * 128:(sub + 1) * 128],
                                                    pctx[sub][:, 0:128],
                                                    recips[:, sub:sub + 1], None, ALU.mult))
                @blk.tensor
                def _(t, qc=qc):
                    ctx.dve.wait(t)
                    for sub in range(4):
                        ins = t.transpose(ptr[0][:, sub * 128:(sub + 1) * 128],
                                          ctx_tm[:, sub * 128:(sub + 1) * 128], ident[:, :])
                    ctx.pe.inc(ins)
                @blk.vector
                def _(v, qc=qc):
                    ctx.pe.wait(v)
                    ctx.dve.inc(v.tensor_copy(
                        ctx_fm[:, h * TOWN + qc * 512:h * TOWN + (qc + 1) * 512],
                        ptr[0][:, 0:512]))

    ctx_v = ctx_fm[:, :].rearrange("p (i c) -> p i c", c=TOWN)
    for oc in range(4):
        with blk_new() as blk:
            for sub in range(4):
                @blk.sync
                def _(s, sub=sub, oc=oc):
                    ctx.pool.wait(s)
                    s.wait_ge(ctx.dve.h, ctx.dve.n)
                    ld_wo.inc(s.dma_start(wraw[sub % 2][:, :],
                                          io["wo"][(oc * 4 + sub) * 128:(oc * 4 + sub + 1) * 128, :]), 16)
                @blk.gpsimd
                def _(g, sub=sub):
                    ld_wo.wait(g)
                    ctx.dve.wait(g)
                    ctx.pool.inc(g.tensor_copy(wbf[:, :], wraw[sub % 2][:, :]))
                transpose_w(blk, wbf, woT_v, NI, col_off=sub * 128)
        with blk_new() as blk:
            t_marks = {}
            for tt in range(NT):
                bank = tt % 4
                @blk.sync
                def _(s, tt=tt, oc=oc):
                    if tt >= 2:
                        s.wait_ge(ctx.dve.h, t_marks[tt - 2])
                    ld_xr[tt % 2].inc(s.dma_start(xres[tt % 2][:, :],
                                                  io["x_own"][tt * 128:(tt + 1) * 128,
                                                              oc * 512:(oc + 1) * 512]), 16)
                @blk.tensor
                def _(t, tt=tt, bank=bank, oc=oc):
                    if tt >= 4:
                        t.wait_ge(ctx.dve.h, t_marks[tt - 4])
                    t.matmul(pctx[bank][:, :], ones_row[0:1, 0:128],
                             bo_row[0:1, oc * 512:(oc + 1) * 512], start=True, stop=False)
                    for j in range(NI):
                        ins = t.matmul(pctx[bank][:, :], ctx_v[:, j, tt * 128:(tt + 1) * 128],
                                       woT_v[:, j, :], start=False, stop=(j == NI - 1))
                    ctx.pe.inc(ins)
                @blk.vector
                def _(v, tt=tt, bank=bank):
                    ctx.pe.wait(v)
                    ld_xr[tt % 2].wait(v)
                    v.wait_ge(ctx.ld2.h, ctx.ld2.n)
                    t_marks[tt] = ctx.dve.inc(v.tensor_add(ysb[tt % 2][:, :],
                                                           xres[tt % 2][:, :],
                                                           pctx[bank][:, :]))
                @blk.scalar
                def _(sc, tt=tt, oc=oc):
                    sc.wait_ge(ctx.dve.h, t_marks[tt])
                    ctx.ld2.inc(sc.dma_start(io["y"][tt * 128:(tt + 1) * 128,
                                                     oc * 512:(oc + 1) * 512],
                                             ysb[tt % 2][:, :]), 16)
    with blk_new() as blk:
        @blk.scalar
        def _(sc):
            ctx.ld2.wait(sc)


def emit_l1_attn(ctx, blk_new, io):
    nc = ctx.nc
    TK = 1280
    NTK = TK // 128
    W = 256
    pk = [ctx.ps("c_pk%d" % i, [128, 512], F32) for i in range(2)]
    pctx = [ctx.ps("c_pctx%d" % i, [128, 512], F32) for i in range(4)]
    ptr = [ctx.ps("c_ptr%d" % i, [128, 1024], BF16) for i in range(2)]

    ident, ones_row = emit_common_consts(ctx, blk_new)
    ld_x = Trk(nc, "c_ld_x")
    ld_w = Trk(nc, "c_ld_w")
    ld_wo = Trk(nc, "c_ld_wo")
    ld_b = Trk(nc, "c_ld_b")
    ld_xr = [Trk(nc, "c_ld_xr%d" % i) for i in range(2)]

    xln = ctx.sb("c_xln", [128, NI * TK], BF16)
    wraw = [ctx.sb("c_wraw%d" % i, [128, D], F32) for i in range(2)]
    scr_f32 = wraw[1]
    xstage = wraw[0]
    xln_tm = ctx.sb("c_xlntm", [128, D], BF16)
    wbf = ctx.sb("c_wbf", [128, D], BF16)
    wpool = ctx.sb("c_wpool", [128, 4 * D], BF16)
    kfm = ctx.sb("c_kfm", [128, TK], BF16)
    qfm = ctx.sb("c_qfm", [128, TOWN], BF16)
    vfm = ctx.sb("c_vfm", [128, TK], BF16)
    vone = ctx.sb("c_vone", [128, NTK * 129], BF16)
    ptil = ctx.sb("c_ptil", [128, 4 * 256], BF16)
    ropeT = ctx.sb("c_ropeT", [128, 1024], BF16)
    csub = ctx.sb("c_csub", [128, 64], BF16)
    cadd = ctx.sb("c_cadd", [128, 64], BF16)
    maskA = ctx.sb("c_maskA", [128, 256], BF16)
    maskB = ctx.sb("c_maskB", [128, 256], BF16)
    prevflag = ctx.sb("c_prevflag", [128, 1], F32)
    ctx_tm = ctx.sb("c_ctxtm", [128, 512], BF16)
    ctx_fm = ctx.sb("c_ctxfm", [128, H * TOWN], BF16)
    recips = ctx.sb("c_recips", [128, 4], F32)
    xres = [ctx.sb("c_xres%d" % i, [128, 512], F32) for i in range(2)]
    ysb = [ctx.sb("c_ysb%d" % i, [128, 512], F32) for i in range(2)]

    cs_t = emit_rope_tables(ctx, blk_new, pctx[0], wraw[0], wraw[1],
                            io["pos_all"], io["th_hi"], io["th_lo"], 1536, "rt1")
    sc_t = emit_rope_tables(ctx, blk_new, pctx[0], wraw[0], wraw[1],
                            io["pos_all"], io["th_hi"], io["th_lo"], 1536, "rt1s",
                            swap=True)

    xln_v = xln[:, :].rearrange("p (i c) -> p i c", c=TK)
    vone_v = vone[:, :].rearrange("p (t c) -> p t c", c=129)
    wqT_v = wpool[:, 0 * D:1 * D].rearrange("p (j c) -> p j c", c=128)
    wkT_v = wpool[:, 1 * D:2 * D].rearrange("p (j c) -> p j c", c=128)
    wvT_v = wpool[:, 2 * D:3 * D].rearrange("p (j c) -> p j c", c=128)
    woT_v = wpool[:, 0:4 * D].rearrange("p (j c) -> p j c", c=512)
    wT_v = {"k": wkT_v, "q": wqT_v, "v": wvT_v}

    with blk_new() as blk:
        @blk.sync
        def _(s):
            ld_b.inc(s.dma_start(csub[:, :], io["csub"]), 16)
            ld_b.inc(s.dma_start(cadd[:, :], io["cadd"]), 16)
            ld_b.inc(s.dma_start(prevflag[:, :], io["prevflag"]), 16)
            ld_b.wait(s)
        @blk.vector
        def _(v):
            ctx.dve.inc(v.memset(vone_v[:, :, 128:129], 1.0))
        @blk.gpsimd
        def _(g):
            ctx.pool.inc(g.memset(maskA[:, :], 1.0))
            ctx.pool.inc(g.memset(maskB[:, :], 1.0))
            ctx.pool.wait(g)
            g.affine_select(maskA[:, :], maskA[:, :], [[1, 256]], ALU.is_ge, 0.0,
                            base=0, channel_multiplier=-1)
            ctx.pool.inc(g.affine_select(maskB[:, :], maskB[:, :], [[1, 256]],
                                         ALU.is_ge, 0.0, base=-128,
                                         channel_multiplier=-1))

    for part, (src_nm, ntile) in enumerate([("x_prev", 2), ("x_own", NT)]):
        for t in range(ntile):
            with blk_new() as blk:
                @blk.sync
                def _(s, t=t, src_nm=src_nm):
                    s.wait_ge(ctx.act.h, ctx.act.n)
                    s.wait_ge(ctx.dve.h, ctx.dve.n)
                    s.wait_ge(ctx.pe.h, ctx.pe.n)
                    ld_x.inc(s.dma_start(xstage[:, :],
                                         io[src_nm][t * 128:(t + 1) * 128, :]), 16)
                    ld_x.wait(s)
                emit_ln_fm(ctx, blk, xstage, xln, part * 256 + t * 128, ptr[0],
                           scr_f32, xln_tm, ident, TK, ld_x)

    def transpose_w(blk, src_bf, dstT_v, nj, col_off=0):
        for bi, j0 in enumerate(range(0, nj, 8)):
            j1 = min(nj, j0 + 8)
            @blk.tensor
            def _(t, j0=j0, j1=j1, bi=bi):
                ctx.pool.wait(t)
                ctx.dve.wait(t)
                for j in range(j0, j1):
                    ins = t.transpose(ptr[bi % 2][:, (j - j0) * 128:(j - j0 + 1) * 128],
                                      src_bf[:, j * 128:(j + 1) * 128], ident[:, :])
                ctx.pe.inc(ins)
            @blk.vector
            def _(v, j0=j0, j1=j1, bi=bi):
                ctx.pe.wait(v)
                ctx.dve.inc(v.tensor_copy(
                    dstT_v[:, j0:j1, col_off:col_off + 128],
                    ptr[bi % 2][:, 0:(j1 - j0) * 128].rearrange("p (j c) -> p j c", c=128)))

    def proj_chunk(t, which, cs, bank, w):
        for j in range(NI):
            ins = t.matmul(pk[bank][:, 0:w], wT_v[which][:, j, :], xln_v[:, j, cs],
                           start=(j == 0), stop=(j == NI - 1))
        return ctx.pe.inc(ins)

    kq_chunks = [(0, 512), (512, 512), (1024, 256)]
    q_chunks = [(0, 512), (512, 512)]

    for h in range(H):
        with blk_new() as blk:
            @blk.sync
            def _(s):
                s.wait_ge(ctx.pool.h, ctx.pool.n)
                s.wait_ge(ctx.act.h, ctx.act.n)
                s.wait_ge(ctx.dve.h, ctx.dve.n)
                ld_w.inc(s.dma_start(wraw[0][:, :], io["Wq_p"][h * 128:(h + 1) * 128, :]), 16)
                ld_w.inc(s.dma_start(wraw[1][:, :], io["Wk_p"][h * 128:(h + 1) * 128, :]), 16)
            @blk.gpsimd
            def _(g):
                ld_w.wait(g)
                ctx.pool.inc(g.tensor_copy(wbf[:, :], wraw[0][:, :]))
            transpose_w(blk, wbf, wqT_v, NI)
            @blk.gpsimd
            def _(g):
                ld_w.wait(g)
                ctx.dve.wait(g)
                ctx.pool.inc(g.tensor_copy(wbf[:, :], wraw[1][:, :]))
            transpose_w(blk, wbf, wkT_v, NI)
            @blk.sync
            def _(s):
                s.wait_ge(ctx.pool.h, ctx.pool.n)
                ld_w.inc(s.dma_start(wraw[0][:, :], io["Wv"][h * 128:(h + 1) * 128, :]), 16)
            @blk.gpsimd
            def _(g):
                ld_w.wait(g)
                ctx.dve.wait(g)
                ctx.pool.inc(g.tensor_copy(wbf[:, :], wraw[0][:, :]))
            transpose_w(blk, wbf, wvT_v, NI)

        with blk_new() as blk:
            def rope_emit(blk, dst_fm, dcol, tcol, w, bank):
                @blk.vector
                def _(v):
                    ctx.pe.wait(v)
                    ctx.act.wait(v)
                    ctx.dve.inc(v.tensor_mul(ropeT[:, 0:w], pk[bank][:, 0:w],
                                             cs_t[:, tcol:tcol + w]))
                    ctx.dve.inc(v.tensor_mul(ropeT[:, 512:512 + w], pk[bank][:, 0:w],
                                             sc_t[:, tcol:tcol + w]))
                @blk.tensor
                def _(t):
                    ctx.dve.wait(t)
                    ctx.act.wait(t)
                    t.matmul(pctx[bank][0:64, 0:w], csub[:, :], ropeT[:, 0:w],
                             start=True, stop=True)
                    ctx.pe.inc(t.matmul(pctx[bank][64:128, 0:w], cadd[:, :],
                                        ropeT[:, 512:512 + w],
                                        start=True, stop=True, tile_position=(0, 64)))
                @blk.scalar
                def _(sc):
                    ctx.pe.wait(sc)
                    ctx.act.inc(sc.activation(dst_fm[:, dcol:dcol + w],
                                              pctx[bank][:, 0:w], AF.Identity))

            for c, (col, w) in enumerate(kq_chunks):
                @blk.tensor
                def _(t, c=c, col=col, w=w):
                    ctx.dve.wait(t)
                    ctx.act.wait(t)
                    proj_chunk(t, "k", slice(col, col + w), c % 2, w)
                rope_emit(blk, kfm, col, col, w, c % 2)
            for c, (col, w) in enumerate(q_chunks):
                @blk.tensor
                def _(t, c=c, col=col, w=w):
                    ctx.dve.wait(t)
                    ctx.act.wait(t)
                    proj_chunk(t, "q", slice(256 + col, 256 + col + w), c % 2, w)
                rope_emit(blk, qfm, col, 256 + col, w, c % 2)
            for c, (col, w) in enumerate(kq_chunks):
                @blk.tensor
                def _(t, c=c, col=col, w=w):
                    ctx.dve.wait(t)
                    ctx.act.wait(t)
                    proj_chunk(t, "v", slice(col, col + w), c % 2, w)
                @blk.scalar
                def _(sc, c=c, col=col, w=w):
                    ctx.pe.wait(sc)
                    ctx.act.inc(sc.activation(vfm[:, col:col + w], pk[c % 2][:, 0:w],
                                              AF.Identity))
            for bi, j0 in enumerate(range(0, NTK, 8)):
                j1 = min(NTK, j0 + 8)
                @blk.tensor
                def _(t, j0=j0, j1=j1, bi=bi):
                    ctx.act.wait(t)
                    ctx.dve.wait(t)
                    for j in range(j0, j1):
                        ins = t.transpose(ptr[bi % 2][:, (j - j0) * 128:(j - j0 + 1) * 128],
                                          vfm[:, j * 128:(j + 1) * 128], ident[:, :])
                    ctx.pe.inc(ins)
                @blk.vector
                def _(v, j0=j0, j1=j1, bi=bi):
                    ctx.pe.wait(v)
                    ctx.dve.inc(v.tensor_copy(
                        vone_v[:, j0:j1, 0:128],
                        ptr[bi % 2][:, 0:(j1 - j0) * 128].rearrange("p (j c) -> p j c", c=128)))

        with blk_new() as blk:
            pe_prev_lc = ctx.pe.n
            for lc in range(4):
                qs = slice(lc * 256, (lc + 1) * 256)
                s_marks, pv_marks = {}, {}
                p_ready = {}

                def emit_pv(t, j, lc):
                    for sub in range(2):
                        ins = t.matmul(pctx[sub][:, 0:129],
                                       ptil[:, j * 256 + sub * 128:j * 256 + (sub + 1) * 128],
                                       vone_v[:, 2 * lc + j, :],
                                       start=(j == 0), stop=(j == 3))
                    pv_marks[j] = ctx.pe.inc(ins)

                for j in range(4):
                    @blk.tensor
                    def _(t, j=j, lc=lc, qs=qs):
                        if j == 0:
                            ctx.act.wait(t)
                            ctx.dve.wait(t)
                        if j >= 2:
                            h2, n2 = p_ready[j - 2]
                            t.wait_ge(h2, n2)
                        ins = t.matmul(pk[j % 2][:, 0:256],
                                       kfm[:, (2 * lc + j) * 128:(2 * lc + j + 1) * 128],
                                       qfm[:, qs], start=True, stop=True)
                        s_marks[j] = ctx.pe.inc(ins)
                        if j >= 1:
                            h1, n1 = p_ready[j - 1]
                            t.wait_ge(h1, n1)
                            emit_pv(t, j - 1, lc)
                    @blk.scalar
                    def _(sc, j=j, lc=lc):
                        sc.wait_ge(ctx.pe.h, s_marks[j])
                        if j == 0:
                            sc.wait_ge(ctx.pe.h, pe_prev_lc)
                        mk = ctx.act.inc(
                            sc.activation(ptil[:, j * 256:(j + 1) * 256],
                                          pk[j % 2][:, 0:256], AF.Exp, scale=ATTN_SCALE))
                        p_ready[j] = (ctx.act.h, mk)
                    @blk.vector
                    def _(v, j=j, lc=lc):
                        h0, n0 = p_ready[j]
                        v.wait_ge(h0, n0)
                        pt = ptil[:, j * 256:(j + 1) * 256]
                        if j >= 2:
                            m = maskA if j == 2 else maskB
                            mk = ctx.dve.inc(v.tensor_mul(pt, pt, m[:, :]))
                            p_ready[j] = (ctx.dve.h, mk)
                        elif lc == 0:
                            mk = ctx.dve.inc(
                                v.tensor_scalar(pt, pt, prevflag[:, 0:1], None, ALU.mult))
                            p_ready[j] = (ctx.dve.h, mk)
                @blk.tensor
                def _(t, lc=lc):
                    h3, n3 = p_ready[3]
                    t.wait_ge(h3, n3)
                    emit_pv(t, 3, lc)
                pe_prev_lc = ctx.pe.n
                @blk.vector
                def _(v, lc=lc):
                    v.wait_ge(ctx.pe.h, pv_marks[3])
                    ctx.dve.wait(v)
                    for sub in range(2):
                        ctx.dve.inc(v.reciprocal(recips[:, sub:sub + 1],
                                                 pctx[sub][:, 128:129]))
                    ctx.dve.wait(v)
                    for sub in range(2):
                        ctx.dve.inc(v.tensor_scalar(ctx_tm[:, sub * 128:(sub + 1) * 128],
                                                    pctx[sub][:, 0:128],
                                                    recips[:, sub:sub + 1], None, ALU.mult))
                @blk.tensor
                def _(t, lc=lc):
                    ctx.dve.wait(t)
                    for sub in range(2):
                        ins = t.transpose(ptr[0][:, sub * 128:(sub + 1) * 128],
                                          ctx_tm[:, sub * 128:(sub + 1) * 128], ident[:, :])
                    ctx.pe.inc(ins)
                @blk.vector
                def _(v, lc=lc):
                    ctx.pe.wait(v)
                    ctx.dve.inc(v.tensor_copy(
                        ctx_fm[:, h * TOWN + lc * 256:h * TOWN + (lc + 1) * 256],
                        ptr[0][:, 0:256]))

    ctx_v = ctx_fm[:, :].rearrange("p (i c) -> p i c", c=TOWN)
    for oc in range(4):
        with blk_new() as blk:
            for sub in range(4):
                @blk.sync
                def _(s, sub=sub, oc=oc):
                    ctx.pool.wait(s)
                    s.wait_ge(ctx.dve.h, ctx.dve.n)
                    ld_wo.inc(s.dma_start(wraw[sub % 2][:, :],
                                          io["Wo"][(oc * 4 + sub) * 128:(oc * 4 + sub + 1) * 128, :]), 16)
                @blk.gpsimd
                def _(g, sub=sub):
                    ld_wo.wait(g)
                    ctx.dve.wait(g)
                    ctx.pool.inc(g.tensor_copy(wbf[:, :], wraw[sub % 2][:, :]))
                transpose_w(blk, wbf, woT_v, NI, col_off=sub * 128)
        with blk_new() as blk:
            t_marks = {}
            for tt in range(NT):
                bank = tt % 4
                @blk.sync
                def _(s, tt=tt, oc=oc):
                    if tt >= 2:
                        s.wait_ge(ctx.dve.h, t_marks[tt - 2])
                    ld_xr[tt % 2].inc(s.dma_start(xres[tt % 2][:, :],
                                                  io["x_own"][tt * 128:(tt + 1) * 128,
                                                              oc * 512:(oc + 1) * 512]), 16)
                @blk.tensor
                def _(t, tt=tt, bank=bank):
                    if tt >= 4:
                        t.wait_ge(ctx.dve.h, t_marks[tt - 4])
                    for j in range(NI):
                        ins = t.matmul(pctx[bank][:, :], ctx_v[:, j, tt * 128:(tt + 1) * 128],
                                       woT_v[:, j, :], start=(j == 0), stop=(j == NI - 1))
                    ctx.pe.inc(ins)
                @blk.vector
                def _(v, tt=tt, bank=bank):
                    ctx.pe.wait(v)
                    ld_xr[tt % 2].wait(v)
                    v.wait_ge(ctx.ld2.h, ctx.ld2.n)
                    t_marks[tt] = ctx.dve.inc(v.tensor_add(ysb[tt % 2][:, :],
                                                           xres[tt % 2][:, :],
                                                           pctx[bank][:, :]))
                @blk.scalar
                def _(sc, tt=tt, oc=oc):
                    sc.wait_ge(ctx.dve.h, t_marks[tt])
                    ctx.ld2.inc(sc.dma_start(io["y"][tt * 128:(tt + 1) * 128,
                                                     oc * 512:(oc + 1) * 512],
                                             ysb[tt % 2][:, :]), 16)
    with blk_new() as blk:
        @blk.scalar
        def _(sc):
            ctx.ld2.wait(sc)


def emit_l0_ffn(ctx, blk_new, io, ncores=8):
    nc = ctx.nc
    DK = float(2 ** 20 + 1)
    pb = [ctx.ps("b_p%d" % i, [128, 512], F32) for i in range(8)]
    ld_x = Trk(nc, "b_ld_x")
    ld_w = Trk(nc, "b_ld_w")
    ld_wq = Trk(nc, "b_ld_wq")
    ld_xr = [Trk(nc, "b_ld_xr%d" % i) for i in range(2)]
    ld_cc = Trk(nc, "b_ld_cc")
    cc_sem = Trk(nc, "b_cc")

    identf = ctx.sb("b_identf", [128, 128], F32)
    ctx.eps_col = ctx.sb("b_eps", [128, 1], F32)
    ones_f = ctx.sb("b_onesf", [1, 128], F32)
    xln = ctx.sb("b_xln", [128, NI * TOWN], F32)
    xq = ctx.sb("b_xq", [128, NI * TOWN], BF16)
    xstage = ctx.sb("b_xstage", [128, D], F32)
    scr = ctx.sb("b_scr", [128, D], F32)
    xln_tm = ctx.sb("b_xlntm", [128, D], F32)
    qs1 = ctx.sb("b_qs1", [128, D], F32)
    qs2 = ctx.sb("b_qs2", [128, D], F32)
    wstage = [ctx.sb("b_wst%d" % i, [128, D], BF16) for i in range(2)]
    wqf = ctx.sb("b_wqf", [128, D], F32)
    wqT = ctx.sb("b_wqT", [128, NI * 512], BF16)
    stats = ctx.sb("b_stats", [128, 8], F32)
    amx = ctx.sb("b_amx", [128, 2], F32)
    sca = ctx.sb("b_sca", [1, 8], F32)
    scol = ctx.sb("b_scol", [128, 4], F32)
    otmp = ctx.sb("b_otmp", [128, 4 * 512], BF16)
    xres = [ctx.sb("b_xres%d" % i, [128, 512], F32) for i in range(2)]
    ysb = [ctx.sb("b_ysb%d" % i, [128, 512], F32) for i in range(2)]

    xln_v = xln[:, :].rearrange("p (i c) -> p i c", c=TOWN)
    xq_v = xq[:, :].rearrange("p (i c) -> p i c", c=TOWN)
    wqT_v = wqT[:, :].rearrange("p (j c) -> p j c", c=512)

    def tr_f32(blk, src_f32_t, dst_v, j0_dst, col_off, ncol):
        for b4 in range(4):
            @blk.tensor
            def _(t, b4=b4):
                ctx.act.wait(t)
                ctx.dve.wait(t)
                for j in range(4):
                    jj = b4 * 4 + j
                    ins = t.transpose(pb[6 + b4 % 2][:, j * 128:(j + 1) * 128],
                                      src_f32_t[:, jj * 128:(jj + 1) * 128], identf[:, :])
                ctx.pe.inc(ins)
            @blk.vector
            def _(v, b4=b4):
                ctx.pe.wait(v)
                ctx.dve.inc(v.tensor_copy(
                    dst_v[:, j0_dst + b4 * 4:j0_dst + b4 * 4 + 4, col_off:col_off + 128],
                    pb[6 + b4 % 2][:, 0:512].rearrange("p (j c) -> p j c", c=128)))

    with blk_new() as blk:
        @blk.gpsimd
        def _(g):
            ctx.pool.inc(g.memset(identf[:, :], 1.0))
            g.memset(ctx.eps_col[:, :], 1e-5)
            g.memset(ones_f[:, :], 1.0)
            ctx.pool.wait(g)
            g.affine_select(identf[:, :], identf[:, :], [[1, 128]], ALU.is_equal, 0.0,
                            base=0, channel_multiplier=-1)
        @blk.vector
        def _(v):
            ctx.dve.inc(v.memset(amx[:, :], 0.0))

    for t in range(NT):
        with blk_new() as blk:
            @blk.sync
            def _(s, t=t):
                s.wait_ge(ctx.act.h, ctx.act.n)
                s.wait_ge(ctx.dve.h, ctx.dve.n)
                ld_x.inc(s.dma_start(xstage[:, :], io["x_own"][t * 128:(t + 1) * 128, :]), 16)
                ld_x.wait(s)
            emit_ln_core(ctx, blk, xstage, scr, xln_tm, ld_x, F32)
            @blk.vector
            def _(v, t=t):
                ctx.act.wait(v)
                ctx.dve.inc(v.tensor_reduce(stats[:, 0:1], xln_tm[:, :], AX.X, ALU.max,
                                            apply_absolute_value=True))
                ctx.dve.wait(v)
                ctx.dve.inc(v.tensor_max(amx[:, 0:1], amx[:, 0:1], stats[:, 0:1]))
            tr_f32(blk, xln_tm, xln_v, 0, t * 128, TOWN)
    for t in range(NI):
        with blk_new() as blk:
            @blk.sync
            def _(s, t=t):
                s.wait_ge(ctx.dve.h, ctx.dve.n)
                ld_w.inc(s.dma_start(wstage[t % 2][:, :], io["w"][t * 128:(t + 1) * 128, :]), 16)
                ld_w.wait(s)
            @blk.vector
            def _(v, t=t):
                ld_w.wait(v)
                ctx.dve.inc(v.tensor_reduce(stats[:, 0:1], wstage[t % 2][:, :], AX.X,
                                            ALU.max, apply_absolute_value=True))
                ctx.dve.wait(v)
                ctx.dve.inc(v.tensor_max(amx[:, 1:2], amx[:, 1:2], stats[:, 0:1]))

    with blk_new() as blk:
        @blk.tensor
        def _(t):
            ctx.dve.wait(t)
            t.transpose(pb[6][0:1, 0:128], amx[:, 0:1], identf[:, :])
            ctx.pe.inc(t.transpose(pb[6][0:1, 128:256], amx[:, 1:2], identf[:, :]))
        @blk.vector
        def _(v):
            ctx.pe.wait(v)
            ctx.dve.inc(v.reduce_max(sca[0:1, 0:1], pb[6][0:1, 0:128], axis=AX.X))
            ctx.dve.inc(v.reduce_max(sca[0:1, 3:4], pb[6][0:1, 128:256], axis=AX.X))
    with blk_new() as blk:
        @blk.gpsimd
        def _(g):
            ctx.dve.wait(g)
            ld_cc.inc(g.dma_start(io["cc_in"], sca[0:1, 0:1]), 16)
            ld_cc.wait(g)
            cc_sem.inc(g.collective_compute("AllReduce", ALU.max,
                                            [list(range(ncores))],
                                            [io["cc_in"].opt()], [io["cc_out"].opt()]))
            cc_sem.wait(g)
            ld_cc.inc(g.dma_start(sca[0:1, 0:1], io["cc_out"]), 16)
            ld_cc.wait(g)
    with blk_new() as blk:
        @blk.vector
        def _(v):
            ld_cc.wait(v)
            def step(ins):
                ctx.dve.inc(ins)
                ctx.dve.wait(v)
            step(v.tensor_scalar(sca[0:1, 1:2], sca[0:1, 0:1], 1.0 / 448.0, 1e-12,
                                 ALU.mult, ALU.max))
            step(v.reciprocal(sca[0:1, 2:3], sca[0:1, 1:2]))
            step(v.tensor_scalar(sca[0:1, 4:5], sca[0:1, 3:4], 1.0 / 448.0, 1e-12,
                                 ALU.mult, ALU.max))
            step(v.reciprocal(sca[0:1, 5:6], sca[0:1, 4:5]))
            step(v.tensor_tensor(sca[0:1, 6:7], sca[0:1, 1:2], sca[0:1, 4:5], ALU.mult))
        @blk.tensor
        def _(t):
            ctx.dve.wait(t)
            t.matmul(pb[5][:, 0:1], ones_f[:, :], sca[0:1, 2:3], start=True, stop=True)
            t.matmul(pb[5][:, 1:2], ones_f[:, :], sca[0:1, 5:6], start=True, stop=True)
            ctx.pe.inc(t.matmul(pb[5][:, 2:3], ones_f[:, :], sca[0:1, 6:7], start=True, stop=True))
        @blk.vector
        def _(v):
            ctx.pe.wait(v)
            ctx.dve.inc(v.tensor_copy(scol[:, 0:3], pb[5][:, 0:3]))

    for i in range(NI):
        with blk_new() as blk:
            @blk.vector
            def _(v, i=i):
                ctx.act.wait(v)
                ctx.pool.wait(v)
                ctx.dve.inc(v.tensor_scalar(qs1[:, 0:TOWN], xln_v[:, i, :],
                                            scol[:, 0:1], None, ALU.mult))
            @blk.scalar
            def _(sc, i=i):
                ctx.dve.wait(sc)
                ctx.act.inc(sc.activation(qs2[:, 0:TOWN], qs1[:, 0:TOWN], AF.Identity,
                                          scale=DK))
            @blk.gpsimd
            def _(g, i=i):
                ctx.act.wait(g)
                ctx.pool.inc(g.tensor_sub(qs2[:, TOWN:2 * TOWN], qs2[:, 0:TOWN],
                                          qs1[:, 0:TOWN]))
            @blk.vector
            def _(v, i=i):
                ctx.pool.wait(v)
                ctx.dve.inc(v.tensor_sub(xq_v[:, i, :], qs2[:, 0:TOWN],
                                         qs2[:, TOWN:2 * TOWN]))

    for oc in range(4):
        with blk_new() as blk:
            for sub in range(4):
                @blk.sync
                def _(s, sub=sub, oc=oc):
                    s.wait_ge(ctx.dve.h, ctx.dve.n)
                    ld_wq.inc(s.dma_start(wstage[sub % 2][:, :],
                                          io["w"][(oc * 4 + sub) * 128:(oc * 4 + sub + 1) * 128, :]), 16)
                @blk.vector
                def _(v, sub=sub):
                    ld_wq.wait(v)
                    ctx.act.wait(v)
                    ctx.pool.wait(v)
                    ctx.dve.inc(v.tensor_scalar(qs1[:, :], wstage[sub % 2][:, :],
                                                scol[:, 1:2], None, ALU.mult))
                @blk.scalar
                def _(sc, sub=sub):
                    ctx.dve.wait(sc)
                    ctx.act.inc(sc.activation(qs2[:, :], qs1[:, :], AF.Identity, scale=DK))
                @blk.gpsimd
                def _(g, sub=sub):
                    ctx.act.wait(g)
                    ctx.pool.inc(g.tensor_sub(scr[:, :], qs2[:, :], qs1[:, :]))
                @blk.vector
                def _(v, sub=sub):
                    ctx.pool.wait(v)
                    ctx.dve.inc(v.tensor_sub(wqf[:, :], qs2[:, :], scr[:, :]))
                tr_f32(blk, wqf, wqT_v, 0, sub * 128, 512)
        with blk_new() as blk:
            t_marks = {}
            a_marks = {}
            for tt in range(NT):
                bank = tt % 4
                @blk.sync
                def _(s, tt=tt, oc=oc):
                    if tt >= 2:
                        s.wait_ge(ctx.dve.h, t_marks[tt - 2])
                    ld_xr[tt % 2].inc(s.dma_start(xres[tt % 2][:, :],
                                                  io["x_own"][tt * 128:(tt + 1) * 128,
                                                              oc * 512:(oc + 1) * 512]), 16)
                @blk.tensor
                def _(t, tt=tt, bank=bank):
                    if tt >= 4:
                        t.wait_ge(ctx.act.h, a_marks[tt - 4])
                    for j in range(NI):
                        ins = t.matmul(pb[bank][:, :], xq_v[:, j, tt * 128:(tt + 1) * 128],
                                       wqT_v[:, j, :], start=(j == 0), stop=(j == NI - 1))
                    ctx.pe.inc(ins)
                @blk.scalar
                def _(sc, tt=tt, bank=bank):
                    ctx.pe.wait(sc)
                    if tt >= 4:
                        sc.wait_ge(ctx.dve.h, t_marks[tt - 4])
                    a_marks[tt] = ctx.act.inc(
                        sc.activation(otmp[:, (tt % 4) * 512:(tt % 4 + 1) * 512],
                                      pb[bank][:, :], AF.Identity, scale=scol[:, 2:3]))
                @blk.vector
                def _(v, tt=tt):
                    v.wait_ge(ctx.act.h, a_marks[tt])
                    ld_xr[tt % 2].wait(v)
                    v.wait_ge(ctx.ld2.h, ctx.ld2.n)
                    t_marks[tt] = ctx.dve.inc(
                        v.tensor_add(ysb[tt % 2][:, :], xres[tt % 2][:, :],
                                     otmp[:, (tt % 4) * 512:(tt % 4 + 1) * 512]))
                @blk.scalar
                def _(sc, tt=tt, oc=oc):
                    sc.wait_ge(ctx.dve.h, t_marks[tt])
                    ctx.ld2.inc(sc.dma_start(io["y"][tt * 128:(tt + 1) * 128,
                                                     oc * 512:(oc + 1) * 512],
                                             ysb[tt % 2][:, :]), 16)
    with blk_new() as blk:
        @blk.scalar
        def _(sc):
            ctx.ld2.wait(sc)


def emit_l1_ffn(ctx, blk_new, io):
    nc = ctx.nc
    FS, FE, E = 1536, 768, 8
    pb = [ctx.ps("d_p%d" % i, [128, 512], F32) for i in range(8)]
    ld_x = Trk(nc, "d_ld_x")
    ld_r = Trk(nc, "d_ld_r")
    ld_w = [Trk(nc, "d_ld_w%d" % i) for i in range(2)]
    ld_xr = [Trk(nc, "d_ld_xr%d" % i) for i in range(2)]

    identf = ctx.sb("d_identf", [128, 128], F32)
    ctx.eps_col = ctx.sb("d_eps", [128, 1], F32)
    xln = ctx.sb("d_xln", [128, NI * TOWN], BF16)
    rtr = ctx.sb("d_rtr", [128, NI * 8], BF16)
    lgt = ctx.sb("d_lgt", [128, 8], F32)
    pstat = ctx.sb("d_pstat", [128, 8], F32)
    probs = ctx.sb("d_probs", [128, 8], F32)
    keep = ctx.sb("d_keep", [128, 8], F32)
    cw = ctx.sb("d_cw", [128, NT * 8], F32)
    gbuf = ctx.sb("d_gbuf", [128, NT * 512], F32)
    ubuf = ctx.sb("d_ubuf", [128, NT * 512], F32)
    xstage = ubuf[:, 0:D]
    scr = gbuf[:, 0:D]
    hsh = ctx.sb("d_hsh", [128, (FS // 128) * TOWN], BF16)
    hex_ = [ctx.sb("d_hex%d" % e, [128, (FE // 128) * TOWN], BF16) for e in range(E)]
    wslc = [ctx.sb("d_wslc%d" % i, [128, 512], BF16) for i in range(2)]
    xres = [ctx.sb("d_xres%d" % i, [128, 512], F32) for i in range(2)]
    ysb = [ctx.sb("d_ysb%d" % i, [128, 512], F32) for i in range(2)]

    xln_v = xln[:, :].rearrange("p (i c) -> p i c", c=TOWN)
    hsh_v = hsh[:, :].rearrange("p (f c) -> p f c", c=TOWN)
    hex_v = [h[:, :].rearrange("p (f c) -> p f c", c=TOWN) for h in hex_]

    with blk_new() as blk:
        @blk.gpsimd
        def _(g):
            ctx.pool.inc(g.memset(identf[:, :], 1.0))
            g.memset(ctx.eps_col[:, :], 1e-5)
            ctx.pool.wait(g)
            g.affine_select(identf[:, :], identf[:, :], [[1, 128]], ALU.is_equal, 0.0,
                            base=0, channel_multiplier=-1)
        @blk.gpsimd
        def _(g):
            ld_r.inc(g.dma_start(rtr[:, :].rearrange("p (i c) -> p i c", c=8),
                                 io["router"].rearrange("(i p) c -> p i c", p=128)), 16)
            ld_r.wait(g)

    for t in range(NT):
        with blk_new() as blk:
            @blk.sync
            def _(s, t=t):
                s.wait_ge(ctx.act.h, ctx.act.n)
                s.wait_ge(ctx.dve.h, ctx.dve.n)
                s.wait_ge(ctx.pe.h, ctx.pe.n)
                ld_x.inc(s.dma_start(xstage[:, :], io["x_own"][t * 128:(t + 1) * 128, :]), 16)
                ld_x.wait(s)
            emit_ln_core(ctx, blk, xstage, scr, scr, ld_x, F32)
            for b4 in range(4):
                @blk.tensor
                def _(tt, b4=b4, t=t):
                    ctx.act.wait(tt)
                    ctx.dve.wait(tt)
                    for j in range(4):
                        jj = b4 * 4 + j
                        ins = tt.transpose(pb[6 + b4 % 2][:, j * 128:(j + 1) * 128],
                                           scr[:, jj * 128:(jj + 1) * 128], identf[:, :])
                    ctx.pe.inc(ins)
                @blk.vector
                def _(v, b4=b4, t=t):
                    ctx.pe.wait(v)
                    ctx.dve.inc(v.tensor_copy(
                        xln_v[:, b4 * 4:b4 * 4 + 4, t * 128:(t + 1) * 128],
                        pb[6 + b4 % 2][:, 0:512].rearrange("p (j c) -> p j c", c=128)))
        with blk_new() as blk:
            @blk.tensor
            def _(tt, t=t):
                ld_r.wait(tt)
                ctx.act.wait(tt)
                for j in range(NI):
                    ins = tt.matmul(pb[5][:, 0:8], xln_v[:, j, t * 128:(t + 1) * 128],
                                    rtr[:, j * 8:(j + 1) * 8],
                                    start=(j == 0), stop=(j == NI - 1))
                ctx.pe.inc(ins)
            @blk.vector
            def _(v, t=t):
                ctx.pe.wait(v)
                ctx.dve.wait(v)
                def step(ins):
                    ctx.dve.inc(ins)
                    ctx.dve.wait(v)
                step(v.tensor_copy(lgt[:, :], pb[5][:, 0:8]))
                step(v.reduce_max(pstat[:, 0:1], lgt[:, :], axis=AX.X))
                step(v.tensor_scalar(pstat[:, 1:2], pstat[:, 0:1], -1.0, None, ALU.mult))
            @blk.scalar
            def _(sc, t=t):
                ctx.dve.wait(sc)
                ctx.act.inc(sc.activation(probs[:, :], lgt[:, :], AF.Exp,
                                          bias=pstat[:, 1:2], accum_out=pstat[:, 2:3]))
            @blk.vector
            def _(v, t=t):
                ctx.act.wait(v)
                def step(ins):
                    ctx.dve.inc(ins)
                    ctx.dve.wait(v)
                step(v.reciprocal(pstat[:, 3:4], pstat[:, 2:3]))
                step(v.tensor_scalar(probs[:, :], probs[:, :], pstat[:, 3:4], None, ALU.mult))
                step(v.reduce_max(pstat[:, 4:5], probs[:, :], axis=AX.X))
                step(v.tensor_scalar(keep[:, :], probs[:, :], pstat[:, 4:5], None, ALU.is_ge))
                step(v.scalar_tensor_tensor(keep[:, :], keep[:, :], -1e30, probs[:, :],
                                            ALU.mult, ALU.add))
                step(v.reduce_max(pstat[:, 5:6], keep[:, :], axis=AX.X))
                step(v.tensor_scalar(keep[:, :], probs[:, :], pstat[:, 5:6], None, ALU.is_ge))
                step(v.tensor_mul(cw[:, t * 8:(t + 1) * 8], probs[:, :], keep[:, :]))

    def gated_block(w_gate_sl, w_up_sl, cw_col, hdst_v, f0_tiles, fcw, fc_off, tag):
        with blk_new() as blk:
            pem = {}
            for mat, (wsl, buf) in enumerate([(w_gate_sl, gbuf), (w_up_sl, ubuf)]):
                for d in range(NI):
                    si = mat * NI + d
                    @blk.gpsimd
                    def _(g, d=d, mat=mat, wsl=wsl, si=si):
                        par = si % 2
                        if si >= 2:
                            g.wait_ge(ctx.pe.h, pem[si - 2])
                        ld_w[par].inc(g.dma_start(wslc[par][:, 0:fcw], wsl(d)), 16)
                    @blk.tensor
                    def _(t, d=d, mat=mat, si=si):
                        par = si % 2
                        ld_w[par].wait(t)
                        if d == 0:
                            ctx.act.wait(t)
                        for tt in range(NT):
                            ins = t.matmul(pb[tt][:, 0:fcw], xln_v[:, d, tt * 128:(tt + 1) * 128],
                                           wslc[par][:, 0:fcw],
                                           start=(d == 0), stop=(d == NI - 1))
                        pem[si] = ctx.pe.inc(ins)
                @blk.scalar
                def _(sc, mat=mat, buf=buf):
                    ctx.pe.wait(sc)
                    ctx.dve.wait(sc)
                    for tt in range(NT):
                        ctx.act.inc(sc.activation(buf[:, tt * 512:tt * 512 + fcw],
                                                  pb[tt][:, 0:fcw],
                                                  AF.Silu if mat == 0 else AF.Identity))
            @blk.vector
            def _(v):
                ctx.act.wait(v)
                for tt in range(NT):
                    gsl = gbuf[:, tt * 512:tt * 512 + fcw]
                    usl = ubuf[:, tt * 512:tt * 512 + fcw]
                    if cw_col is None:
                        ctx.dve.inc(v.tensor_mul(gsl, gsl, usl))
                    else:
                        ctx.dve.inc(v.scalar_tensor_tensor(gsl, gsl, cw_col(tt), usl,
                                                           ALU.mult, ALU.mult))
            nfs = fcw // 128
            for tt in range(NT):
                @blk.tensor
                def _(t, tt=tt):
                    ctx.dve.wait(t)
                    if tt >= 2:
                        ctx.dve.wait(t)
                    for fsub in range(nfs):
                        ins = t.transpose(pb[6 + tt % 2][:, fsub * 128:(fsub + 1) * 128],
                                          gbuf[:, tt * 512 + fsub * 128:tt * 512 + (fsub + 1) * 128],
                                          identf[:, :])
                    ctx.pe.inc(ins)
                @blk.vector
                def _(v, tt=tt):
                    ctx.pe.wait(v)
                    ctx.dve.inc(v.tensor_copy(
                        hdst_v[:, f0_tiles:f0_tiles + nfs, tt * 128:(tt + 1) * 128],
                        pb[6 + tt % 2][:, 0:nfs * 128].rearrange("p (j c) -> p j c", c=128)))

    for fc in range(3):
        gated_block(lambda d, fc=fc: io["shared_gate"][d * 128:(d + 1) * 128,
                                                       fc * 512:(fc + 1) * 512],
                    lambda d, fc=fc: io["shared_up"][d * 128:(d + 1) * 128,
                                                     fc * 512:(fc + 1) * 512],
                    None, hsh_v, fc * 4, 512, fc, "sh%d" % fc)
    for e in range(E):
        for fc, fcw in [(0, 512), (1, 256)]:
            gated_block(lambda d, e=e, fc=fc, fcw=fcw: io["w_gate"][e][d * 128:(d + 1) * 128,
                                                                      fc * 512:fc * 512 + fcw],
                        lambda d, e=e, fc=fc, fcw=fcw: io["w_up"][e][d * 128:(d + 1) * 128,
                                                                    fc * 512:fc * 512 + fcw],
                        (lambda tt, e=e: cw[:, tt * 8 + e:tt * 8 + e + 1]),
                        hex_v[e], fc * 4, fcw, fc, "e%d_%d" % (e, fc))

    for oc in range(4):
        with blk_new() as blk:
            slices = [("shared_down", None, f) for f in range(FS // 128)]
            for e in range(E):
                slices += [("w_down", e, f) for f in range(FE // 128)]
            pem = {}
            for si, (nm, e, f) in enumerate(slices):
                @blk.gpsimd
                def _(g, si=si, nm=nm, e=e, f=f, oc=oc):
                    par = si % 2
                    if si >= 2:
                        g.wait_ge(ctx.pe.h, pem[si - 2])
                    src = (io[nm][f * 128:(f + 1) * 128, oc * 512:(oc + 1) * 512]
                           if e is None else
                           io[nm][e][f * 128:(f + 1) * 128, oc * 512:(oc + 1) * 512])
                    ld_w[par].inc(g.dma_start(wslc[par][:, :], src), 16)
                @blk.tensor
                def _(t, si=si, e=e, f=f):
                    par = si % 2
                    ld_w[par].wait(t)
                    if si == 0:
                        ctx.dve.wait(t)
                    hsrc = hsh_v if e is None else hex_v[e]
                    for tt in range(NT):
                        ins = t.matmul(pb[tt][:, :], hsrc[:, f, tt * 128:(tt + 1) * 128],
                                       wslc[par][:, :],
                                       start=(si == 0), stop=(si == len(slices) - 1))
                    pem[si] = ctx.pe.inc(ins)
            pe_done = ctx.pe.n
            t_marks = {}
            for tt in range(NT):
                @blk.sync
                def _(s, tt=tt, oc=oc):
                    if tt >= 2:
                        s.wait_ge(ctx.dve.h, t_marks[tt - 2])
                    ld_xr[tt % 2].inc(s.dma_start(xres[tt % 2][:, :],
                                                  io["x_own"][tt * 128:(tt + 1) * 128,
                                                              oc * 512:(oc + 1) * 512]), 16)
                @blk.vector
                def _(v, tt=tt, oc=oc):
                    v.wait_ge(ctx.pe.h, pe_done)
                    ld_xr[tt % 2].wait(v)
                    v.wait_ge(ctx.ld2.h, ctx.ld2.n)
                    t_marks[tt] = ctx.dve.inc(v.tensor_add(ysb[tt % 2][:, :],
                                                           xres[tt % 2][:, :],
                                                           pb[tt][:, :]))
                @blk.scalar
                def _(sc, tt=tt, oc=oc):
                    sc.wait_ge(ctx.dve.h, t_marks[tt])
                    ctx.ld2.inc(sc.dma_start(io["y"][tt * 128:(tt + 1) * 128,
                                                     oc * 512:(oc + 1) * 512],
                                             ysb[tt % 2][:, :]), 16)
    with blk_new() as blk:
        @blk.scalar
        def _(sc):
            ctx.ld2.wait(sc)


def emit_ln_core(ctx, blk, src_f32, scr_f32, out_tm, ld, out_dt):
    if not hasattr(ctx, "_ln_st2"):
        ctx._ln_st2 = ctx.sb("ln_stats2", [128, 8], F32)
    st = ctx._ln_st2
    rs, mean, sumsq, var, rstd, nb = (st[:, 0:1], st[:, 1:2], st[:, 2:3], st[:, 3:4],
                                      st[:, 4:5], st[:, 5:6])

    @blk.scalar
    def _(sc):
        ld.wait(sc)
        ctx.dve.wait(sc)
        ctx.act.wait(sc)
        ctx.act.inc(sc.activation(scr_f32[:, :], src_f32[:, :], AF.Square,
                                  accum_out=sumsq))

    @blk.vector
    def _(v):
        ld.wait(v)
        ctx.dve.inc(v.reduce_sum(rs, src_f32[:, :], axis=AX.X))
        ctx.dve.wait(v)
        ctx.dve.inc(v.tensor_scalar(mean[:, :], rs, 1.0 / D, None, ALU.mult))
        ctx.act.wait(v)
        ctx.dve.wait(v)
        ctx.dve.inc(v.tensor_tensor(var[:, :], mean, mean, ALU.mult))
        ctx.dve.wait(v)
        ctx.dve.inc(v.scalar_tensor_tensor(var[:, :], sumsq, 1.0 / D, var,
                                           ALU.mult, ALU.subtract))

    @blk.scalar
    def _(sc):
        ctx.dve.wait(sc)
        ctx.act.inc(sc.activation(var[:, :], var[:, :], AF.Sqrt, bias=ctx.eps_col[:, :]))

    @blk.vector
    def _(v):
        ctx.act.wait(v)
        ctx.dve.inc(v.reciprocal(rstd[:, :], var[:, :]))
        ctx.dve.wait(v)
        ctx.dve.inc(v.scalar_tensor_tensor(nb[:, :], mean, -1.0, rstd, ALU.mult, ALU.mult))

    @blk.scalar
    def _(sc):
        ctx.dve.wait(sc)
        ctx.act.inc(sc.activation(out_tm[:, :], src_f32[:, :], AF.Identity,
                                  bias=nb, scale=rstd))



def _theta_split():
    th64 = 10000.0 ** (-np.arange(0, 128, 2, dtype=np.float64) / 128.0)
    th64 = np.concatenate([th64, th64])
    hi = th64.astype(np.float32)
    lo = (th64 - hi.astype(np.float64)).astype(np.float32)
    return hi.reshape(1, 128), lo.reshape(1, 128)


def _csub():
    c = np.zeros((128, 64), dtype=ml_dtypes.bfloat16)
    for j in range(64):
        c[j, j] = 1.0
        c[64 + j, j] = -1.0
    return c


def _bias_cols(b):
    return np.ascontiguousarray(np.asarray(b, dtype=np.float32).reshape(16, 128).T)


def _perm_rows(w):
    w = np.asarray(w)
    v = w.reshape(16, 64, 2, *w.shape[1:])
    return np.ascontiguousarray(v.transpose(0, 2, 1, *range(3, v.ndim)).reshape(w.shape))


def build_l0_attn():
    nc = bass.Bass("TRN2", target_bir_lowering=False)
    io = {}
    io["x_own"] = nc.dram_tensor("x_own", [TOWN, D], F32, kind="ExternalInput").ap()
    io["x_oth"] = nc.dram_tensor("x_oth", [TOWN, D], F32, kind="ExternalInput").ap()
    for nm in ["wq_p", "wk_p", "wv", "wo"]:
        io[nm] = nc.dram_tensor(nm, [D, D], F32, kind="ExternalInput").ap()
    for nm in ["bq_c", "bk_c", "bv_c"]:
        io[nm] = nc.dram_tensor(nm, [128, H], F32, kind="ExternalInput").ap()
    io["bo"] = nc.dram_tensor("bo", [D], F32, kind="ExternalInput").ap()
    io["pos_all"] = nc.dram_tensor("pos_all", [1, TSEQ], F32, kind="ExternalInput").ap()
    io["th_hi"] = nc.dram_tensor("th_hi", [1, 128], F32, kind="ExternalInput").ap()
    io["th_lo"] = nc.dram_tensor("th_lo", [1, 128], F32, kind="ExternalInput").ap()
    io["csub"] = nc.dram_tensor("csub", [128, 64], BF16, kind="ExternalInput").ap()
    io["y"] = nc.dram_tensor("y", [TOWN, D], F32, kind="ExternalOutput").ap()
    ctx = Ctx(nc)
    emit_l0_attn(ctx, nc.Block, io)
    return nc


def run_l0_attn(x_full, wq, bq, wk, bk, wv, bv, wo, bo):
    th_hi, th_lo = _theta_split()
    wq_p, wk_p = _perm_rows(wq), _perm_rows(wk)
    bq_c = _bias_cols(_perm_rows(bq))
    bk_c = _bias_cols(_perm_rows(bk))
    bv_c = _bias_cols(np.asarray(bv))
    nc = build_l0_attn()
    in_maps = []
    for c in range(8):
        own = x_full[c * 1024:(c + 1) * 1024]
        oth = x_full[(c ^ 1) * 1024:((c ^ 1) + 1) * 1024]
        pos_own = (c % 2) * 1024 + np.arange(1024, dtype=np.float32)
        pos_oth = ((c ^ 1) % 2) * 1024 + np.arange(1024, dtype=np.float32)
        pos_all = np.concatenate([pos_own, pos_oth]).reshape(1, 2048)
        in_maps.append(dict(x_own=own, x_oth=oth, wq_p=wq_p, wk_p=wk_p, wv=wv, wo=wo,
                            bq_c=bq_c, bk_c=bk_c, bv_c=bv_c, bo=bo, pos_all=pos_all,
                            th_hi=th_hi, th_lo=th_lo, csub=_csub()))
    res = run_bass_kernel_spmd(nc, in_maps, core_ids=list(range(8)))
    return np.concatenate([r["y"] for r in res.results], axis=0)


def build_l0_ffn(ncores=8):
    nc = bass.Bass("TRN2", target_bir_lowering=False)
    io = {}
    io["x_own"] = nc.dram_tensor("x_own", [TOWN, D], F32, kind="ExternalInput").ap()
    io["w"] = nc.dram_tensor("w", [D, D], BF16, kind="ExternalInput").ap()
    io["y"] = nc.dram_tensor("y", [TOWN, D], F32, kind="ExternalOutput").ap()
    io["cc_in"] = nc.dram_tensor("cc_in", [1, 1], F32).ap()
    io["cc_out"] = nc.dram_tensor("cc_out", [1, 1], F32).ap()
    ctx = Ctx(nc)
    emit_l0_ffn(ctx, nc.Block, io, ncores=ncores)
    return nc


def run_l0_ffn(x_full, w_bf16):
    nc = build_l0_ffn(8)
    in_maps = [dict(x_own=x_full[c * 1024:(c + 1) * 1024], w=w_bf16) for c in range(8)]
    res = run_bass_kernel_spmd(nc, in_maps, core_ids=list(range(8)))
    return np.concatenate([r["y"] for r in res.results], axis=0)


def _cadd():
    c = np.zeros((128, 64), dtype=ml_dtypes.bfloat16)
    for j in range(64):
        c[j, j] = 1.0
        c[64 + j, j] = 1.0
    return c


def build_l1_attn():
    nc = bass.Bass("TRN2", target_bir_lowering=False)
    io = {}
    io["x_own"] = nc.dram_tensor("x_own", [TOWN, D], F32, kind="ExternalInput").ap()
    io["x_prev"] = nc.dram_tensor("x_prev", [256, D], F32, kind="ExternalInput").ap()
    for nm in ["Wq_p", "Wk_p", "Wv", "Wo"]:
        io[nm] = nc.dram_tensor(nm, [D, D], F32, kind="ExternalInput").ap()
    io["pos_all"] = nc.dram_tensor("pos_all", [1, 1536], F32, kind="ExternalInput").ap()
    io["th_hi"] = nc.dram_tensor("th_hi", [1, 128], F32, kind="ExternalInput").ap()
    io["th_lo"] = nc.dram_tensor("th_lo", [1, 128], F32, kind="ExternalInput").ap()
    io["csub"] = nc.dram_tensor("csub", [128, 64], BF16, kind="ExternalInput").ap()
    io["cadd"] = nc.dram_tensor("cadd", [128, 64], BF16, kind="ExternalInput").ap()
    io["prevflag"] = nc.dram_tensor("prevflag", [128, 1], F32, kind="ExternalInput").ap()
    io["y"] = nc.dram_tensor("y", [TOWN, D], F32, kind="ExternalOutput").ap()
    ctx = Ctx(nc)
    emit_l1_attn(ctx, nc.Block, io)
    return nc


def _l1_in_map(x_full, c, Wq_p, Wk_p, Wv, Wo, th_hi, th_lo):
    own = x_full[c * 1024:(c + 1) * 1024]
    if c % 2 == 1:
        x_prev = x_full[c * 1024 - 256:c * 1024]
        pos_prev = 768.0 + np.arange(256, dtype=np.float32)
        flag = np.ones((128, 1), np.float32)
    else:
        x_prev = np.zeros((256, D), np.float32)
        pos_prev = np.arange(256, dtype=np.float32)
        flag = np.zeros((128, 1), np.float32)
    pos_own = (c % 2) * 1024 + np.arange(1024, dtype=np.float32)
    pos_all = np.concatenate([pos_prev, pos_own,
                              np.zeros(256, np.float32)]).reshape(1, 1536)
    return dict(x_own=own, x_prev=x_prev, Wq_p=Wq_p, Wk_p=Wk_p, Wv=Wv, Wo=Wo,
                pos_all=pos_all, th_hi=th_hi, th_lo=th_lo, csub=_csub(),
                cadd=_cadd(), prevflag=flag)


def run_l1_attn(x_full, Wq, Wk, Wv, Wo):
    th_hi, th_lo = _theta_split()
    Wq_p, Wk_p = _perm_rows(Wq), _perm_rows(Wk)
    nc = build_l1_attn()
    in_maps = [_l1_in_map(x_full, c, Wq_p, Wk_p, Wv, Wo, th_hi, th_lo) for c in range(8)]
    res = run_bass_kernel_spmd(nc, in_maps, core_ids=list(range(8)))
    return np.concatenate([r["y"] for r in res.results], axis=0)


def build_l1_ffn():
    nc = bass.Bass("TRN2", target_bir_lowering=False)
    io = {}
    io["x_own"] = nc.dram_tensor("x_own", [TOWN, D], F32, kind="ExternalInput").ap()
    io["router"] = nc.dram_tensor("router", [D, 8], F32, kind="ExternalInput").ap()
    io["shared_gate"] = nc.dram_tensor("shared_gate", [D, 1536], F32, kind="ExternalInput").ap()
    io["shared_up"] = nc.dram_tensor("shared_up", [D, 1536], F32, kind="ExternalInput").ap()
    io["shared_down"] = nc.dram_tensor("shared_down", [1536, D], F32, kind="ExternalInput").ap()
    io["w_gate"] = nc.dram_tensor("w_gate", [8, D, 768], F32, kind="ExternalInput").ap()
    io["w_up"] = nc.dram_tensor("w_up", [8, D, 768], F32, kind="ExternalInput").ap()
    io["w_down"] = nc.dram_tensor("w_down", [8, 768, D], F32, kind="ExternalInput").ap()
    io["y"] = nc.dram_tensor("y", [TOWN, D], F32, kind="ExternalOutput").ap()
    ctx = Ctx(nc)
    emit_l1_ffn(ctx, nc.Block, io)
    return nc


def run_l1_ffn(x_full, router, sg, su, sd, wg, wu, wd):
    nc = build_l1_ffn()
    in_maps = [dict(x_own=x_full[c * 1024:(c + 1) * 1024], router=router,
                    shared_gate=sg, shared_up=su, shared_down=sd,
                    w_gate=wg, w_up=wu, w_down=wd) for c in range(8)]
    res = run_bass_kernel_spmd(nc, in_maps, core_ids=list(range(8)))
    return np.concatenate([r["y"] for r in res.results], axis=0)


def build_fused(ncores=8):
    nc = bass.Bass("TRN2", target_bir_lowering=False)
    EI = lambda nm, sh, dt=F32: nc.dram_tensor(nm, sh, dt, kind="ExternalInput").ap()
    x_own = EI("x_own", [TOWN, D])
    x_oth = EI("x_oth", [TOWN, D])
    y1 = nc.dram_tensor("y1", [TOWN, D], F32).ap()
    y2 = nc.dram_tensor("y2", [TOWN, D], F32).ap()
    y3 = nc.dram_tensor("y3", [TOWN, D], F32).ap()
    prevg = nc.dram_tensor("prevg", [512, D], F32).ap()
    yout = nc.dram_tensor("y", [TOWN, D], F32, kind="ExternalOutput").ap()
    cc_in = nc.dram_tensor("cc_in", [1, 1], F32).ap()
    cc_out = nc.dram_tensor("cc_out", [1, 1], F32).ap()

    ioA = dict(x_own=x_own, x_oth=x_oth, y=y1,
               wq_p=EI("wq_p", [D, D]), wk_p=EI("wk_p", [D, D]),
               wv=EI("wv", [D, D]), wo=EI("wo", [D, D]),
               bq_c=EI("bq_c", [128, H]), bk_c=EI("bk_c", [128, H]),
               bv_c=EI("bv_c", [128, H]), bo=EI("bo", [D]),
               pos_all=EI("pos_allA", [1, TSEQ]),
               th_hi=EI("th_hiA", [1, 128]), th_lo=EI("th_loA", [1, 128]),
               csub=EI("csubA", [128, 64], BF16))
    ioB = dict(x_own=y1, w=EI("w8", [D, D], BF16), y=y2, cc_in=cc_in, cc_out=cc_out)
    ioC = dict(x_own=y2, x_prev=prevg[0:256, :], y=y3,
               Wq_p=EI("Wq_p", [D, D]), Wk_p=EI("Wk_p", [D, D]),
               Wv=EI("Wv1", [D, D]), Wo=EI("Wo1", [D, D]),
               pos_all=EI("pos_allC", [1, 1536]),
               th_hi=EI("th_hiC", [1, 128]), th_lo=EI("th_loC", [1, 128]),
               csub=EI("csubC", [128, 64], BF16), cadd=EI("caddC", [128, 64], BF16),
               prevflag=EI("prevflag", [128, 1]))
    ioD = dict(x_own=y3, y=yout, router=EI("router", [D, 8]),
               shared_gate=EI("shared_gate", [D, 1536]),
               shared_up=EI("shared_up", [D, 1536]),
               shared_down=EI("shared_down", [1536, D]),
               w_gate=EI("w_gate", [8, D, 768]), w_up=EI("w_up", [8, D, 768]),
               w_down=EI("w_down", [8, 768, D]))

    ctx = Ctx(nc)

    blkno = [0]

    def run_block(emit, io, **kw):
        ctx.prefix = "blk%d_" % blkno[0]
        blkno[0] += 1
        with ExitStack() as st:
            ctx.stack = st
            emit(ctx, nc.Block, io, **kw)
        ctx.stack = None
        ctx.prefix = ""
        for attr in ("_ln_st", "_ln_st2", "eps_col"):
            if hasattr(ctx, attr):
                delattr(ctx, attr)

    run_block(emit_l0_attn, ioA)
    run_block(emit_l0_ffn, ioB, ncores=ncores)
    ccg = Trk(nc, "ccg")
    groups = [[2 * i, 2 * i + 1] for i in range(ncores // 2)]
    with nc.Block() as blk:
        @blk.gpsimd
        def _(g):
            g.wait_ge(ctx.ld2.h, ctx.ld2.n)
            ccg.inc(g.collective_compute("AllGather", ALU.bypass, groups,
                                         [y2[768:1024, :].opt()], [prevg.opt()]))
            ccg.wait(g)
    run_block(emit_l1_attn, ioC)
    run_block(emit_l1_ffn, ioD)
    return nc


def kernel(L0_attn_x, L0_attn_wq, L0_attn_bq, L0_attn_wk, L0_attn_bk, L0_attn_wv,
           L0_attn_bv, L0_attn_wo, L0_attn_bo, L0_ffn_weight, L1_attn_Wq, L1_attn_Wk,
           L1_attn_Wv, L1_attn_Wo, L1_ffn_router, L1_ffn_shared_gate, L1_ffn_shared_up,
           L1_ffn_shared_down, L1_ffn_w_gate, L1_ffn_w_up, L1_ffn_w_down):
    x = np.ascontiguousarray(np.asarray(L0_attn_x, dtype=np.float32).reshape(8192, 2048))
    th_hi, th_lo = _theta_split()
    wq_p, wk_p = _perm_rows(L0_attn_wq), _perm_rows(L0_attn_wk)
    bq_c = _bias_cols(_perm_rows(L0_attn_bq))
    bk_c = _bias_cols(_perm_rows(L0_attn_bk))
    bv_c = _bias_cols(L0_attn_bv)
    Wq_p, Wk_p = _perm_rows(L1_attn_Wq), _perm_rows(L1_attn_Wk)
    nc = build_fused(8)
    in_maps = []
    for c in range(8):
        pos_own = (c % 2) * 1024 + np.arange(1024, dtype=np.float32)
        pos_oth = ((c ^ 1) % 2) * 1024 + np.arange(1024, dtype=np.float32)
        posA = np.concatenate([pos_own, pos_oth]).reshape(1, 2048)
        if c % 2 == 1:
            pos_prev = 768.0 + np.arange(256, dtype=np.float32)
            flag = np.ones((128, 1), np.float32)
        else:
            pos_prev = np.arange(256, dtype=np.float32)
            flag = np.zeros((128, 1), np.float32)
        posC = np.concatenate([pos_prev, pos_own,
                               np.zeros(256, np.float32)]).reshape(1, 1536)
        in_maps.append(dict(
            x_own=x[c * 1024:(c + 1) * 1024],
            x_oth=x[(c ^ 1) * 1024:((c ^ 1) + 1) * 1024],
            wq_p=wq_p, wk_p=wk_p, wv=np.asarray(L0_attn_wv), wo=np.asarray(L0_attn_wo),
            bq_c=bq_c, bk_c=bk_c, bv_c=bv_c, bo=np.asarray(L0_attn_bo),
            pos_allA=posA, th_hiA=th_hi, th_loA=th_lo, csubA=_csub(),
            w8=np.asarray(L0_ffn_weight),
            Wq_p=Wq_p, Wk_p=Wk_p, Wv1=np.asarray(L1_attn_Wv), Wo1=np.asarray(L1_attn_Wo),
            pos_allC=posC, th_hiC=th_hi, th_loC=th_lo, csubC=_csub(), caddC=_cadd(),
            prevflag=flag,
            router=np.asarray(L1_ffn_router),
            shared_gate=np.asarray(L1_ffn_shared_gate),
            shared_up=np.asarray(L1_ffn_shared_up),
            shared_down=np.asarray(L1_ffn_shared_down),
            w_gate=np.asarray(L1_ffn_w_gate), w_up=np.asarray(L1_ffn_w_up),
            w_down=np.asarray(L1_ffn_w_down)))
    res = run_bass_kernel_spmd(nc, in_maps, core_ids=list(range(8)))
    y = np.concatenate([r["y"] for r in res.results], axis=0)
    return y.reshape(4, 2048, 2048)


# version 04
# speedup vs baseline: 1.3386x; 1.3386x over previous
import math
from contextlib import ExitStack
import numpy as np
import ml_dtypes
import concourse.bass as bass
import concourse.mybir as mybir
from concourse.bass_utils import run_bass_kernel_spmd

F32 = mybir.dt.float32
BF16 = mybir.dt.bfloat16
I32 = mybir.dt.int32
AF = mybir.ActivationFunctionType
ALU = mybir.AluOpType
AX = mybir.AxisListType

D = 2048
H = 16
TOWN = 1024
TSEQ = 2048
NT = TOWN // 128
NTS = TSEQ // 128
NI = D // 128

TWO_PI = 6.283185307179586
INV_2PI = 1.0 / TWO_PI
C_HI = 6.28125
C_LO = 0.00193530717958647692
PI_C = float(np.nextafter(np.float32(np.pi), np.float32(0.0)))
HALF_PI = float(np.float32(np.pi / 2))
ATTN_SCALE = 1.0 / math.sqrt(128.0)


_TRKN = [0]


class Trk:
    def __init__(self, nc, name):
        _TRKN[0] += 1
        self.h = nc.alloc_semaphore("%s_%d" % (name, _TRKN[0]))
        self.n = 0

    def inc(self, instr, k=1):
        instr.then_inc(self.h, k)
        self.n += k
        return self.n

    def wait(self, eng, val=None):
        eng.wait_ge(self.h, self.n if val is None else val)


class Ctx:
    def __init__(self, nc):
        self.nc = nc
        self.pe = Trk(nc, "t_pe")
        self.act = Trk(nc, "t_act")
        self.dve = Trk(nc, "t_dve")
        self.pool = Trk(nc, "t_pool")
        self.ld = Trk(nc, "t_ld")
        self.ld2 = Trk(nc, "t_ld2")
        self.ld3 = Trk(nc, "t_ld3")

        self.stack = None
        self.prefix = ""

    def sb(self, name, shape, dt):
        name = self.prefix + name
        if self.stack is not None:
            return self.stack.enter_context(self.nc.sbuf_tensor(name, shape, dt))
        return self.nc.alloc_sbuf_tensor(name, shape, dt)

    def ps(self, name, shape, dt):
        name = self.prefix + name
        if self.stack is not None:
            return self.stack.enter_context(self.nc.psum_tensor(name, shape, dt))
        return self.nc.alloc_psum_tensor(name, shape, dt)


def emit_common_consts(ctx, blk_new):
    nc = ctx.nc
    ident = ctx.sb("c_ident", [128, 128], BF16)
    ones_row = ctx.sb("c_ones_row", [1, 512], BF16)
    ctx.eps_col = ctx.sb("c_eps", [128, 1], F32)
    with blk_new() as blk:
        @blk.gpsimd
        def _(g):
            ctx.pool.inc(g.memset(ident[:, :], 1.0))
            g.memset(ones_row[:, :], 1.0)
            g.memset(ctx.eps_col[:, :], 1e-5)
            ctx.pool.wait(g)
            g.affine_select(ident[:, :], ident[:, :], [[1, 128]], ALU.is_equal, 0.0,
                            base=0, channel_multiplier=-1)
    return ident, ones_row


def emit_rope_tables(ctx, blk_new, pfr, scrA, scrB, pos_ap, theta_hi_ap, theta_lo_ap,
                     ncols, name, swap=False):
    nc = ctx.nc
    cs_t = ctx.sb(name + "_cs", [128, ncols], F32)
    s_pos = scrB[0:1, 0:ncols]
    s_th = ctx.sb(name + "_th", [1, 128], F32)
    s_tl = ctx.sb(name + "_tl", [1, 128], F32)
    s_u = scrA[:, 0:512]
    s_n = ctx.sb(name + "_n", [128, 512], I32)
    s_nf = scrA[:, 512:1024]
    s_m = scrA[:, 1024:1536]
    s_w = scrA[:, 1536:2048]
    cos_half = slice(64, 128) if swap else slice(0, 64)
    ldm = Trk(ctx.nc, name + "_ld")
    with blk_new() as blk:
        @blk.sync
        def _(s):
            ldm.inc(s.dma_start(s_pos, pos_ap), 16)
            ldm.inc(s.dma_start(s_th[:, :], theta_hi_ap), 16)
            ldm.inc(s.dma_start(s_tl[:, :], theta_lo_ap), 16)
            ldm.wait(s)
    nch = ncols // 512
    for c in range(nch):
        cs = slice(c * 512, (c + 1) * 512)
        with blk_new() as blk:
            @blk.tensor
            def _(t):
                ldm.wait(t)
                t.matmul(pfr[:, :], s_th[:, :], scrB[0:1, c * 512:(c + 1) * 512],
                         start=True, stop=False)
                ctx.pe.inc(t.matmul(pfr[:, :], s_tl[:, :], scrB[0:1, c * 512:(c + 1) * 512],
                                    start=False, stop=True))
            @blk.vector
            def _(v):
                ctx.pe.wait(v)
                def step(ins):
                    ctx.dve.inc(ins)
                    ctx.dve.wait(v)
                step(v.tensor_scalar(s_u, pfr[:, :], INV_2PI, None, ALU.mult))
                step(v.tensor_copy(s_n[:, :], s_u))
                step(v.tensor_copy(s_nf, s_n[:, :]))
                step(v.scalar_tensor_tensor(s_m, s_nf, -C_HI, pfr[:, :], ALU.mult, ALU.add))
                step(v.scalar_tensor_tensor(s_m, s_nf, -C_LO, s_m, ALU.mult, ALU.add))
                step(v.tensor_scalar(s_w, s_m, -PI_C, TWO_PI, ALU.is_lt, ALU.mult))
                step(v.tensor_add(s_m, s_m, s_w))
                step(v.tensor_scalar(s_w, s_m, PI_C, TWO_PI, ALU.is_gt, ALU.mult))
                step(v.tensor_sub(s_m, s_m, s_w))
                step(v.tensor_scalar(s_m, s_m, -PI_C, PI_C, ALU.max, ALU.min))
                mh = s_m[cos_half, :]
                wh = s_w[cos_half, :]
                step(v.tensor_scalar(mh, mh, HALF_PI, None, ALU.add))
                step(v.tensor_scalar(wh, mh, PI_C, TWO_PI, ALU.is_gt, ALU.mult))
                step(v.tensor_sub(mh, mh, wh))
                step(v.tensor_scalar(mh, mh, -PI_C, PI_C, ALU.max, ALU.min))
            @blk.scalar
            def _(sc):
                ctx.dve.wait(sc)
                ctx.act.inc(sc.activation(cs_t[:, cs], s_m, AF.Sin))
    return cs_t


def emit_ln_fm(ctx, blk, src_f32, xln_fm, tcol0, pbank_t, scr_f32, xln_tm, ident,
               fm_cols, ld):
    if not hasattr(ctx, "_ln_st"):
        ctx._ln_st = ctx.sb("ln_stats", [128, 8], F32)
    st = ctx._ln_st
    rs, mean, sumsq, var, rstd, nb = (st[:, 0:1], st[:, 1:2], st[:, 2:3], st[:, 3:4],
                                      st[:, 4:5], st[:, 5:6])

    @blk.scalar
    def _(sc):
        ld.wait(sc)
        ctx.dve.wait(sc)
        ctx.act.wait(sc)
        ctx.act.inc(sc.activation(scr_f32[:, :], src_f32[:, :], AF.Square,
                                  accum_out=sumsq))

    @blk.vector
    def _(v):
        ld.wait(v)
        ctx.dve.inc(v.reduce_sum(rs, src_f32[:, :], axis=AX.X))
        ctx.dve.wait(v)
        ctx.dve.inc(v.tensor_scalar(mean[:, :], rs, 1.0 / D, None, ALU.mult))
        ctx.act.wait(v)
        ctx.dve.wait(v)
        ctx.dve.inc(v.tensor_tensor(var[:, :], mean, mean, ALU.mult))
        ctx.dve.wait(v)
        ctx.dve.inc(v.scalar_tensor_tensor(var[:, :], sumsq, 1.0 / D, var, ALU.mult, ALU.subtract))

    @blk.scalar
    def _(sc):
        ctx.dve.wait(sc)
        ctx.act.inc(sc.activation(var[:, :], var[:, :], AF.Sqrt, bias=ctx.eps_col[:, :]))

    @blk.vector
    def _(v):
        ctx.act.wait(v)
        ctx.dve.inc(v.reciprocal(rstd[:, :], var[:, :]))
        ctx.dve.wait(v)
        ctx.dve.inc(v.scalar_tensor_tensor(nb[:, :], mean, -1.0, rstd, ALU.mult, ALU.mult))

    @blk.scalar
    def _(sc):
        ctx.dve.wait(sc)
        ctx.act.inc(sc.activation(xln_tm[:, :], src_f32[:, :], AF.Identity,
                                  bias=nb, scale=rstd))

    n_feat_tiles = D // 128
    for half in range(n_feat_tiles // 8):
        @blk.tensor
        def _(t, half=half):
            ctx.act.wait(t)
            ctx.dve.wait(t)
            for j in range(8):
                jj = half * 8 + j
                ins = t.transpose(pbank_t[:, j * 128:(j + 1) * 128],
                                  xln_tm[:, jj * 128:(jj + 1) * 128], ident[:, :])
            ctx.pe.inc(ins)

        @blk.vector
        def _(v, half=half):
            ctx.pe.wait(v)
            dst = xln_fm[:, :].rearrange("p (i c) -> p i c", c=fm_cols)
            ctx.dve.inc(v.tensor_copy(
                dst[:, half * 8:(half + 1) * 8, tcol0:tcol0 + 128],
                pbank_t[:, :].rearrange("p (i c) -> p i c", c=128)))


def emit_l0_attn(ctx, blk_new, io):
    nc = ctx.nc
    pk = [ctx.ps("a_pk%d" % i, [128, 512], F32) for i in range(2)]
    pctx = [ctx.ps("a_pctx%d" % i, [128, 512], F32) for i in range(4)]
    ptr = [ctx.ps("a_ptr%d" % i, [128, 1024], BF16) for i in range(2)]

    ident, ones_row = emit_common_consts(ctx, blk_new)
    ld_x = Trk(nc, "a_ld_x")
    ld_w = Trk(nc, "a_ld_w")
    ld_wo = Trk(nc, "a_ld_wo")
    ld_b = Trk(nc, "a_ld_b")
    ld_xr = [Trk(nc, "a_ld_xr%d" % i) for i in range(2)]

    xln = ctx.sb("a_xln", [128, NI * TSEQ], BF16)
    wraw = [ctx.sb("a_wraw%d" % i, [128, D], F32) for i in range(2)]
    scr_f32 = wraw[1]
    xstage = wraw[0]
    xln_tm = ctx.sb("a_xlntm", [128, D], BF16)
    wbf = ctx.sb("a_wbf", [128, D], BF16)
    wpool = ctx.sb("a_wpool", [128, 4 * D], BF16)
    kfm = ctx.sb("a_kfm", [128, TSEQ], BF16)
    qfm = ctx.sb("a_qfm", [128, TOWN], BF16)
    vfm = ctx.sb("a_vfm", [128, TSEQ], BF16)
    vone = ctx.sb("a_vone", [128, NTS * 129], BF16)
    ptil = [ctx.sb("a_ptil%d" % i, [128, 512], BF16) for i in range(2)]
    ropeT = ctx.sb("a_ropeT", [128, 512], BF16)
    csub = ctx.sb("a_csub", [128, 64], BF16)
    ctx_tm = ctx.sb("a_ctxtm", [128, 512], BF16)
    ctx_fm = ctx.sb("a_ctxfm", [128, H * TOWN], BF16)
    recips = ctx.sb("a_recips", [128, 4], F32)
    bqc = ctx.sb("a_bqc", [128, H], F32)
    bkc = ctx.sb("a_bkc", [128, H], F32)
    bvc = ctx.sb("a_bvc", [128, H], F32)
    bo_row = ctx.sb("a_borow", [1, D], BF16)
    xres = [ctx.sb("a_xres%d" % i, [128, 512], F32) for i in range(2)]
    ysb = [ctx.sb("a_ysb%d" % i, [128, 512], F32) for i in range(2)]

    cs_t = emit_rope_tables(ctx, blk_new, pctx[0], wraw[0], wraw[1],
                            io["pos_all"], io["th_hi"], io["th_lo"], TSEQ, "rt0")

    xln_v = xln[:, :].rearrange("p (i c) -> p i c", c=TSEQ)
    vone_v = vone[:, :].rearrange("p (t c) -> p t c", c=129)
    wqT_v = wpool[:, 0 * D:1 * D].rearrange("p (j c) -> p j c", c=128)
    wkT_v = wpool[:, 1 * D:2 * D].rearrange("p (j c) -> p j c", c=128)
    wvT_v = wpool[:, 2 * D:3 * D].rearrange("p (j c) -> p j c", c=128)
    woT_v = wpool[:, 0:4 * D].rearrange("p (j c) -> p j c", c=512)
    wT_v = {"k": wkT_v, "q": wqT_v, "v": wvT_v}

    with blk_new() as blk:
        @blk.sync
        def _(s):
            ld_b.inc(s.dma_start(bqc[:, :], io["bq_c"]), 16)
            ld_b.inc(s.dma_start(bkc[:, :], io["bk_c"]), 16)
            ld_b.inc(s.dma_start(bvc[:, :], io["bv_c"]), 16)
            ld_b.inc(s.dma_start(csub[:, :], io["csub"]), 16)
            ld_b.wait(s)
        @blk.gpsimd
        def _(g):
            ctx.ld3.inc(g.dma_start(bo_row[:, :], io["bo"].rearrange("(a d) -> a d", a=1)), 16)
            ctx.ld3.wait(g)
        @blk.vector
        def _(v):
            ctx.dve.inc(v.memset(vone_v[:, :, 128:129], 1.0))


    with nc.named_scope("a_ln"):
     for part in range(2):
        src_ap = io["x_own"] if part == 0 else io["x_oth"]
        for t in range(NT):
            with blk_new() as blk:
                @blk.sync
                def _(s, t=t, src_ap=src_ap):
                    s.wait_ge(ctx.act.h, ctx.act.n)
                    s.wait_ge(ctx.dve.h, ctx.dve.n)
                    s.wait_ge(ctx.pe.h, ctx.pe.n)
                    ld_x.inc(s.dma_start(xstage[:, :],
                                         src_ap[t * 128:(t + 1) * 128, :]), 16)
                    ld_x.wait(s)
                emit_ln_fm(ctx, blk, xstage, xln, part * TOWN + t * 128, ptr[0],
                           scr_f32, xln_tm, ident, TSEQ, ld_x)

    def transpose_w(blk, src_bf, dstT_v, nj, col_off=0):
        for bi, j0 in enumerate(range(0, nj, 8)):
            j1 = min(nj, j0 + 8)
            @blk.tensor
            def _(t, j0=j0, j1=j1, bi=bi):
                ctx.pool.wait(t)
                ctx.dve.wait(t)
                for j in range(j0, j1):
                    ins = t.transpose(ptr[bi % 2][:, (j - j0) * 128:(j - j0 + 1) * 128],
                                      src_bf[:, j * 128:(j + 1) * 128], ident[:, :])
                ctx.pe.inc(ins)
            @blk.vector
            def _(v, j0=j0, j1=j1, bi=bi):
                ctx.pe.wait(v)
                ctx.dve.inc(v.tensor_copy(
                    dstT_v[:, j0:j1, col_off:col_off + 128],
                    ptr[bi % 2][:, 0:(j1 - j0) * 128].rearrange("p (j c) -> p j c", c=128)))

    def proj_chunk(t, which, cs, bank):
        for j in range(NI):
            ins = t.matmul(pk[bank][:, :], wT_v[which][:, j, :], xln_v[:, j, cs],
                           start=(j == 0), stop=(j == NI - 1))
        return ctx.pe.inc(ins)

    wnames = ["wq_p", "wk_p", "wv"]
    ldw = [Trk(nc, "a_ldw%d" % m) for m in range(3)]
    wbf3 = [ctx.sb("a_wbf%d" % m, [128, D], BF16) for m in range(3)]
    with nc.named_scope("a_heads"), blk_new() as blk:
        marks = {
            "ptr_free": {0: ("dve", 0), 1: ("dve", 0)},
            "pk_free": {0: ("act", 0), 1: ("act", 0)},
            "rot_free": {0: ("act", 0), 1: ("act", 0)},
            "ctx_free": {i: ("dve", 0) for i in range(4)},
            "wb_free": {m: 0 for m in range(3)},
            "wT_ready": {m: 0 for m in range(3)},
            "kfm_free": 0, "qfm_free": 0, "pv_last": 0, "vfm_free": 0,
            "ctxtm_free": 0, "vone_dve": 0, "ropeT_dve": 0, "vfm_act": 0,
        }

        def wait_mark(eng, mk):
            nm, val = mk
            eng.wait_ge(getattr(ctx, nm).h, val)

        @blk.gpsimd
        def _(g):
            for m in range(3):
                ldw[m].inc(g.dma_start(wbf3[m][:, :], io[wnames[m]][0:128, :]), 16)

        for h in range(H):
            for m, dstv in enumerate([wqT_v, wkT_v, wvT_v]):
                for bi in range(2):
                    @blk.tensor
                    def _(t, m=m, bi=bi, h=h):
                        if bi == 0:
                            ldw[m].wait(t)
                        wait_mark(t, marks["ptr_free"][bi % 2])
                        for j in range(8):
                            jj = bi * 8 + j
                            ins = t.transpose(ptr[bi % 2][:, j * 128:(j + 1) * 128],
                                              wbf3[m][:, jj * 128:(jj + 1) * 128],
                                              ident[:, :])
                        mk = ctx.pe.inc(ins)
                        if bi == 1:
                            marks["wb_free"][m] = mk
                    @blk.vector
                    def _(v, m=m, bi=bi, dstv=dstv):
                        ctx.pe.wait(v)
                        mk = ctx.dve.inc(v.tensor_copy(
                            dstv[:, bi * 8:(bi + 1) * 8, 0:128],
                            ptr[bi % 2][:, :].rearrange("p (j c) -> p j c", c=128)))
                        marks["ptr_free"][bi % 2] = ("dve", mk)
                        if bi == 1:
                            marks["wT_ready"][m] = mk

            def proj_chunk_m(t, which, cs, bank):
                wait_mark(t, marks["pk_free"][bank])
                t.wait_ge(ctx.dve.h, marks["wT_ready"][{"q": 0, "k": 1, "v": 2}[which]])
                for j in range(NI):
                    ins = t.matmul(pk[bank][:, :], wT_v[which][:, j, :], xln_v[:, j, cs],
                                   start=(j == 0), stop=(j == NI - 1))
                return ctx.pe.inc(ins)

            def rope_emit(dst_fm, bc_even, bc_odd, cs, bank, dst_gate, h=h):
                @blk.vector
                def _(v):
                    ctx.pe.wait(v)
                    mk = ctx.dve.inc(v.scalar_tensor_tensor(ropeT[:, :], pk[bank][:, :],
                                                            bc_even if False else bc_full,
                                                            cs_t[:, cs], ALU.add, ALU.mult))
                    marks["ropeT_dve"] = mk
                @blk.tensor
                def _(t):
                    t.wait_ge(ctx.dve.h, marks["ropeT_dve"])
                    wait_mark(t, marks["rot_free"][bank])
                    ctx.pe.inc(t.matmul(pctx[bank][0:64, 0:512], csub[:, :], ropeT[:, :],
                                        start=True, stop=True))
                @blk.scalar
                def _(sc):
                    ctx.pe.wait(sc)
                    if dst_gate is not None and h > 0:
                        sc.wait_ge(ctx.pe.h, marks[dst_gate])
                    ctx.act.inc(sc.activation(dst_fm[64:128, cs], pk[bank][64:128, :],
                                              AF.Identity, bias=bc_odd))
                    mk = ctx.act.inc(sc.activation(dst_fm[0:64, cs],
                                                   pctx[bank][0:64, 0:512], AF.Identity))
                    marks["pk_free"][bank] = ("act", mk)
                    marks["rot_free"][bank] = ("act", mk)

            for c in range(4):
                cs = slice(c * 512, (c + 1) * 512)
                @blk.tensor
                def _(t, c=c, cs=cs):
                    proj_chunk_m(t, "k", cs, c % 2)
                bc_full = bkc[:, h:h + 1]
                rope_emit(kfm, bkc[0:64, h:h + 1], bkc[64:128, h:h + 1], cs, c % 2,
                          "kfm_free" if c == 0 else None)
            for c in range(2):
                cs = slice(c * 512, (c + 1) * 512)
                @blk.tensor
                def _(t, c=c, cs=cs):
                    proj_chunk_m(t, "q", cs, c % 2)
                bc_full = bqc[:, h:h + 1]
                rope_emit(qfm, bqc[0:64, h:h + 1], bqc[64:128, h:h + 1], cs, c % 2,
                          "qfm_free" if c == 0 else None)
            for c in range(4):
                cs = slice(c * 512, (c + 1) * 512)
                @blk.tensor
                def _(t, c=c, cs=cs):
                    proj_chunk_m(t, "v", cs, c % 2)
                @blk.scalar
                def _(sc, c=c, cs=cs, h=h):
                    ctx.pe.wait(sc)
                    if h > 0 and c == 0:
                        sc.wait_ge(ctx.pe.h, marks["vfm_free"])
                    mk = ctx.act.inc(sc.activation(vfm[:, cs], pk[c % 2][:, :],
                                                   AF.Identity, bias=bvc[:, h:h + 1]))
                    marks["pk_free"][c % 2] = ("act", mk)
                    marks["vfm_act"] = mk
            for bi in range(2):
                @blk.tensor
                def _(t, bi=bi, h=h):
                    t.wait_ge(ctx.act.h, marks["vfm_act"])
                    wait_mark(t, marks["ptr_free"][bi % 2])
                    if h > 0 and bi == 0:
                        t.wait_ge(ctx.pe.h, marks["pv_last"])
                    for j in range(8):
                        jj = bi * 8 + j
                        ins = t.transpose(ptr[bi % 2][:, j * 128:(j + 1) * 128],
                                          vfm[:, jj * 128:(jj + 1) * 128], ident[:, :])
                    mk = ctx.pe.inc(ins)
                    if bi == 1:
                        marks["vfm_free"] = mk
                @blk.vector
                def _(v, bi=bi):
                    ctx.pe.wait(v)
                    mk = ctx.dve.inc(v.tensor_copy(
                        vone_v[:, bi * 8:(bi + 1) * 8, 0:128],
                        ptr[bi % 2][:, :].rearrange("p (j c) -> p j c", c=128)))
                    marks["ptr_free"][bi % 2] = ("dve", mk)
                    marks["vone_dve"] = mk

            if h + 1 < H:
                @blk.gpsimd
                def _(g, h=h):
                    for m in range(3):
                        g.wait_ge(ctx.pe.h, marks["wb_free"][m])
                        ldw[m].inc(g.dma_start(
                            wbf3[m][:, :], io[wnames[m]][(h + 1) * 128:(h + 2) * 128, :]), 16)

            for qc in range(2):
                qs = slice(qc * 512, (qc + 1) * 512)
                s_marks, e_marks, pv_marks = {}, {}, {}

                def emit_pv(t, tk):
                    pbuf = ptil[tk % 2]
                    if tk == 0:
                        for sub in range(4):
                            wait_mark(t, marks["ctx_free"][sub])
                    for sub in range(4):
                        ins = t.matmul(pctx[sub][:, 0:129], pbuf[:, sub * 128:(sub + 1) * 128],
                                       vone_v[:, tk, :],
                                       start=(tk == 0), stop=(tk == NTS - 1))
                    pv_marks[tk] = ctx.pe.inc(ins)

                for tk in range(NTS):
                    @blk.tensor
                    def _(t, tk=tk, qs=qs, qc=qc):
                        if tk >= 2:
                            t.wait_ge(ctx.act.h, e_marks[tk - 2])
                        else:
                            wait_mark(t, marks["pk_free"][tk % 2])
                        if tk == 0 and qc == 0:
                            t.wait_ge(ctx.dve.h, marks["vone_dve"])
                        ins = t.matmul(pk[tk % 2][:, :], kfm[:, tk * 128:(tk + 1) * 128],
                                       qfm[:, qs], start=True, stop=True)
                        s_marks[tk] = ctx.pe.inc(ins)
                        if tk >= 1:
                            t.wait_ge(ctx.act.h, e_marks[tk - 1])
                            emit_pv(t, tk - 1)
                    @blk.scalar
                    def _(sc, tk=tk, qc=qc):
                        sc.wait_ge(ctx.pe.h, s_marks[tk])
                        if tk >= 2:
                            sc.wait_ge(ctx.pe.h, pv_marks[tk - 2])
                        elif qc == 0:
                            sc.wait_ge(ctx.pe.h, marks["pv_last"])
                        ins = sc.activation(ptil[tk % 2][:, :], pk[tk % 2][:, :], AF.Exp,
                                            scale=ATTN_SCALE)
                        e_marks[tk] = ctx.act.inc(ins)
                        if tk == NTS - 1:
                            marks["pk_free"][0] = ("act", e_marks[NTS - 2])
                            marks["pk_free"][1] = ("act", e_marks[NTS - 1])
                @blk.tensor
                def _(t, qc=qc):
                    t.wait_ge(ctx.act.h, e_marks[NTS - 1])
                    emit_pv(t, NTS - 1)
                marks["pv_last"] = ctx.pe.n
                @blk.vector
                def _(v, qc=qc):
                    v.wait_ge(ctx.pe.h, pv_marks[NTS - 1])
                    v.wait_ge(ctx.pe.h, marks["ctxtm_free"])
                    ctx.dve.wait(v)
                    for sub in range(4):
                        ctx.dve.inc(v.reciprocal(recips[:, sub:sub + 1],
                                                 pctx[sub][:, 128:129]))
                    ctx.dve.wait(v)
                    for sub in range(4):
                        mk = ctx.dve.inc(v.tensor_scalar(ctx_tm[:, sub * 128:(sub + 1) * 128],
                                                         pctx[sub][:, 0:128],
                                                         recips[:, sub:sub + 1], None, ALU.mult))
                        marks["ctx_free"][sub] = ("dve", mk)
                @blk.tensor
                def _(t, qc=qc):
                    ctx.dve.wait(t)
                    wait_mark(t, marks["ptr_free"][0])
                    for sub in range(4):
                        ins = t.transpose(ptr[0][:, sub * 128:(sub + 1) * 128],
                                          ctx_tm[:, sub * 128:(sub + 1) * 128], ident[:, :])
                    marks["ctxtm_free"] = ctx.pe.inc(ins)
                @blk.vector
                def _(v, qc=qc, h=h):
                    ctx.pe.wait(v)
                    mk = ctx.dve.inc(v.tensor_copy(
                        ctx_fm[:, h * TOWN + qc * 512:h * TOWN + (qc + 1) * 512],
                        ptr[0][:, 0:512]))
                    marks["ptr_free"][0] = ("dve", mk)
            marks["kfm_free"] = marks["pv_last"]
            marks["qfm_free"] = marks["pv_last"]

    ctx_v = ctx_fm[:, :].rearrange("p (i c) -> p i c", c=TOWN)
    for oc in range(4):
        with blk_new() as blk:
            for sub in range(4):
                @blk.sync
                def _(s, sub=sub, oc=oc):
                    ctx.pool.wait(s)
                    s.wait_ge(ctx.dve.h, ctx.dve.n)
                    ld_wo.inc(s.dma_start(wraw[sub % 2][:, :],
                                          io["wo"][(oc * 4 + sub) * 128:(oc * 4 + sub + 1) * 128, :]), 16)
                @blk.gpsimd
                def _(g, sub=sub):
                    ld_wo.wait(g)
                    ctx.dve.wait(g)
                    ctx.pool.inc(g.tensor_copy(wbf[:, :], wraw[sub % 2][:, :]))
                transpose_w(blk, wbf, woT_v, NI, col_off=sub * 128)
        with blk_new() as blk:
            t_marks = {}
            for tt in range(NT):
                bank = tt % 4
                @blk.sync
                def _(s, tt=tt, oc=oc):
                    if tt >= 2:
                        s.wait_ge(ctx.dve.h, t_marks[tt - 2])
                    ld_xr[tt % 2].inc(s.dma_start(xres[tt % 2][:, :],
                                                  io["x_own"][tt * 128:(tt + 1) * 128,
                                                              oc * 512:(oc + 1) * 512]), 16)
                @blk.tensor
                def _(t, tt=tt, bank=bank, oc=oc):
                    if tt >= 4:
                        t.wait_ge(ctx.dve.h, t_marks[tt - 4])
                    t.matmul(pctx[bank][:, :], ones_row[0:1, 0:128],
                             bo_row[0:1, oc * 512:(oc + 1) * 512], start=True, stop=False)
                    for j in range(NI):
                        ins = t.matmul(pctx[bank][:, :], ctx_v[:, j, tt * 128:(tt + 1) * 128],
                                       woT_v[:, j, :], start=False, stop=(j == NI - 1))
                    ctx.pe.inc(ins)
                @blk.vector
                def _(v, tt=tt, bank=bank):
                    ctx.pe.wait(v)
                    ld_xr[tt % 2].wait(v)
                    v.wait_ge(ctx.ld2.h, ctx.ld2.n)
                    t_marks[tt] = ctx.dve.inc(v.tensor_add(ysb[tt % 2][:, :],
                                                           xres[tt % 2][:, :],
                                                           pctx[bank][:, :]))
                @blk.scalar
                def _(sc, tt=tt, oc=oc):
                    sc.wait_ge(ctx.dve.h, t_marks[tt])
                    ctx.ld2.inc(sc.dma_start(io["y"][tt * 128:(tt + 1) * 128,
                                                     oc * 512:(oc + 1) * 512],
                                             ysb[tt % 2][:, :]), 16)
    with blk_new() as blk:
        @blk.scalar
        def _(sc):
            ctx.ld2.wait(sc)


def emit_l1_attn(ctx, blk_new, io):
    nc = ctx.nc
    TK = 1280
    NTK = TK // 128
    W = 256
    pk = [ctx.ps("c_pk%d" % i, [128, 512], F32) for i in range(2)]
    pctx = [ctx.ps("c_pctx%d" % i, [128, 512], F32) for i in range(4)]
    ptr = [ctx.ps("c_ptr%d" % i, [128, 1024], BF16) for i in range(2)]

    ident, ones_row = emit_common_consts(ctx, blk_new)
    ld_x = Trk(nc, "c_ld_x")
    ld_w = Trk(nc, "c_ld_w")
    ld_wo = Trk(nc, "c_ld_wo")
    ld_b = Trk(nc, "c_ld_b")
    ld_xr = [Trk(nc, "c_ld_xr%d" % i) for i in range(2)]

    xln = ctx.sb("c_xln", [128, NI * TK], BF16)
    wraw = [ctx.sb("c_wraw%d" % i, [128, D], F32) for i in range(2)]
    scr_f32 = wraw[1]
    xstage = wraw[0]
    xln_tm = ctx.sb("c_xlntm", [128, D], BF16)
    wbf = ctx.sb("c_wbf", [128, D], BF16)
    wpool = ctx.sb("c_wpool", [128, 4 * D], BF16)
    kfm = ctx.sb("c_kfm", [128, TK], BF16)
    qfm = ctx.sb("c_qfm", [128, TOWN], BF16)
    vfm = ctx.sb("c_vfm", [128, TK], BF16)
    vone = ctx.sb("c_vone", [128, NTK * 129], BF16)
    ptil = ctx.sb("c_ptil", [128, 4 * 256], BF16)
    ropeT = ctx.sb("c_ropeT", [128, 1024], BF16)
    csub = ctx.sb("c_csub", [128, 64], BF16)
    cadd = ctx.sb("c_cadd", [128, 64], BF16)
    maskA = ctx.sb("c_maskA", [128, 256], BF16)
    maskB = ctx.sb("c_maskB", [128, 256], BF16)
    prevflag = ctx.sb("c_prevflag", [128, 1], F32)
    ctx_tm = ctx.sb("c_ctxtm", [128, 512], BF16)
    ctx_fm = ctx.sb("c_ctxfm", [128, H * TOWN], BF16)
    recips = ctx.sb("c_recips", [128, 4], F32)
    xres = [ctx.sb("c_xres%d" % i, [128, 512], F32) for i in range(2)]
    ysb = [ctx.sb("c_ysb%d" % i, [128, 512], F32) for i in range(2)]

    cs_t = emit_rope_tables(ctx, blk_new, pctx[0], wraw[0], wraw[1],
                            io["pos_all"], io["th_hi"], io["th_lo"], 1536, "rt1")
    sc_t = emit_rope_tables(ctx, blk_new, pctx[0], wraw[0], wraw[1],
                            io["pos_all"], io["th_hi"], io["th_lo"], 1536, "rt1s",
                            swap=True)

    xln_v = xln[:, :].rearrange("p (i c) -> p i c", c=TK)
    vone_v = vone[:, :].rearrange("p (t c) -> p t c", c=129)
    wqT_v = wpool[:, 0 * D:1 * D].rearrange("p (j c) -> p j c", c=128)
    wkT_v = wpool[:, 1 * D:2 * D].rearrange("p (j c) -> p j c", c=128)
    wvT_v = wpool[:, 2 * D:3 * D].rearrange("p (j c) -> p j c", c=128)
    woT_v = wpool[:, 0:4 * D].rearrange("p (j c) -> p j c", c=512)
    wT_v = {"k": wkT_v, "q": wqT_v, "v": wvT_v}

    with blk_new() as blk:
        @blk.sync
        def _(s):
            ld_b.inc(s.dma_start(csub[:, :], io["csub"]), 16)
            ld_b.inc(s.dma_start(cadd[:, :], io["cadd"]), 16)
            ld_b.inc(s.dma_start(prevflag[:, :], io["prevflag"]), 16)
            ld_b.wait(s)
        @blk.vector
        def _(v):
            ctx.dve.inc(v.memset(vone_v[:, :, 128:129], 1.0))
        @blk.gpsimd
        def _(g):
            ctx.pool.inc(g.memset(maskA[:, :], 1.0))
            ctx.pool.inc(g.memset(maskB[:, :], 1.0))
            ctx.pool.wait(g)
            g.affine_select(maskA[:, :], maskA[:, :], [[1, 256]], ALU.is_ge, 0.0,
                            base=0, channel_multiplier=-1)
            ctx.pool.inc(g.affine_select(maskB[:, :], maskB[:, :], [[1, 256]],
                                         ALU.is_ge, 0.0, base=-128,
                                         channel_multiplier=-1))

    for part, (src_nm, ntile) in enumerate([("x_prev", 2), ("x_own", NT)]):
        for t in range(ntile):
            with blk_new() as blk:
                @blk.sync
                def _(s, t=t, src_nm=src_nm):
                    s.wait_ge(ctx.act.h, ctx.act.n)
                    s.wait_ge(ctx.dve.h, ctx.dve.n)
                    s.wait_ge(ctx.pe.h, ctx.pe.n)
                    ld_x.inc(s.dma_start(xstage[:, :],
                                         io[src_nm][t * 128:(t + 1) * 128, :]), 16)
                    ld_x.wait(s)
                emit_ln_fm(ctx, blk, xstage, xln, part * 256 + t * 128, ptr[0],
                           scr_f32, xln_tm, ident, TK, ld_x)

    def transpose_w(blk, src_bf, dstT_v, nj, col_off=0):
        for bi, j0 in enumerate(range(0, nj, 8)):
            j1 = min(nj, j0 + 8)
            @blk.tensor
            def _(t, j0=j0, j1=j1, bi=bi):
                ctx.pool.wait(t)
                ctx.dve.wait(t)
                for j in range(j0, j1):
                    ins = t.transpose(ptr[bi % 2][:, (j - j0) * 128:(j - j0 + 1) * 128],
                                      src_bf[:, j * 128:(j + 1) * 128], ident[:, :])
                ctx.pe.inc(ins)
            @blk.vector
            def _(v, j0=j0, j1=j1, bi=bi):
                ctx.pe.wait(v)
                ctx.dve.inc(v.tensor_copy(
                    dstT_v[:, j0:j1, col_off:col_off + 128],
                    ptr[bi % 2][:, 0:(j1 - j0) * 128].rearrange("p (j c) -> p j c", c=128)))

    def proj_chunk(t, which, cs, bank, w):
        for j in range(NI):
            ins = t.matmul(pk[bank][:, 0:w], wT_v[which][:, j, :], xln_v[:, j, cs],
                           start=(j == 0), stop=(j == NI - 1))
        return ctx.pe.inc(ins)

    kq_chunks = [(0, 512), (512, 512), (1024, 256)]
    q_chunks = [(0, 512), (512, 512)]

    wnamesC = ["Wq_p", "Wk_p", "Wv"]
    ldw = [Trk(nc, "c_ldw%d" % m) for m in range(3)]
    wbf3 = [ctx.sb("c_wbf%d" % m, [128, D], BF16) for m in range(3)]
    with nc.named_scope("c_heads"), blk_new() as blk:
        marks = {
            "ptr_free": {0: ("dve", 0), 1: ("dve", 0)},
            "pk_free": {0: ("act", 0), 1: ("act", 0)},
            "rot_free": {0: ("act", 0), 1: ("act", 0)},
            "ctx_free": {i: ("dve", 0) for i in range(2)},
            "wb_free": {m: 0 for m in range(3)},
            "wT_ready": {m: 0 for m in range(3)},
            "kfm_free": 0, "qfm_free": 0, "pv_last": 0, "vfm_free": 0,
            "ctxtm_free": 0, "vone_dve": 0, "ropeT_dve": 0, "vfm_act": 0,
        }

        def wait_mark(eng, mk):
            nm, val = mk
            eng.wait_ge(getattr(ctx, nm).h, val)

        @blk.gpsimd
        def _(g):
            for m in range(3):
                ldw[m].inc(g.dma_start(wbf3[m][:, :], io[wnamesC[m]][0:128, :]), 16)

        for h in range(H):
            for m, dstv in enumerate([wqT_v, wkT_v, wvT_v]):
                for bi in range(2):
                    @blk.tensor
                    def _(t, m=m, bi=bi, h=h):
                        if bi == 0:
                            ldw[m].wait(t)
                        wait_mark(t, marks["ptr_free"][bi % 2])
                        for j in range(8):
                            jj = bi * 8 + j
                            ins = t.transpose(ptr[bi % 2][:, j * 128:(j + 1) * 128],
                                              wbf3[m][:, jj * 128:(jj + 1) * 128],
                                              ident[:, :])
                        mk = ctx.pe.inc(ins)
                        if bi == 1:
                            marks["wb_free"][m] = mk
                    @blk.vector
                    def _(v, m=m, bi=bi, dstv=dstv):
                        ctx.pe.wait(v)
                        mk = ctx.dve.inc(v.tensor_copy(
                            dstv[:, bi * 8:(bi + 1) * 8, 0:128],
                            ptr[bi % 2][:, :].rearrange("p (j c) -> p j c", c=128)))
                        marks["ptr_free"][bi % 2] = ("dve", mk)
                        if bi == 1:
                            marks["wT_ready"][m] = mk

            def proj_chunk_m(t, which, cs, bank, w):
                wait_mark(t, marks["pk_free"][bank])
                t.wait_ge(ctx.dve.h, marks["wT_ready"][{"q": 0, "k": 1, "v": 2}[which]])
                for j in range(NI):
                    ins = t.matmul(pk[bank][:, 0:w], wT_v[which][:, j, :], xln_v[:, j, cs],
                                   start=(j == 0), stop=(j == NI - 1))
                return ctx.pe.inc(ins)

            def rope_emit(dst_fm, dcol, tcol, w, bank, dst_gate, h=h):
                @blk.vector
                def _(v):
                    ctx.pe.wait(v)
                    ctx.dve.inc(v.tensor_mul(ropeT[:, 0:w], pk[bank][:, 0:w],
                                             cs_t[:, tcol:tcol + w]))
                    mk = ctx.dve.inc(v.tensor_mul(ropeT[:, 512:512 + w], pk[bank][:, 0:w],
                                                  sc_t[:, tcol:tcol + w]))
                    marks["ropeT_dve"] = mk
                @blk.tensor
                def _(t):
                    t.wait_ge(ctx.dve.h, marks["ropeT_dve"])
                    wait_mark(t, marks["rot_free"][bank])
                    t.matmul(pctx[bank][0:64, 0:w], csub[:, :], ropeT[:, 0:w],
                             start=True, stop=True)
                    ctx.pe.inc(t.matmul(pctx[bank][64:128, 0:w], cadd[:, :],
                                        ropeT[:, 512:512 + w],
                                        start=True, stop=True, tile_position=(0, 64)))
                @blk.scalar
                def _(sc):
                    ctx.pe.wait(sc)
                    if dst_gate is not None and h > 0:
                        sc.wait_ge(ctx.pe.h, marks[dst_gate])
                    mk = ctx.act.inc(sc.activation(dst_fm[:, dcol:dcol + w],
                                                   pctx[bank][:, 0:w], AF.Identity))
                    marks["pk_free"][bank] = ("act", mk)
                    marks["rot_free"][bank] = ("act", mk)

            for c, (col, w) in enumerate(kq_chunks):
                @blk.tensor
                def _(t, c=c, col=col, w=w):
                    proj_chunk_m(t, "k", slice(col, col + w), c % 2, w)
                rope_emit(kfm, col, col, w, c % 2, "kfm_free" if c == 0 else None)
            for c, (col, w) in enumerate(q_chunks):
                @blk.tensor
                def _(t, c=c, col=col, w=w):
                    proj_chunk_m(t, "q", slice(256 + col, 256 + col + w), c % 2, w)
                rope_emit(qfm, col, 256 + col, w, c % 2, "qfm_free" if c == 0 else None)
            for c, (col, w) in enumerate(kq_chunks):
                @blk.tensor
                def _(t, c=c, col=col, w=w):
                    proj_chunk_m(t, "v", slice(col, col + w), c % 2, w)
                @blk.scalar
                def _(sc, c=c, col=col, w=w, h=h):
                    ctx.pe.wait(sc)
                    if h > 0 and c == 0:
                        sc.wait_ge(ctx.pe.h, marks["vfm_free"])
                    mk = ctx.act.inc(sc.activation(vfm[:, col:col + w], pk[c % 2][:, 0:w],
                                                   AF.Identity))
                    marks["pk_free"][c % 2] = ("act", mk)
                    marks["vfm_act"] = mk
            for bi, j0 in enumerate(range(0, NTK, 8)):
                j1 = min(NTK, j0 + 8)
                @blk.tensor
                def _(t, j0=j0, j1=j1, bi=bi, h=h):
                    t.wait_ge(ctx.act.h, marks["vfm_act"])
                    wait_mark(t, marks["ptr_free"][bi % 2])
                    if h > 0 and bi == 0:
                        t.wait_ge(ctx.pe.h, marks["pv_last"])
                    for j in range(j0, j1):
                        ins = t.transpose(ptr[bi % 2][:, (j - j0) * 128:(j - j0 + 1) * 128],
                                          vfm[:, j * 128:(j + 1) * 128], ident[:, :])
                    mk = ctx.pe.inc(ins)
                    if bi == 1:
                        marks["vfm_free"] = mk
                @blk.vector
                def _(v, j0=j0, j1=j1, bi=bi):
                    ctx.pe.wait(v)
                    mk = ctx.dve.inc(v.tensor_copy(
                        vone_v[:, j0:j1, 0:128],
                        ptr[bi % 2][:, 0:(j1 - j0) * 128].rearrange("p (j c) -> p j c", c=128)))
                    marks["ptr_free"][bi % 2] = ("dve", mk)
                    marks["vone_dve"] = mk

            if h + 1 < H:
                @blk.gpsimd
                def _(g, h=h):
                    for m in range(3):
                        g.wait_ge(ctx.pe.h, marks["wb_free"][m])
                        ldw[m].inc(g.dma_start(
                            wbf3[m][:, :], io[wnamesC[m]][(h + 1) * 128:(h + 2) * 128, :]), 16)

            for lc in range(4):
                qs = slice(lc * 256, (lc + 1) * 256)
                s_marks, pv_marks = {}, {}
                p_ready = {}

                def emit_pv(t, j, lc):
                    if j == 0:
                        for sub in range(2):
                            wait_mark(t, marks["ctx_free"][sub])
                    for sub in range(2):
                        ins = t.matmul(pctx[sub][:, 0:129],
                                       ptil[:, j * 256 + sub * 128:j * 256 + (sub + 1) * 128],
                                       vone_v[:, 2 * lc + j, :],
                                       start=(j == 0), stop=(j == 3))
                    pv_marks[j] = ctx.pe.inc(ins)

                for j in range(4):
                    @blk.tensor
                    def _(t, j=j, lc=lc, qs=qs):
                        if j >= 2:
                            h2, n2 = p_ready[j - 2]
                            t.wait_ge(h2, n2)
                        else:
                            wait_mark(t, marks["pk_free"][j % 2])
                        if j == 0 and lc == 0:
                            t.wait_ge(ctx.dve.h, marks["vone_dve"])
                        ins = t.matmul(pk[j % 2][:, 0:256],
                                       kfm[:, (2 * lc + j) * 128:(2 * lc + j + 1) * 128],
                                       qfm[:, qs], start=True, stop=True)
                        s_marks[j] = ctx.pe.inc(ins)
                        if j >= 1:
                            h1, n1 = p_ready[j - 1]
                            t.wait_ge(h1, n1)
                            emit_pv(t, j - 1, lc)
                    @blk.scalar
                    def _(sc, j=j, lc=lc):
                        sc.wait_ge(ctx.pe.h, s_marks[j])
                        sc.wait_ge(ctx.pe.h, marks["pv_last"])
                        mk = ctx.act.inc(
                            sc.activation(ptil[:, j * 256:(j + 1) * 256],
                                          pk[j % 2][:, 0:256], AF.Exp, scale=ATTN_SCALE))
                        p_ready[j] = (ctx.act.h, mk)
                        if j >= 2:
                            marks["pk_free"][j % 2] = ("act", mk)
                    @blk.vector
                    def _(v, j=j, lc=lc):
                        h0, n0 = p_ready[j]
                        v.wait_ge(h0, n0)
                        pt = ptil[:, j * 256:(j + 1) * 256]
                        if j >= 2:
                            m = maskA if j == 2 else maskB
                            mk = ctx.dve.inc(v.tensor_mul(pt, pt, m[:, :]))
                            p_ready[j] = (ctx.dve.h, mk)
                        elif lc == 0:
                            mk = ctx.dve.inc(
                                v.tensor_scalar(pt, pt, prevflag[:, 0:1], None, ALU.mult))
                            p_ready[j] = (ctx.dve.h, mk)
                @blk.tensor
                def _(t, lc=lc):
                    h3, n3 = p_ready[3]
                    t.wait_ge(h3, n3)
                    emit_pv(t, 3, lc)
                marks["pv_last"] = ctx.pe.n
                @blk.vector
                def _(v, lc=lc):
                    v.wait_ge(ctx.pe.h, pv_marks[3])
                    v.wait_ge(ctx.pe.h, marks["ctxtm_free"])
                    ctx.dve.wait(v)
                    for sub in range(2):
                        ctx.dve.inc(v.reciprocal(recips[:, sub:sub + 1],
                                                 pctx[sub][:, 128:129]))
                    ctx.dve.wait(v)
                    for sub in range(2):
                        mk = ctx.dve.inc(v.tensor_scalar(ctx_tm[:, sub * 128:(sub + 1) * 128],
                                                         pctx[sub][:, 0:128],
                                                         recips[:, sub:sub + 1], None, ALU.mult))
                        marks["ctx_free"][sub] = ("dve", mk)
                @blk.tensor
                def _(t, lc=lc):
                    ctx.dve.wait(t)
                    wait_mark(t, marks["ptr_free"][0])
                    for sub in range(2):
                        ins = t.transpose(ptr[0][:, sub * 128:(sub + 1) * 128],
                                          ctx_tm[:, sub * 128:(sub + 1) * 128], ident[:, :])
                    marks["ctxtm_free"] = ctx.pe.inc(ins)
                @blk.vector
                def _(v, lc=lc, h=h):
                    ctx.pe.wait(v)
                    mk = ctx.dve.inc(v.tensor_copy(
                        ctx_fm[:, h * TOWN + lc * 256:h * TOWN + (lc + 1) * 256],
                        ptr[0][:, 0:256]))
                    marks["ptr_free"][0] = ("dve", mk)
            marks["kfm_free"] = marks["pv_last"]
            marks["qfm_free"] = marks["pv_last"]

    ctx_v = ctx_fm[:, :].rearrange("p (i c) -> p i c", c=TOWN)
    for oc in range(4):
        with blk_new() as blk:
            for sub in range(4):
                @blk.sync
                def _(s, sub=sub, oc=oc):
                    ctx.pool.wait(s)
                    s.wait_ge(ctx.dve.h, ctx.dve.n)
                    ld_wo.inc(s.dma_start(wraw[sub % 2][:, :],
                                          io["Wo"][(oc * 4 + sub) * 128:(oc * 4 + sub + 1) * 128, :]), 16)
                @blk.gpsimd
                def _(g, sub=sub):
                    ld_wo.wait(g)
                    ctx.dve.wait(g)
                    ctx.pool.inc(g.tensor_copy(wbf[:, :], wraw[sub % 2][:, :]))
                transpose_w(blk, wbf, woT_v, NI, col_off=sub * 128)
        with blk_new() as blk:
            t_marks = {}
            for tt in range(NT):
                bank = tt % 4
                @blk.sync
                def _(s, tt=tt, oc=oc):
                    if tt >= 2:
                        s.wait_ge(ctx.dve.h, t_marks[tt - 2])
                    ld_xr[tt % 2].inc(s.dma_start(xres[tt % 2][:, :],
                                                  io["x_own"][tt * 128:(tt + 1) * 128,
                                                              oc * 512:(oc + 1) * 512]), 16)
                @blk.tensor
                def _(t, tt=tt, bank=bank):
                    if tt >= 4:
                        t.wait_ge(ctx.dve.h, t_marks[tt - 4])
                    for j in range(NI):
                        ins = t.matmul(pctx[bank][:, :], ctx_v[:, j, tt * 128:(tt + 1) * 128],
                                       woT_v[:, j, :], start=(j == 0), stop=(j == NI - 1))
                    ctx.pe.inc(ins)
                @blk.vector
                def _(v, tt=tt, bank=bank):
                    ctx.pe.wait(v)
                    ld_xr[tt % 2].wait(v)
                    v.wait_ge(ctx.ld2.h, ctx.ld2.n)
                    t_marks[tt] = ctx.dve.inc(v.tensor_add(ysb[tt % 2][:, :],
                                                           xres[tt % 2][:, :],
                                                           pctx[bank][:, :]))
                @blk.scalar
                def _(sc, tt=tt, oc=oc):
                    sc.wait_ge(ctx.dve.h, t_marks[tt])
                    ctx.ld2.inc(sc.dma_start(io["y"][tt * 128:(tt + 1) * 128,
                                                     oc * 512:(oc + 1) * 512],
                                             ysb[tt % 2][:, :]), 16)
    with blk_new() as blk:
        @blk.scalar
        def _(sc):
            ctx.ld2.wait(sc)


def emit_l0_ffn(ctx, blk_new, io, ncores=8):
    nc = ctx.nc
    DK = float(2 ** 20 + 1)
    pb = [ctx.ps("b_p%d" % i, [128, 512], F32) for i in range(8)]
    ld_x = Trk(nc, "b_ld_x")
    ld_w = Trk(nc, "b_ld_w")
    ld_wq = Trk(nc, "b_ld_wq")
    ld_xr = [Trk(nc, "b_ld_xr%d" % i) for i in range(2)]
    ld_cc = Trk(nc, "b_ld_cc")
    cc_sem = Trk(nc, "b_cc")

    identf = ctx.sb("b_identf", [128, 128], F32)
    ctx.eps_col = ctx.sb("b_eps", [128, 1], F32)
    ones_f = ctx.sb("b_onesf", [1, 128], F32)
    xln = ctx.sb("b_xln", [128, NI * TOWN], F32)
    xq = ctx.sb("b_xq", [128, NI * TOWN], BF16)
    xstage = ctx.sb("b_xstage", [128, D], F32)
    scr = ctx.sb("b_scr", [128, D], F32)
    xln_tm = ctx.sb("b_xlntm", [128, D], F32)
    qs1 = ctx.sb("b_qs1", [128, D], F32)
    qs2 = ctx.sb("b_qs2", [128, D], F32)
    wstage = [ctx.sb("b_wst%d" % i, [128, D], BF16) for i in range(2)]
    wqf = ctx.sb("b_wqf", [128, D], F32)
    wqT = ctx.sb("b_wqT", [128, NI * 512], BF16)
    stats = ctx.sb("b_stats", [128, 8], F32)
    amx = ctx.sb("b_amx", [128, 2], F32)
    sca = ctx.sb("b_sca", [1, 8], F32)
    scol = ctx.sb("b_scol", [128, 4], F32)
    otmp = ctx.sb("b_otmp", [128, 4 * 512], BF16)
    xres = [ctx.sb("b_xres%d" % i, [128, 512], F32) for i in range(2)]
    ysb = [ctx.sb("b_ysb%d" % i, [128, 512], F32) for i in range(2)]

    xln_v = xln[:, :].rearrange("p (i c) -> p i c", c=TOWN)
    xq_v = xq[:, :].rearrange("p (i c) -> p i c", c=TOWN)
    wqT_v = wqT[:, :].rearrange("p (j c) -> p j c", c=512)

    def tr_f32(blk, src_f32_t, dst_v, j0_dst, col_off, ncol):
        for b4 in range(4):
            @blk.tensor
            def _(t, b4=b4):
                ctx.act.wait(t)
                ctx.dve.wait(t)
                for j in range(4):
                    jj = b4 * 4 + j
                    ins = t.transpose(pb[6 + b4 % 2][:, j * 128:(j + 1) * 128],
                                      src_f32_t[:, jj * 128:(jj + 1) * 128], identf[:, :])
                ctx.pe.inc(ins)
            @blk.vector
            def _(v, b4=b4):
                ctx.pe.wait(v)
                ctx.dve.inc(v.tensor_copy(
                    dst_v[:, j0_dst + b4 * 4:j0_dst + b4 * 4 + 4, col_off:col_off + 128],
                    pb[6 + b4 % 2][:, 0:512].rearrange("p (j c) -> p j c", c=128)))

    with blk_new() as blk:
        @blk.gpsimd
        def _(g):
            ctx.pool.inc(g.memset(identf[:, :], 1.0))
            g.memset(ctx.eps_col[:, :], 1e-5)
            g.memset(ones_f[:, :], 1.0)
            ctx.pool.wait(g)
            g.affine_select(identf[:, :], identf[:, :], [[1, 128]], ALU.is_equal, 0.0,
                            base=0, channel_multiplier=-1)
        @blk.vector
        def _(v):
            ctx.dve.inc(v.memset(amx[:, :], 0.0))

    for t in range(NT):
        with blk_new() as blk:
            @blk.sync
            def _(s, t=t):
                s.wait_ge(ctx.act.h, ctx.act.n)
                s.wait_ge(ctx.dve.h, ctx.dve.n)
                ld_x.inc(s.dma_start(xstage[:, :], io["x_own"][t * 128:(t + 1) * 128, :]), 16)
                ld_x.wait(s)
            emit_ln_core(ctx, blk, xstage, scr, xln_tm, ld_x, F32)
            @blk.vector
            def _(v, t=t):
                ctx.act.wait(v)
                ctx.dve.inc(v.tensor_reduce(stats[:, 0:1], xln_tm[:, :], AX.X, ALU.max,
                                            apply_absolute_value=True))
                ctx.dve.wait(v)
                ctx.dve.inc(v.tensor_max(amx[:, 0:1], amx[:, 0:1], stats[:, 0:1]))
            tr_f32(blk, xln_tm, xln_v, 0, t * 128, TOWN)
    for t in range(NI):
        with blk_new() as blk:
            @blk.sync
            def _(s, t=t):
                s.wait_ge(ctx.dve.h, ctx.dve.n)
                ld_w.inc(s.dma_start(wstage[t % 2][:, :], io["w"][t * 128:(t + 1) * 128, :]), 16)
                ld_w.wait(s)
            @blk.vector
            def _(v, t=t):
                ld_w.wait(v)
                ctx.dve.inc(v.tensor_reduce(stats[:, 0:1], wstage[t % 2][:, :], AX.X,
                                            ALU.max, apply_absolute_value=True))
                ctx.dve.wait(v)
                ctx.dve.inc(v.tensor_max(amx[:, 1:2], amx[:, 1:2], stats[:, 0:1]))

    with blk_new() as blk:
        @blk.tensor
        def _(t):
            ctx.dve.wait(t)
            t.transpose(pb[6][0:1, 0:128], amx[:, 0:1], identf[:, :])
            ctx.pe.inc(t.transpose(pb[6][0:1, 128:256], amx[:, 1:2], identf[:, :]))
        @blk.vector
        def _(v):
            ctx.pe.wait(v)
            ctx.dve.inc(v.reduce_max(sca[0:1, 0:1], pb[6][0:1, 0:128], axis=AX.X))
            ctx.dve.inc(v.reduce_max(sca[0:1, 3:4], pb[6][0:1, 128:256], axis=AX.X))
    with blk_new() as blk:
        @blk.gpsimd
        def _(g):
            ctx.dve.wait(g)
            ld_cc.inc(g.dma_start(io["cc_in"], sca[0:1, 0:1]), 16)
            ld_cc.wait(g)
            cc_sem.inc(g.collective_compute("AllReduce", ALU.max,
                                            [list(range(ncores))],
                                            [io["cc_in"].opt()], [io["cc_out"].opt()]))
            cc_sem.wait(g)
            ld_cc.inc(g.dma_start(sca[0:1, 0:1], io["cc_out"]), 16)
            ld_cc.wait(g)
    with blk_new() as blk:
        @blk.vector
        def _(v):
            ld_cc.wait(v)
            def step(ins):
                ctx.dve.inc(ins)
                ctx.dve.wait(v)
            step(v.tensor_scalar(sca[0:1, 1:2], sca[0:1, 0:1], 1.0 / 448.0, 1e-12,
                                 ALU.mult, ALU.max))
            step(v.reciprocal(sca[0:1, 2:3], sca[0:1, 1:2]))
            step(v.tensor_scalar(sca[0:1, 4:5], sca[0:1, 3:4], 1.0 / 448.0, 1e-12,
                                 ALU.mult, ALU.max))
            step(v.reciprocal(sca[0:1, 5:6], sca[0:1, 4:5]))
            step(v.tensor_tensor(sca[0:1, 6:7], sca[0:1, 1:2], sca[0:1, 4:5], ALU.mult))
        @blk.tensor
        def _(t):
            ctx.dve.wait(t)
            t.matmul(pb[5][:, 0:1], ones_f[:, :], sca[0:1, 2:3], start=True, stop=True)
            t.matmul(pb[5][:, 1:2], ones_f[:, :], sca[0:1, 5:6], start=True, stop=True)
            ctx.pe.inc(t.matmul(pb[5][:, 2:3], ones_f[:, :], sca[0:1, 6:7], start=True, stop=True))
        @blk.vector
        def _(v):
            ctx.pe.wait(v)
            ctx.dve.inc(v.tensor_copy(scol[:, 0:3], pb[5][:, 0:3]))

    for i in range(NI):
        with blk_new() as blk:
            @blk.vector
            def _(v, i=i):
                ctx.act.wait(v)
                ctx.pool.wait(v)
                ctx.dve.inc(v.tensor_scalar(qs1[:, 0:TOWN], xln_v[:, i, :],
                                            scol[:, 0:1], None, ALU.mult))
            @blk.scalar
            def _(sc, i=i):
                ctx.dve.wait(sc)
                ctx.act.inc(sc.activation(qs2[:, 0:TOWN], qs1[:, 0:TOWN], AF.Identity,
                                          scale=DK))
            @blk.gpsimd
            def _(g, i=i):
                ctx.act.wait(g)
                ctx.pool.inc(g.tensor_sub(qs2[:, TOWN:2 * TOWN], qs2[:, 0:TOWN],
                                          qs1[:, 0:TOWN]))
            @blk.vector
            def _(v, i=i):
                ctx.pool.wait(v)
                ctx.dve.inc(v.tensor_sub(xq_v[:, i, :], qs2[:, 0:TOWN],
                                         qs2[:, TOWN:2 * TOWN]))

    for oc in range(4):
        with blk_new() as blk:
            for sub in range(4):
                @blk.sync
                def _(s, sub=sub, oc=oc):
                    s.wait_ge(ctx.dve.h, ctx.dve.n)
                    ld_wq.inc(s.dma_start(wstage[sub % 2][:, :],
                                          io["w"][(oc * 4 + sub) * 128:(oc * 4 + sub + 1) * 128, :]), 16)
                @blk.vector
                def _(v, sub=sub):
                    ld_wq.wait(v)
                    ctx.act.wait(v)
                    ctx.pool.wait(v)
                    ctx.dve.inc(v.tensor_scalar(qs1[:, :], wstage[sub % 2][:, :],
                                                scol[:, 1:2], None, ALU.mult))
                @blk.scalar
                def _(sc, sub=sub):
                    ctx.dve.wait(sc)
                    ctx.act.inc(sc.activation(qs2[:, :], qs1[:, :], AF.Identity, scale=DK))
                @blk.gpsimd
                def _(g, sub=sub):
                    ctx.act.wait(g)
                    ctx.pool.inc(g.tensor_sub(scr[:, :], qs2[:, :], qs1[:, :]))
                @blk.vector
                def _(v, sub=sub):
                    ctx.pool.wait(v)
                    ctx.dve.inc(v.tensor_sub(wqf[:, :], qs2[:, :], scr[:, :]))
                tr_f32(blk, wqf, wqT_v, 0, sub * 128, 512)
        with blk_new() as blk:
            t_marks = {}
            a_marks = {}
            for tt in range(NT):
                bank = tt % 4
                @blk.sync
                def _(s, tt=tt, oc=oc):
                    if tt >= 2:
                        s.wait_ge(ctx.dve.h, t_marks[tt - 2])
                    ld_xr[tt % 2].inc(s.dma_start(xres[tt % 2][:, :],
                                                  io["x_own"][tt * 128:(tt + 1) * 128,
                                                              oc * 512:(oc + 1) * 512]), 16)
                @blk.tensor
                def _(t, tt=tt, bank=bank):
                    if tt >= 4:
                        t.wait_ge(ctx.act.h, a_marks[tt - 4])
                    for j in range(NI):
                        ins = t.matmul(pb[bank][:, :], xq_v[:, j, tt * 128:(tt + 1) * 128],
                                       wqT_v[:, j, :], start=(j == 0), stop=(j == NI - 1))
                    ctx.pe.inc(ins)
                @blk.scalar
                def _(sc, tt=tt, bank=bank):
                    ctx.pe.wait(sc)
                    if tt >= 4:
                        sc.wait_ge(ctx.dve.h, t_marks[tt - 4])
                    a_marks[tt] = ctx.act.inc(
                        sc.activation(otmp[:, (tt % 4) * 512:(tt % 4 + 1) * 512],
                                      pb[bank][:, :], AF.Identity, scale=scol[:, 2:3]))
                @blk.vector
                def _(v, tt=tt):
                    v.wait_ge(ctx.act.h, a_marks[tt])
                    ld_xr[tt % 2].wait(v)
                    v.wait_ge(ctx.ld2.h, ctx.ld2.n)
                    t_marks[tt] = ctx.dve.inc(
                        v.tensor_add(ysb[tt % 2][:, :], xres[tt % 2][:, :],
                                     otmp[:, (tt % 4) * 512:(tt % 4 + 1) * 512]))
                @blk.scalar
                def _(sc, tt=tt, oc=oc):
                    sc.wait_ge(ctx.dve.h, t_marks[tt])
                    ctx.ld2.inc(sc.dma_start(io["y"][tt * 128:(tt + 1) * 128,
                                                     oc * 512:(oc + 1) * 512],
                                             ysb[tt % 2][:, :]), 16)
    with blk_new() as blk:
        @blk.scalar
        def _(sc):
            ctx.ld2.wait(sc)


def emit_l1_ffn(ctx, blk_new, io):
    nc = ctx.nc
    FS, FE, E = 1536, 768, 8
    pb = [ctx.ps("d_p%d" % i, [128, 512], F32) for i in range(8)]
    ld_x = Trk(nc, "d_ld_x")
    ld_r = Trk(nc, "d_ld_r")
    ld_w = [Trk(nc, "d_ld_w%d" % i) for i in range(4)]
    ld_xr = [Trk(nc, "d_ld_xr%d" % i) for i in range(2)]

    identf = ctx.sb("d_identf", [128, 128], F32)
    ctx.eps_col = ctx.sb("d_eps", [128, 1], F32)
    xln = ctx.sb("d_xln", [128, NI * TOWN], BF16)
    rtr = ctx.sb("d_rtr", [128, NI * 8], BF16)
    lgt = ctx.sb("d_lgt", [128, 8], F32)
    pstat = ctx.sb("d_pstat", [128, 8], F32)
    probs = ctx.sb("d_probs", [128, 8], F32)
    keep = ctx.sb("d_keep", [128, 8], F32)
    cw = ctx.sb("d_cw", [128, NT * 8], F32)
    gbuf = ctx.sb("d_gbuf", [128, NT * 512], F32)
    ubuf = ctx.sb("d_ubuf", [128, NT * 512], F32)
    xstage = ubuf[:, 0:D]
    scr = gbuf[:, 0:D]
    hsh = ctx.sb("d_hsh", [128, (FS // 128) * TOWN], BF16)
    hex_ = [ctx.sb("d_hex%d" % e, [128, (FE // 128) * TOWN], BF16) for e in range(E)]
    wslc = [ctx.sb("d_wslc%d" % i, [128, 512], BF16) for i in range(4)]
    xres = [ctx.sb("d_xres%d" % i, [128, 512], F32) for i in range(2)]
    ysb = [ctx.sb("d_ysb%d" % i, [128, 512], F32) for i in range(2)]

    xln_v = xln[:, :].rearrange("p (i c) -> p i c", c=TOWN)
    hsh_v = hsh[:, :].rearrange("p (f c) -> p f c", c=TOWN)
    hex_v = [h[:, :].rearrange("p (f c) -> p f c", c=TOWN) for h in hex_]

    with blk_new() as blk:
        @blk.gpsimd
        def _(g):
            ctx.pool.inc(g.memset(identf[:, :], 1.0))
            g.memset(ctx.eps_col[:, :], 1e-5)
            ctx.pool.wait(g)
            g.affine_select(identf[:, :], identf[:, :], [[1, 128]], ALU.is_equal, 0.0,
                            base=0, channel_multiplier=-1)
        @blk.gpsimd
        def _(g):
            ld_r.inc(g.dma_start(rtr[:, :].rearrange("p (i c) -> p i c", c=8),
                                 io["router"].rearrange("(i p) c -> p i c", p=128)), 16)
            ld_r.wait(g)

    for t in range(NT):
        with blk_new() as blk:
            @blk.sync
            def _(s, t=t):
                s.wait_ge(ctx.act.h, ctx.act.n)
                s.wait_ge(ctx.dve.h, ctx.dve.n)
                s.wait_ge(ctx.pe.h, ctx.pe.n)
                ld_x.inc(s.dma_start(xstage[:, :], io["x_own"][t * 128:(t + 1) * 128, :]), 16)
                ld_x.wait(s)
            emit_ln_core(ctx, blk, xstage, scr, scr, ld_x, F32)
            for b4 in range(4):
                @blk.tensor
                def _(tt, b4=b4, t=t):
                    ctx.act.wait(tt)
                    ctx.dve.wait(tt)
                    for j in range(4):
                        jj = b4 * 4 + j
                        ins = tt.transpose(pb[6 + b4 % 2][:, j * 128:(j + 1) * 128],
                                           scr[:, jj * 128:(jj + 1) * 128], identf[:, :])
                    ctx.pe.inc(ins)
                @blk.vector
                def _(v, b4=b4, t=t):
                    ctx.pe.wait(v)
                    ctx.dve.inc(v.tensor_copy(
                        xln_v[:, b4 * 4:b4 * 4 + 4, t * 128:(t + 1) * 128],
                        pb[6 + b4 % 2][:, 0:512].rearrange("p (j c) -> p j c", c=128)))
        with blk_new() as blk:
            @blk.tensor
            def _(tt, t=t):
                ld_r.wait(tt)
                ctx.act.wait(tt)
                for j in range(NI):
                    ins = tt.matmul(pb[5][:, 0:8], xln_v[:, j, t * 128:(t + 1) * 128],
                                    rtr[:, j * 8:(j + 1) * 8],
                                    start=(j == 0), stop=(j == NI - 1))
                ctx.pe.inc(ins)
            @blk.vector
            def _(v, t=t):
                ctx.pe.wait(v)
                ctx.dve.wait(v)
                def step(ins):
                    ctx.dve.inc(ins)
                    ctx.dve.wait(v)
                step(v.tensor_copy(lgt[:, :], pb[5][:, 0:8]))
                step(v.reduce_max(pstat[:, 0:1], lgt[:, :], axis=AX.X))
                step(v.tensor_scalar(pstat[:, 1:2], pstat[:, 0:1], -1.0, None, ALU.mult))
            @blk.scalar
            def _(sc, t=t):
                ctx.dve.wait(sc)
                ctx.act.inc(sc.activation(probs[:, :], lgt[:, :], AF.Exp,
                                          bias=pstat[:, 1:2], accum_out=pstat[:, 2:3]))
            @blk.vector
            def _(v, t=t):
                ctx.act.wait(v)
                def step(ins):
                    ctx.dve.inc(ins)
                    ctx.dve.wait(v)
                step(v.reciprocal(pstat[:, 3:4], pstat[:, 2:3]))
                step(v.tensor_scalar(probs[:, :], probs[:, :], pstat[:, 3:4], None, ALU.mult))
                step(v.reduce_max(pstat[:, 4:5], probs[:, :], axis=AX.X))
                step(v.tensor_scalar(keep[:, :], probs[:, :], pstat[:, 4:5], None, ALU.is_ge))
                step(v.scalar_tensor_tensor(keep[:, :], keep[:, :], -1e30, probs[:, :],
                                            ALU.mult, ALU.add))
                step(v.reduce_max(pstat[:, 5:6], keep[:, :], axis=AX.X))
                step(v.tensor_scalar(keep[:, :], probs[:, :], pstat[:, 5:6], None, ALU.is_ge))
                step(v.tensor_mul(cw[:, t * 8:(t + 1) * 8], probs[:, :], keep[:, :]))

    def gated_block(w_gate_sl, w_up_sl, cw_col, hdst_v, f0_tiles, fcw, fc_off, tag):
        with nc.named_scope("d_gated"), blk_new() as blk:
            pem = {}
            for mat, (wsl, buf) in enumerate([(w_gate_sl, gbuf), (w_up_sl, ubuf)]):
                for d in range(NI):
                    si = mat * NI + d
                    @blk.gpsimd
                    def _(g, d=d, mat=mat, wsl=wsl, si=si):
                        par = si % 4
                        if si >= 4:
                            g.wait_ge(ctx.pe.h, pem[si - 4])
                        ld_w[par].inc(g.dma_start(wslc[par][:, 0:fcw], wsl(d)), 16)
                    @blk.tensor
                    def _(t, d=d, mat=mat, si=si):
                        par = si % 4
                        ld_w[par].wait(t)
                        if d == 0:
                            ctx.act.wait(t)
                        for tt in range(NT):
                            ins = t.matmul(pb[tt][:, 0:fcw], xln_v[:, d, tt * 128:(tt + 1) * 128],
                                           wslc[par][:, 0:fcw],
                                           start=(d == 0), stop=(d == NI - 1))
                        pem[si] = ctx.pe.inc(ins)
                @blk.scalar
                def _(sc, mat=mat, buf=buf):
                    ctx.pe.wait(sc)
                    ctx.dve.wait(sc)
                    for tt in range(NT):
                        ctx.act.inc(sc.activation(buf[:, tt * 512:tt * 512 + fcw],
                                                  pb[tt][:, 0:fcw],
                                                  AF.Silu if mat == 0 else AF.Identity))
            @blk.vector
            def _(v):
                ctx.act.wait(v)
                for tt in range(NT):
                    gsl = gbuf[:, tt * 512:tt * 512 + fcw]
                    usl = ubuf[:, tt * 512:tt * 512 + fcw]
                    if cw_col is None:
                        ctx.dve.inc(v.tensor_mul(gsl, gsl, usl))
                    else:
                        ctx.dve.inc(v.scalar_tensor_tensor(gsl, gsl, cw_col(tt), usl,
                                                           ALU.mult, ALU.mult))
            nfs = fcw // 128
            for tt in range(NT):
                @blk.tensor
                def _(t, tt=tt):
                    ctx.dve.wait(t)
                    if tt >= 2:
                        ctx.dve.wait(t)
                    for fsub in range(nfs):
                        ins = t.transpose(pb[6 + tt % 2][:, fsub * 128:(fsub + 1) * 128],
                                          gbuf[:, tt * 512 + fsub * 128:tt * 512 + (fsub + 1) * 128],
                                          identf[:, :])
                    ctx.pe.inc(ins)
                @blk.vector
                def _(v, tt=tt):
                    ctx.pe.wait(v)
                    ctx.dve.inc(v.tensor_copy(
                        hdst_v[:, f0_tiles:f0_tiles + nfs, tt * 128:(tt + 1) * 128],
                        pb[6 + tt % 2][:, 0:nfs * 128].rearrange("p (j c) -> p j c", c=128)))

    for fc in range(3):
        gated_block(lambda d, fc=fc: io["shared_gate"][d * 128:(d + 1) * 128,
                                                       fc * 512:(fc + 1) * 512],
                    lambda d, fc=fc: io["shared_up"][d * 128:(d + 1) * 128,
                                                     fc * 512:(fc + 1) * 512],
                    None, hsh_v, fc * 4, 512, fc, "sh%d" % fc)
    for e in range(E):
        for fc, fcw in [(0, 512), (1, 256)]:
            gated_block(lambda d, e=e, fc=fc, fcw=fcw: io["w_gate"][e][d * 128:(d + 1) * 128,
                                                                      fc * 512:fc * 512 + fcw],
                        lambda d, e=e, fc=fc, fcw=fcw: io["w_up"][e][d * 128:(d + 1) * 128,
                                                                    fc * 512:fc * 512 + fcw],
                        (lambda tt, e=e: cw[:, tt * 8 + e:tt * 8 + e + 1]),
                        hex_v[e], fc * 4, fcw, fc, "e%d_%d" % (e, fc))

    for oc in range(4):
        with nc.named_scope("d_down"), blk_new() as blk:
            slices = [("shared_down", None, f) for f in range(FS // 128)]
            for e in range(E):
                slices += [("w_down", e, f) for f in range(FE // 128)]
            pem = {}
            for si, (nm, e, f) in enumerate(slices):
                @blk.gpsimd
                def _(g, si=si, nm=nm, e=e, f=f, oc=oc):
                    par = si % 4
                    if si >= 4:
                        g.wait_ge(ctx.pe.h, pem[si - 4])
                    src = (io[nm][f * 128:(f + 1) * 128, oc * 512:(oc + 1) * 512]
                           if e is None else
                           io[nm][e][f * 128:(f + 1) * 128, oc * 512:(oc + 1) * 512])
                    ld_w[par].inc(g.dma_start(wslc[par][:, :], src), 16)
                @blk.tensor
                def _(t, si=si, e=e, f=f):
                    par = si % 4
                    ld_w[par].wait(t)
                    if si == 0:
                        ctx.dve.wait(t)
                    hsrc = hsh_v if e is None else hex_v[e]
                    for tt in range(NT):
                        ins = t.matmul(pb[tt][:, :], hsrc[:, f, tt * 128:(tt + 1) * 128],
                                       wslc[par][:, :],
                                       start=(si == 0), stop=(si == len(slices) - 1))
                    pem[si] = ctx.pe.inc(ins)
            pe_done = ctx.pe.n
            t_marks = {}
            for tt in range(NT):
                @blk.sync
                def _(s, tt=tt, oc=oc):
                    if tt >= 2:
                        s.wait_ge(ctx.dve.h, t_marks[tt - 2])
                    ld_xr[tt % 2].inc(s.dma_start(xres[tt % 2][:, :],
                                                  io["x_own"][tt * 128:(tt + 1) * 128,
                                                              oc * 512:(oc + 1) * 512]), 16)
                @blk.vector
                def _(v, tt=tt, oc=oc):
                    v.wait_ge(ctx.pe.h, pe_done)
                    ld_xr[tt % 2].wait(v)
                    v.wait_ge(ctx.ld2.h, ctx.ld2.n)
                    t_marks[tt] = ctx.dve.inc(v.tensor_add(ysb[tt % 2][:, :],
                                                           xres[tt % 2][:, :],
                                                           pb[tt][:, :]))
                @blk.scalar
                def _(sc, tt=tt, oc=oc):
                    sc.wait_ge(ctx.dve.h, t_marks[tt])
                    ctx.ld2.inc(sc.dma_start(io["y"][tt * 128:(tt + 1) * 128,
                                                     oc * 512:(oc + 1) * 512],
                                             ysb[tt % 2][:, :]), 16)
    with blk_new() as blk:
        @blk.scalar
        def _(sc):
            ctx.ld2.wait(sc)


def emit_ln_core(ctx, blk, src_f32, scr_f32, out_tm, ld, out_dt):
    if not hasattr(ctx, "_ln_st2"):
        ctx._ln_st2 = ctx.sb("ln_stats2", [128, 8], F32)
    st = ctx._ln_st2
    rs, mean, sumsq, var, rstd, nb = (st[:, 0:1], st[:, 1:2], st[:, 2:3], st[:, 3:4],
                                      st[:, 4:5], st[:, 5:6])

    @blk.scalar
    def _(sc):
        ld.wait(sc)
        ctx.dve.wait(sc)
        ctx.act.wait(sc)
        ctx.act.inc(sc.activation(scr_f32[:, :], src_f32[:, :], AF.Square,
                                  accum_out=sumsq))

    @blk.vector
    def _(v):
        ld.wait(v)
        ctx.dve.inc(v.reduce_sum(rs, src_f32[:, :], axis=AX.X))
        ctx.dve.wait(v)
        ctx.dve.inc(v.tensor_scalar(mean[:, :], rs, 1.0 / D, None, ALU.mult))
        ctx.act.wait(v)
        ctx.dve.wait(v)
        ctx.dve.inc(v.tensor_tensor(var[:, :], mean, mean, ALU.mult))
        ctx.dve.wait(v)
        ctx.dve.inc(v.scalar_tensor_tensor(var[:, :], sumsq, 1.0 / D, var,
                                           ALU.mult, ALU.subtract))

    @blk.scalar
    def _(sc):
        ctx.dve.wait(sc)
        ctx.act.inc(sc.activation(var[:, :], var[:, :], AF.Sqrt, bias=ctx.eps_col[:, :]))

    @blk.vector
    def _(v):
        ctx.act.wait(v)
        ctx.dve.inc(v.reciprocal(rstd[:, :], var[:, :]))
        ctx.dve.wait(v)
        ctx.dve.inc(v.scalar_tensor_tensor(nb[:, :], mean, -1.0, rstd, ALU.mult, ALU.mult))

    @blk.scalar
    def _(sc):
        ctx.dve.wait(sc)
        ctx.act.inc(sc.activation(out_tm[:, :], src_f32[:, :], AF.Identity,
                                  bias=nb, scale=rstd))



def _theta_split():
    th64 = 10000.0 ** (-np.arange(0, 128, 2, dtype=np.float64) / 128.0)
    th64 = np.concatenate([th64, th64])
    hi = th64.astype(np.float32)
    lo = (th64 - hi.astype(np.float64)).astype(np.float32)
    return hi.reshape(1, 128), lo.reshape(1, 128)


def _csub():
    c = np.zeros((128, 64), dtype=ml_dtypes.bfloat16)
    for j in range(64):
        c[j, j] = 1.0
        c[64 + j, j] = -1.0
    return c


def _bias_cols(b):
    return np.ascontiguousarray(np.asarray(b, dtype=np.float32).reshape(16, 128).T)


def _perm_rows(w):
    w = np.asarray(w)
    v = w.reshape(16, 64, 2, *w.shape[1:])
    return np.ascontiguousarray(v.transpose(0, 2, 1, *range(3, v.ndim)).reshape(w.shape))


def build_l0_attn():
    nc = bass.Bass("TRN2", target_bir_lowering=False)
    io = {}
    io["x_own"] = nc.dram_tensor("x_own", [TOWN, D], F32, kind="ExternalInput").ap()
    io["x_oth"] = nc.dram_tensor("x_oth", [TOWN, D], F32, kind="ExternalInput").ap()
    for nm in ["wq_p", "wk_p", "wv", "wo"]:
        io[nm] = nc.dram_tensor(nm, [D, D], F32, kind="ExternalInput").ap()
    for nm in ["bq_c", "bk_c", "bv_c"]:
        io[nm] = nc.dram_tensor(nm, [128, H], F32, kind="ExternalInput").ap()
    io["bo"] = nc.dram_tensor("bo", [D], F32, kind="ExternalInput").ap()
    io["pos_all"] = nc.dram_tensor("pos_all", [1, TSEQ], F32, kind="ExternalInput").ap()
    io["th_hi"] = nc.dram_tensor("th_hi", [1, 128], F32, kind="ExternalInput").ap()
    io["th_lo"] = nc.dram_tensor("th_lo", [1, 128], F32, kind="ExternalInput").ap()
    io["csub"] = nc.dram_tensor("csub", [128, 64], BF16, kind="ExternalInput").ap()
    io["y"] = nc.dram_tensor("y", [TOWN, D], F32, kind="ExternalOutput").ap()
    ctx = Ctx(nc)
    emit_l0_attn(ctx, nc.Block, io)
    return nc


def run_l0_attn(x_full, wq, bq, wk, bk, wv, bv, wo, bo):
    th_hi, th_lo = _theta_split()
    wq_p, wk_p = _perm_rows(wq), _perm_rows(wk)
    bq_c = _bias_cols(_perm_rows(bq))
    bk_c = _bias_cols(_perm_rows(bk))
    bv_c = _bias_cols(np.asarray(bv))
    nc = build_l0_attn()
    in_maps = []
    for c in range(8):
        own = x_full[c * 1024:(c + 1) * 1024]
        oth = x_full[(c ^ 1) * 1024:((c ^ 1) + 1) * 1024]
        pos_own = (c % 2) * 1024 + np.arange(1024, dtype=np.float32)
        pos_oth = ((c ^ 1) % 2) * 1024 + np.arange(1024, dtype=np.float32)
        pos_all = np.concatenate([pos_own, pos_oth]).reshape(1, 2048)
        in_maps.append(dict(x_own=own, x_oth=oth, wq_p=wq_p, wk_p=wk_p, wv=wv, wo=wo,
                            bq_c=bq_c, bk_c=bk_c, bv_c=bv_c, bo=bo, pos_all=pos_all,
                            th_hi=th_hi, th_lo=th_lo, csub=_csub()))
    res = run_bass_kernel_spmd(nc, in_maps, core_ids=list(range(8)))
    return np.concatenate([r["y"] for r in res.results], axis=0)


def build_l0_ffn(ncores=8):
    nc = bass.Bass("TRN2", target_bir_lowering=False)
    io = {}
    io["x_own"] = nc.dram_tensor("x_own", [TOWN, D], F32, kind="ExternalInput").ap()
    io["w"] = nc.dram_tensor("w", [D, D], BF16, kind="ExternalInput").ap()
    io["y"] = nc.dram_tensor("y", [TOWN, D], F32, kind="ExternalOutput").ap()
    io["cc_in"] = nc.dram_tensor("cc_in", [1, 1], F32).ap()
    io["cc_out"] = nc.dram_tensor("cc_out", [1, 1], F32).ap()
    ctx = Ctx(nc)
    emit_l0_ffn(ctx, nc.Block, io, ncores=ncores)
    return nc


def run_l0_ffn(x_full, w_bf16):
    nc = build_l0_ffn(8)
    in_maps = [dict(x_own=x_full[c * 1024:(c + 1) * 1024], w=w_bf16) for c in range(8)]
    res = run_bass_kernel_spmd(nc, in_maps, core_ids=list(range(8)))
    return np.concatenate([r["y"] for r in res.results], axis=0)


def _cadd():
    c = np.zeros((128, 64), dtype=ml_dtypes.bfloat16)
    for j in range(64):
        c[j, j] = 1.0
        c[64 + j, j] = 1.0
    return c


def build_l1_attn():
    nc = bass.Bass("TRN2", target_bir_lowering=False)
    io = {}
    io["x_own"] = nc.dram_tensor("x_own", [TOWN, D], F32, kind="ExternalInput").ap()
    io["x_prev"] = nc.dram_tensor("x_prev", [256, D], F32, kind="ExternalInput").ap()
    for nm in ["Wq_p", "Wk_p", "Wv", "Wo"]:
        io[nm] = nc.dram_tensor(nm, [D, D], F32, kind="ExternalInput").ap()
    io["pos_all"] = nc.dram_tensor("pos_all", [1, 1536], F32, kind="ExternalInput").ap()
    io["th_hi"] = nc.dram_tensor("th_hi", [1, 128], F32, kind="ExternalInput").ap()
    io["th_lo"] = nc.dram_tensor("th_lo", [1, 128], F32, kind="ExternalInput").ap()
    io["csub"] = nc.dram_tensor("csub", [128, 64], BF16, kind="ExternalInput").ap()
    io["cadd"] = nc.dram_tensor("cadd", [128, 64], BF16, kind="ExternalInput").ap()
    io["prevflag"] = nc.dram_tensor("prevflag", [128, 1], F32, kind="ExternalInput").ap()
    io["y"] = nc.dram_tensor("y", [TOWN, D], F32, kind="ExternalOutput").ap()
    ctx = Ctx(nc)
    emit_l1_attn(ctx, nc.Block, io)
    return nc


def _l1_in_map(x_full, c, Wq_p, Wk_p, Wv, Wo, th_hi, th_lo):
    own = x_full[c * 1024:(c + 1) * 1024]
    if c % 2 == 1:
        x_prev = x_full[c * 1024 - 256:c * 1024]
        pos_prev = 768.0 + np.arange(256, dtype=np.float32)
        flag = np.ones((128, 1), np.float32)
    else:
        x_prev = np.zeros((256, D), np.float32)
        pos_prev = np.arange(256, dtype=np.float32)
        flag = np.zeros((128, 1), np.float32)
    pos_own = (c % 2) * 1024 + np.arange(1024, dtype=np.float32)
    pos_all = np.concatenate([pos_prev, pos_own,
                              np.zeros(256, np.float32)]).reshape(1, 1536)
    return dict(x_own=own, x_prev=x_prev, Wq_p=Wq_p, Wk_p=Wk_p, Wv=Wv, Wo=Wo,
                pos_all=pos_all, th_hi=th_hi, th_lo=th_lo, csub=_csub(),
                cadd=_cadd(), prevflag=flag)


def run_l1_attn(x_full, Wq, Wk, Wv, Wo):
    th_hi, th_lo = _theta_split()
    Wq_p, Wk_p = _perm_rows(Wq), _perm_rows(Wk)
    nc = build_l1_attn()
    in_maps = [_l1_in_map(x_full, c, Wq_p, Wk_p, Wv, Wo, th_hi, th_lo) for c in range(8)]
    res = run_bass_kernel_spmd(nc, in_maps, core_ids=list(range(8)))
    return np.concatenate([r["y"] for r in res.results], axis=0)


def build_l1_ffn():
    nc = bass.Bass("TRN2", target_bir_lowering=False)
    io = {}
    io["x_own"] = nc.dram_tensor("x_own", [TOWN, D], F32, kind="ExternalInput").ap()
    io["router"] = nc.dram_tensor("router", [D, 8], F32, kind="ExternalInput").ap()
    io["shared_gate"] = nc.dram_tensor("shared_gate", [D, 1536], F32, kind="ExternalInput").ap()
    io["shared_up"] = nc.dram_tensor("shared_up", [D, 1536], F32, kind="ExternalInput").ap()
    io["shared_down"] = nc.dram_tensor("shared_down", [1536, D], F32, kind="ExternalInput").ap()
    io["w_gate"] = nc.dram_tensor("w_gate", [8, D, 768], F32, kind="ExternalInput").ap()
    io["w_up"] = nc.dram_tensor("w_up", [8, D, 768], F32, kind="ExternalInput").ap()
    io["w_down"] = nc.dram_tensor("w_down", [8, 768, D], F32, kind="ExternalInput").ap()
    io["y"] = nc.dram_tensor("y", [TOWN, D], F32, kind="ExternalOutput").ap()
    ctx = Ctx(nc)
    emit_l1_ffn(ctx, nc.Block, io)
    return nc


def run_l1_ffn(x_full, router, sg, su, sd, wg, wu, wd):
    nc = build_l1_ffn()
    in_maps = [dict(x_own=x_full[c * 1024:(c + 1) * 1024], router=router,
                    shared_gate=sg, shared_up=su, shared_down=sd,
                    w_gate=wg, w_up=wu, w_down=wd) for c in range(8)]
    res = run_bass_kernel_spmd(nc, in_maps, core_ids=list(range(8)))
    return np.concatenate([r["y"] for r in res.results], axis=0)


def build_fused(ncores=8):
    nc = bass.Bass("TRN2", target_bir_lowering=False)
    EI = lambda nm, sh, dt=F32: nc.dram_tensor(nm, sh, dt, kind="ExternalInput").ap()
    x_own = EI("x_own", [TOWN, D])
    x_oth = EI("x_oth", [TOWN, D])
    y1 = nc.dram_tensor("y1", [TOWN, D], F32).ap()
    y2 = nc.dram_tensor("y2", [TOWN, D], F32).ap()
    y3 = nc.dram_tensor("y3", [TOWN, D], F32).ap()
    prevg = nc.dram_tensor("prevg", [512, D], F32).ap()
    yout = nc.dram_tensor("y", [TOWN, D], F32, kind="ExternalOutput").ap()
    cc_in = nc.dram_tensor("cc_in", [1, 1], F32).ap()
    cc_out = nc.dram_tensor("cc_out", [1, 1], F32).ap()

    ioA = dict(x_own=x_own, x_oth=x_oth, y=y1,
               wq_p=EI("wq_p", [D, D]), wk_p=EI("wk_p", [D, D]),
               wv=EI("wv", [D, D]), wo=EI("wo", [D, D]),
               bq_c=EI("bq_c", [128, H]), bk_c=EI("bk_c", [128, H]),
               bv_c=EI("bv_c", [128, H]), bo=EI("bo", [D]),
               pos_all=EI("pos_allA", [1, TSEQ]),
               th_hi=EI("th_hiA", [1, 128]), th_lo=EI("th_loA", [1, 128]),
               csub=EI("csubA", [128, 64], BF16))
    ioB = dict(x_own=y1, w=EI("w8", [D, D], BF16), y=y2, cc_in=cc_in, cc_out=cc_out)
    ioC = dict(x_own=y2, x_prev=prevg[0:256, :], y=y3,
               Wq_p=EI("Wq_p", [D, D]), Wk_p=EI("Wk_p", [D, D]),
               Wv=EI("Wv1", [D, D]), Wo=EI("Wo1", [D, D]),
               pos_all=EI("pos_allC", [1, 1536]),
               th_hi=EI("th_hiC", [1, 128]), th_lo=EI("th_loC", [1, 128]),
               csub=EI("csubC", [128, 64], BF16), cadd=EI("caddC", [128, 64], BF16),
               prevflag=EI("prevflag", [128, 1]))
    ioD = dict(x_own=y3, y=yout, router=EI("router", [D, 8]),
               shared_gate=EI("shared_gate", [D, 1536]),
               shared_up=EI("shared_up", [D, 1536]),
               shared_down=EI("shared_down", [1536, D]),
               w_gate=EI("w_gate", [8, D, 768]), w_up=EI("w_up", [8, D, 768]),
               w_down=EI("w_down", [8, 768, D]))

    ctx = Ctx(nc)

    blkno = [0]

    def run_block(emit, io, **kw):
        ctx.prefix = "blk%d_" % blkno[0]
        blkno[0] += 1
        with ExitStack() as st:
            ctx.stack = st
            with nc.named_scope(ctx.prefix + emit.__name__):
                emit(ctx, nc.Block, io, **kw)
        ctx.stack = None
        ctx.prefix = ""
        for attr in ("_ln_st", "_ln_st2", "eps_col"):
            if hasattr(ctx, attr):
                delattr(ctx, attr)

    run_block(emit_l0_attn, ioA)
    run_block(emit_l0_ffn, ioB, ncores=ncores)
    ccg = Trk(nc, "ccg")
    groups = [[2 * i, 2 * i + 1] for i in range(ncores // 2)]
    with nc.Block() as blk:
        @blk.gpsimd
        def _(g):
            g.wait_ge(ctx.ld2.h, ctx.ld2.n)
            ccg.inc(g.collective_compute("AllGather", ALU.bypass, groups,
                                         [y2[768:1024, :].opt()], [prevg.opt()]))
            ccg.wait(g)
    run_block(emit_l1_attn, ioC)
    run_block(emit_l1_ffn, ioD)
    return nc


def kernel(L0_attn_x, L0_attn_wq, L0_attn_bq, L0_attn_wk, L0_attn_bk, L0_attn_wv,
           L0_attn_bv, L0_attn_wo, L0_attn_bo, L0_ffn_weight, L1_attn_Wq, L1_attn_Wk,
           L1_attn_Wv, L1_attn_Wo, L1_ffn_router, L1_ffn_shared_gate, L1_ffn_shared_up,
           L1_ffn_shared_down, L1_ffn_w_gate, L1_ffn_w_up, L1_ffn_w_down):
    x = np.ascontiguousarray(np.asarray(L0_attn_x, dtype=np.float32).reshape(8192, 2048))
    th_hi, th_lo = _theta_split()
    wq_p, wk_p = _perm_rows(L0_attn_wq), _perm_rows(L0_attn_wk)
    bq_c = _bias_cols(_perm_rows(L0_attn_bq))
    bk_c = _bias_cols(_perm_rows(L0_attn_bk))
    bv_c = _bias_cols(L0_attn_bv)
    Wq_p, Wk_p = _perm_rows(L1_attn_Wq), _perm_rows(L1_attn_Wk)
    nc = build_fused(8)
    in_maps = []
    for c in range(8):
        pos_own = (c % 2) * 1024 + np.arange(1024, dtype=np.float32)
        pos_oth = ((c ^ 1) % 2) * 1024 + np.arange(1024, dtype=np.float32)
        posA = np.concatenate([pos_own, pos_oth]).reshape(1, 2048)
        if c % 2 == 1:
            pos_prev = 768.0 + np.arange(256, dtype=np.float32)
            flag = np.ones((128, 1), np.float32)
        else:
            pos_prev = np.arange(256, dtype=np.float32)
            flag = np.zeros((128, 1), np.float32)
        posC = np.concatenate([pos_prev, pos_own,
                               np.zeros(256, np.float32)]).reshape(1, 1536)
        in_maps.append(dict(
            x_own=x[c * 1024:(c + 1) * 1024],
            x_oth=x[(c ^ 1) * 1024:((c ^ 1) + 1) * 1024],
            wq_p=wq_p, wk_p=wk_p, wv=np.asarray(L0_attn_wv), wo=np.asarray(L0_attn_wo),
            bq_c=bq_c, bk_c=bk_c, bv_c=bv_c, bo=np.asarray(L0_attn_bo),
            pos_allA=posA, th_hiA=th_hi, th_loA=th_lo, csubA=_csub(),
            w8=np.asarray(L0_ffn_weight),
            Wq_p=Wq_p, Wk_p=Wk_p, Wv1=np.asarray(L1_attn_Wv), Wo1=np.asarray(L1_attn_Wo),
            pos_allC=posC, th_hiC=th_hi, th_loC=th_lo, csubC=_csub(), caddC=_cadd(),
            prevflag=flag,
            router=np.asarray(L1_ffn_router),
            shared_gate=np.asarray(L1_ffn_shared_gate),
            shared_up=np.asarray(L1_ffn_shared_up),
            shared_down=np.asarray(L1_ffn_shared_down),
            w_gate=np.asarray(L1_ffn_w_gate), w_up=np.asarray(L1_ffn_w_up),
            w_down=np.asarray(L1_ffn_w_down)))
    res = run_bass_kernel_spmd(nc, in_maps, core_ids=list(range(8)))
    y = np.concatenate([r["y"] for r in res.results], axis=0)
    return y.reshape(4, 2048, 2048)


# version 05
# speedup vs baseline: 1.4320x; 1.0698x over previous
import math
from contextlib import ExitStack
import numpy as np
import ml_dtypes
import concourse.bass as bass
import concourse.mybir as mybir
from concourse.bass_utils import run_bass_kernel_spmd

F32 = mybir.dt.float32
BF16 = mybir.dt.bfloat16
I32 = mybir.dt.int32
AF = mybir.ActivationFunctionType
ALU = mybir.AluOpType
AX = mybir.AxisListType

D = 2048
H = 16
TOWN = 1024
TSEQ = 2048
NT = TOWN // 128
NTS = TSEQ // 128
NI = D // 128

TWO_PI = 6.283185307179586
INV_2PI = 1.0 / TWO_PI
C_HI = 6.28125
C_LO = 0.00193530717958647692
PI_C = float(np.nextafter(np.float32(np.pi), np.float32(0.0)))
HALF_PI = float(np.float32(np.pi / 2))
ATTN_SCALE = 1.0 / math.sqrt(128.0)


_TRKN = [0]


class Trk:
    def __init__(self, nc, name):
        _TRKN[0] += 1
        self.h = nc.alloc_semaphore("%s_%d" % (name, _TRKN[0]))
        self.n = 0

    def inc(self, instr, k=1):
        instr.then_inc(self.h, k)
        self.n += k
        return self.n

    def wait(self, eng, val=None):
        eng.wait_ge(self.h, self.n if val is None else val)


class Ctx:
    def __init__(self, nc):
        self.nc = nc
        self.pe = Trk(nc, "t_pe")
        self.act = Trk(nc, "t_act")
        self.dve = Trk(nc, "t_dve")
        self.pool = Trk(nc, "t_pool")
        self.ld = Trk(nc, "t_ld")
        self.ld2 = Trk(nc, "t_ld2")
        self.ld3 = Trk(nc, "t_ld3")

        self.stack = None
        self.prefix = ""

    def sb(self, name, shape, dt):
        name = self.prefix + name
        if self.stack is not None:
            return self.stack.enter_context(self.nc.sbuf_tensor(name, shape, dt))
        return self.nc.alloc_sbuf_tensor(name, shape, dt)

    def ps(self, name, shape, dt):
        name = self.prefix + name
        if self.stack is not None:
            return self.stack.enter_context(self.nc.psum_tensor(name, shape, dt))
        return self.nc.alloc_psum_tensor(name, shape, dt)


def emit_common_consts(ctx, blk_new):
    nc = ctx.nc
    ident = ctx.sb("c_ident", [128, 128], BF16)
    ones_row = ctx.sb("c_ones_row", [1, 512], BF16)
    ctx.eps_col = ctx.sb("c_eps", [128, 1], F32)
    with blk_new() as blk:
        @blk.gpsimd
        def _(g):
            ctx.pool.inc(g.memset(ident[:, :], 1.0))
            g.memset(ones_row[:, :], 1.0)
            g.memset(ctx.eps_col[:, :], 1e-5)
            ctx.pool.wait(g)
            g.affine_select(ident[:, :], ident[:, :], [[1, 128]], ALU.is_equal, 0.0,
                            base=0, channel_multiplier=-1)
    return ident, ones_row


def emit_rope_tables(ctx, blk_new, pfr, scrA, scrB, pos_ap, theta_hi_ap, theta_lo_ap,
                     ncols, name, swap=False):
    nc = ctx.nc
    cs_t = ctx.sb(name + "_cs", [128, ncols], F32)
    s_pos = scrB[0:1, 0:ncols]
    s_th = ctx.sb(name + "_th", [1, 128], F32)
    s_tl = ctx.sb(name + "_tl", [1, 128], F32)
    s_u = scrA[:, 0:512]
    s_n = ctx.sb(name + "_n", [128, 512], I32)
    s_nf = scrA[:, 512:1024]
    s_m = scrA[:, 1024:1536]
    s_w = scrA[:, 1536:2048]
    cos_half = slice(64, 128) if swap else slice(0, 64)
    ldm = Trk(ctx.nc, name + "_ld")
    with blk_new() as blk:
        @blk.sync
        def _(s):
            ldm.inc(s.dma_start(s_pos, pos_ap), 16)
            ldm.inc(s.dma_start(s_th[:, :], theta_hi_ap), 16)
            ldm.inc(s.dma_start(s_tl[:, :], theta_lo_ap), 16)
            ldm.wait(s)
    nch = ncols // 512
    for c in range(nch):
        cs = slice(c * 512, (c + 1) * 512)
        with blk_new() as blk:
            @blk.tensor
            def _(t):
                ldm.wait(t)
                t.matmul(pfr[:, :], s_th[:, :], scrB[0:1, c * 512:(c + 1) * 512],
                         start=True, stop=False)
                ctx.pe.inc(t.matmul(pfr[:, :], s_tl[:, :], scrB[0:1, c * 512:(c + 1) * 512],
                                    start=False, stop=True))
            @blk.vector
            def _(v):
                ctx.pe.wait(v)
                def step(ins):
                    ctx.dve.inc(ins)
                    ctx.dve.wait(v)
                step(v.tensor_scalar(s_u, pfr[:, :], INV_2PI, None, ALU.mult))
                step(v.tensor_copy(s_n[:, :], s_u))
                step(v.tensor_copy(s_nf, s_n[:, :]))
                step(v.scalar_tensor_tensor(s_m, s_nf, -C_HI, pfr[:, :], ALU.mult, ALU.add))
                step(v.scalar_tensor_tensor(s_m, s_nf, -C_LO, s_m, ALU.mult, ALU.add))
                step(v.tensor_scalar(s_w, s_m, -PI_C, TWO_PI, ALU.is_lt, ALU.mult))
                step(v.tensor_add(s_m, s_m, s_w))
                step(v.tensor_scalar(s_w, s_m, PI_C, TWO_PI, ALU.is_gt, ALU.mult))
                step(v.tensor_sub(s_m, s_m, s_w))
                step(v.tensor_scalar(s_m, s_m, -PI_C, PI_C, ALU.max, ALU.min))
                mh = s_m[cos_half, :]
                wh = s_w[cos_half, :]
                step(v.tensor_scalar(mh, mh, HALF_PI, None, ALU.add))
                step(v.tensor_scalar(wh, mh, PI_C, TWO_PI, ALU.is_gt, ALU.mult))
                step(v.tensor_sub(mh, mh, wh))
                step(v.tensor_scalar(mh, mh, -PI_C, PI_C, ALU.max, ALU.min))
            @blk.scalar
            def _(sc):
                ctx.dve.wait(sc)
                ctx.act.inc(sc.activation(cs_t[:, cs], s_m, AF.Sin))
    return cs_t


def emit_ln_pipe(ctx, blk_new, tiles, xln_fm, fm_cols, tm_dt, banks, bank_cap,
                 ident, ld_unused, post_cb=None, scope=None, xs=None, tms=None):
    nc = ctx.nc
    xstage = xs if xs is not None else [ctx.sb("lnp_xs%d" % i, [128, D], F32)
                                        for i in range(2)]
    xtm = tms if tms is not None else [ctx.sb("lnp_tm%d" % i, [128, D], tm_dt)
                                       for i in range(2)]
    st = ctx.sb("lnp_st", [128, 16], F32)
    ldp = [Trk(ctx.nc, "lnp_ld%d" % i) for i in range(2)]
    nbatch = D // 128 // bank_cap
    xv = xln_fm[:, :].rearrange("p (i c) -> p i c", c=fm_cols)
    import contextlib
    scope_cm = nc.named_scope(scope) if scope else contextlib.nullcontext()
    with scope_cm, blk_new() as blk:
        marks = {"xlnw": {}, "red": {}, "tr": {}, "bank": {0: 0, 1: 0}, "ev": {}}
        for tt, (src, tcol0) in enumerate(tiles):
            par = tt % 2
            s0 = 8 * par
            rs, mean, sumsq, var, rstd, nb = (st[:, s0:s0 + 1], st[:, s0 + 1:s0 + 2],
                                              st[:, s0 + 2:s0 + 3], st[:, s0 + 3:s0 + 4],
                                              st[:, s0 + 4:s0 + 5], st[:, s0 + 5:s0 + 6])
            @blk.sync
            def _(s, tt=tt, par=par, src=src):
                if tt >= 2:
                    s.wait_ge(ctx.act.h, marks["xlnw"][tt - 2])
                    s.wait_ge(ctx.dve.h, marks["red"][tt - 2])
                ldp[par].inc(s.dma_start(xstage[par][:, :], src), 16)

            @blk.scalar
            def _(sc, tt=tt, par=par, rs=rs, sumsq=sumsq):
                ldp[par].wait(sc)
                if tt >= 2:
                    sc.wait_ge(ctx.pe.h, marks["tr"][tt - 2])
                ctx.act.inc(sc.activation(xtm[par][:, :], xstage[par][:, :], AF.Square,
                                          accum_out=sumsq))

            @blk.vector
            def _(v, tt=tt, par=par, rs=rs, mean=mean, sumsq=sumsq, var=var):
                ldp[par].wait(v)
                mk = ctx.dve.inc(v.reduce_sum(rs, xstage[par][:, :], axis=AX.X))
                marks["red"][tt] = mk
                ctx.dve.wait(v)
                ctx.dve.inc(v.tensor_scalar(mean, rs, 1.0 / D, None, ALU.mult))
                ctx.act.wait(v)
                ctx.dve.wait(v)
                ctx.dve.inc(v.tensor_tensor(var, mean, mean, ALU.mult))
                ctx.dve.wait(v)
                ctx.dve.inc(v.scalar_tensor_tensor(var, sumsq, 1.0 / D, var,
                                                   ALU.mult, ALU.subtract))

            @blk.scalar
            def _(sc, tt=tt, var=var):
                ctx.dve.wait(sc)
                ctx.act.inc(sc.activation(var, var, AF.Sqrt, bias=ctx.eps_col[:, :]))

            @blk.vector
            def _(v, tt=tt, var=var, rstd=rstd, mean=mean, nb=nb):
                ctx.act.wait(v)
                ctx.dve.inc(v.reciprocal(rstd, var))
                ctx.dve.wait(v)
                ctx.dve.inc(v.scalar_tensor_tensor(nb, mean, -1.0, rstd, ALU.mult, ALU.mult))

            @blk.scalar
            def _(sc, tt=tt, par=par, rstd=rstd, nb=nb):
                ctx.dve.wait(sc)
                mk = ctx.act.inc(sc.activation(xtm[par][:, :], xstage[par][:, :],
                                               AF.Identity, bias=nb, scale=rstd))
                marks["xlnw"][tt] = mk

            for bi in range(nbatch):
                bank = (tt * nbatch + bi) % 2
                @blk.tensor
                def _(t, tt=tt, par=par, bi=bi, bank=bank):
                    t.wait_ge(ctx.act.h, marks["xlnw"][tt])
                    t.wait_ge(ctx.dve.h, marks["bank"][bank])
                    for j in range(bank_cap):
                        jj = bi * bank_cap + j
                        ins = t.transpose(banks[bank][:, j * 128:(j + 1) * 128],
                                          xtm[par][:, jj * 128:(jj + 1) * 128],
                                          ident[:, :])
                    mk = ctx.pe.inc(ins)
                    if bi == nbatch - 1:
                        marks["tr"][tt] = mk
                @blk.vector
                def _(v, tt=tt, bi=bi, bank=bank, tcol0=tcol0):
                    ctx.pe.wait(v)
                    mk = ctx.dve.inc(v.tensor_copy(
                        xv[:, bi * bank_cap:(bi + 1) * bank_cap, tcol0:tcol0 + 128],
                        banks[bank][:, 0:bank_cap * 128].rearrange("p (j c) -> p j c", c=128)))
                    marks["bank"][bank] = mk
                    marks["ev"][tt] = mk
            if post_cb is not None:
                post_cb(tt, blk, xtm[par], marks)
    return marks


def emit_ln_fm(ctx, blk, src_f32, xln_fm, tcol0, pbank_t, scr_f32, xln_tm, ident,
               fm_cols, ld):
    if not hasattr(ctx, "_ln_st"):
        ctx._ln_st = ctx.sb("ln_stats", [128, 8], F32)
    st = ctx._ln_st
    rs, mean, sumsq, var, rstd, nb = (st[:, 0:1], st[:, 1:2], st[:, 2:3], st[:, 3:4],
                                      st[:, 4:5], st[:, 5:6])

    @blk.scalar
    def _(sc):
        ld.wait(sc)
        ctx.dve.wait(sc)
        ctx.act.wait(sc)
        ctx.act.inc(sc.activation(scr_f32[:, :], src_f32[:, :], AF.Square,
                                  accum_out=sumsq))

    @blk.vector
    def _(v):
        ld.wait(v)
        ctx.dve.inc(v.reduce_sum(rs, src_f32[:, :], axis=AX.X))
        ctx.dve.wait(v)
        ctx.dve.inc(v.tensor_scalar(mean[:, :], rs, 1.0 / D, None, ALU.mult))
        ctx.act.wait(v)
        ctx.dve.wait(v)
        ctx.dve.inc(v.tensor_tensor(var[:, :], mean, mean, ALU.mult))
        ctx.dve.wait(v)
        ctx.dve.inc(v.scalar_tensor_tensor(var[:, :], sumsq, 1.0 / D, var, ALU.mult, ALU.subtract))

    @blk.scalar
    def _(sc):
        ctx.dve.wait(sc)
        ctx.act.inc(sc.activation(var[:, :], var[:, :], AF.Sqrt, bias=ctx.eps_col[:, :]))

    @blk.vector
    def _(v):
        ctx.act.wait(v)
        ctx.dve.inc(v.reciprocal(rstd[:, :], var[:, :]))
        ctx.dve.wait(v)
        ctx.dve.inc(v.scalar_tensor_tensor(nb[:, :], mean, -1.0, rstd, ALU.mult, ALU.mult))

    @blk.scalar
    def _(sc):
        ctx.dve.wait(sc)
        ctx.act.inc(sc.activation(xln_tm[:, :], src_f32[:, :], AF.Identity,
                                  bias=nb, scale=rstd))

    n_feat_tiles = D // 128
    for half in range(n_feat_tiles // 8):
        @blk.tensor
        def _(t, half=half):
            ctx.act.wait(t)
            ctx.dve.wait(t)
            for j in range(8):
                jj = half * 8 + j
                ins = t.transpose(pbank_t[:, j * 128:(j + 1) * 128],
                                  xln_tm[:, jj * 128:(jj + 1) * 128], ident[:, :])
            ctx.pe.inc(ins)

        @blk.vector
        def _(v, half=half):
            ctx.pe.wait(v)
            dst = xln_fm[:, :].rearrange("p (i c) -> p i c", c=fm_cols)
            ctx.dve.inc(v.tensor_copy(
                dst[:, half * 8:(half + 1) * 8, tcol0:tcol0 + 128],
                pbank_t[:, :].rearrange("p (i c) -> p i c", c=128)))


def emit_l0_attn(ctx, blk_new, io):
    nc = ctx.nc
    pk = [ctx.ps("a_pk%d" % i, [128, 512], F32) for i in range(2)]
    pctx = [ctx.ps("a_pctx%d" % i, [128, 512], F32) for i in range(4)]
    ptr = [ctx.ps("a_ptr%d" % i, [128, 1024], BF16) for i in range(2)]

    ident, ones_row = emit_common_consts(ctx, blk_new)
    ld_x = Trk(nc, "a_ld_x")
    ld_w = Trk(nc, "a_ld_w")
    ld_wo = Trk(nc, "a_ld_wo")
    ld_b = Trk(nc, "a_ld_b")
    ld_xr = [Trk(nc, "a_ld_xr%d" % i) for i in range(2)]

    xln = ctx.sb("a_xln", [128, NI * TSEQ], BF16)
    wraw = [ctx.sb("a_wraw%d" % i, [128, D], F32) for i in range(2)]
    wpool = ctx.sb("a_wpool", [128, 4 * D], BF16)
    kfm = ctx.sb("a_kfm", [128, TSEQ], BF16)
    qfm = ctx.sb("a_qfm", [128, TOWN], BF16)
    vfm = ctx.sb("a_vfm", [128, TSEQ], BF16)
    vone = ctx.sb("a_vone", [128, NTS * 129], BF16)
    ptil = [ctx.sb("a_ptil%d" % i, [128, 512], BF16) for i in range(2)]
    ropeT = ctx.sb("a_ropeT", [128, 512], BF16)
    csub = ctx.sb("a_csub", [128, 64], BF16)
    ctx_tm = ctx.sb("a_ctxtm", [128, 512], BF16)
    ctx_fm = ctx.sb("a_ctxfm", [128, H * TOWN], BF16)
    recips = ctx.sb("a_recips", [128, 4], F32)
    bqc = ctx.sb("a_bqc", [128, H], F32)
    bkc = ctx.sb("a_bkc", [128, H], F32)
    bvc = ctx.sb("a_bvc", [128, H], F32)
    bo_row = ctx.sb("a_borow", [1, D], BF16)
    xres = [ctx.sb("a_xres%d" % i, [128, 512], F32) for i in range(2)]
    ysb = [ctx.sb("a_ysb%d" % i, [128, 512], F32) for i in range(2)]

    cs_t = emit_rope_tables(ctx, blk_new, pctx[0], wraw[0], wraw[1],
                            io["pos_all"], io["th_hi"], io["th_lo"], TSEQ, "rt0")

    xln_v = xln[:, :].rearrange("p (i c) -> p i c", c=TSEQ)
    vone_v = vone[:, :].rearrange("p (t c) -> p t c", c=129)
    wqT_v = wpool[:, 0 * D:1 * D].rearrange("p (j c) -> p j c", c=128)
    wkT_v = wpool[:, 1 * D:2 * D].rearrange("p (j c) -> p j c", c=128)
    wvT_v = wpool[:, 2 * D:3 * D].rearrange("p (j c) -> p j c", c=128)
    woT_v = wpool[:, 0:4 * D].rearrange("p (j c) -> p j c", c=512)
    wT_v = {"k": wkT_v, "q": wqT_v, "v": wvT_v}

    with blk_new() as blk:
        @blk.sync
        def _(s):
            ld_b.inc(s.dma_start(bqc[:, :], io["bq_c"]), 16)
            ld_b.inc(s.dma_start(bkc[:, :], io["bk_c"]), 16)
            ld_b.inc(s.dma_start(bvc[:, :], io["bv_c"]), 16)
            ld_b.inc(s.dma_start(csub[:, :], io["csub"]), 16)
            ld_b.wait(s)
        @blk.gpsimd
        def _(g):
            ctx.ld3.inc(g.dma_start(bo_row[:, :], io["bo"].rearrange("(a d) -> a d", a=1)), 16)
            ctx.ld3.wait(g)
        @blk.vector
        def _(v):
            ctx.dve.inc(v.memset(vone_v[:, :, 128:129], 1.0))


    tiles = [(io["x_own"][t * 128:(t + 1) * 128, :], t * 128) for t in range(NT)] + \
            [(io["x_oth"][t * 128:(t + 1) * 128, :], TOWN + t * 128) for t in range(NT)]
    emit_ln_pipe(ctx, blk_new, tiles, xln, TSEQ, BF16, ptr, 8, ident, ld_x,
                 scope="a_ln", xs=wraw)

    wnames = ["wq_p", "wk_p", "wv"]
    ldw = [Trk(nc, "a_ldw%d" % m) for m in range(3)]
    wbf3 = [ctx.sb("a_wbf%d" % m, [128, D], BF16) for m in range(3)]
    with nc.named_scope("a_heads"), blk_new() as blk:
        marks = {
            "ptr_free": {0: ("dve", 0), 1: ("dve", 0)},
            "pk_free": {0: ("act", 0), 1: ("act", 0)},
            "rot_free": {0: ("act", 0), 1: ("act", 0)},
            "ctx_free": {i: ("dve", 0) for i in range(4)},
            "wb_free": {m: 0 for m in range(3)},
            "wT_ready": {m: 0 for m in range(3)},
            "kfm_free": 0, "qfm_free": 0, "pv_last": 0, "vfm_free": 0,
            "ctxtm_free": 0, "vone_dve": 0, "ropeT_dve": 0, "vfm_act": 0,
        }

        def wait_mark(eng, mk):
            nm, val = mk
            eng.wait_ge(getattr(ctx, nm).h, val)

        @blk.gpsimd
        def _(g):
            for m in range(3):
                ldw[m].inc(g.dma_start(wbf3[m][:, :], io[wnames[m]][0:128, :]), 16)

        for h in range(H):
            for m, dstv in enumerate([wqT_v, wkT_v, wvT_v]):
                for bi in range(2):
                    @blk.tensor
                    def _(t, m=m, bi=bi, h=h):
                        if bi == 0:
                            ldw[m].wait(t)
                        wait_mark(t, marks["ptr_free"][bi % 2])
                        for j in range(8):
                            jj = bi * 8 + j
                            ins = t.transpose(ptr[bi % 2][:, j * 128:(j + 1) * 128],
                                              wbf3[m][:, jj * 128:(jj + 1) * 128],
                                              ident[:, :])
                        mk = ctx.pe.inc(ins)
                        if bi == 1:
                            marks["wb_free"][m] = mk
                    @blk.vector
                    def _(v, m=m, bi=bi, dstv=dstv):
                        ctx.pe.wait(v)
                        mk = ctx.dve.inc(v.tensor_copy(
                            dstv[:, bi * 8:(bi + 1) * 8, 0:128],
                            ptr[bi % 2][:, :].rearrange("p (j c) -> p j c", c=128)))
                        marks["ptr_free"][bi % 2] = ("dve", mk)
                        if bi == 1:
                            marks["wT_ready"][m] = mk

            def proj_chunk_m(t, which, cs, bank):
                wait_mark(t, marks["pk_free"][bank])
                t.wait_ge(ctx.dve.h, marks["wT_ready"][{"q": 0, "k": 1, "v": 2}[which]])
                for j in range(NI):
                    ins = t.matmul(pk[bank][:, :], wT_v[which][:, j, :], xln_v[:, j, cs],
                                   start=(j == 0), stop=(j == NI - 1))
                return ctx.pe.inc(ins)

            def rope_emit(dst_fm, bc_even, bc_odd, cs, bank, dst_gate, h=h):
                @blk.vector
                def _(v):
                    ctx.pe.wait(v)
                    mk = ctx.dve.inc(v.scalar_tensor_tensor(ropeT[:, :], pk[bank][:, :],
                                                            bc_even if False else bc_full,
                                                            cs_t[:, cs], ALU.add, ALU.mult))
                    marks["ropeT_dve"] = mk
                @blk.tensor
                def _(t):
                    t.wait_ge(ctx.dve.h, marks["ropeT_dve"])
                    wait_mark(t, marks["rot_free"][bank])
                    ctx.pe.inc(t.matmul(pctx[bank][0:64, 0:512], csub[:, :], ropeT[:, :],
                                        start=True, stop=True))
                @blk.scalar
                def _(sc):
                    ctx.pe.wait(sc)
                    if dst_gate is not None and h > 0:
                        sc.wait_ge(ctx.pe.h, marks[dst_gate])
                    ctx.act.inc(sc.activation(dst_fm[64:128, cs], pk[bank][64:128, :],
                                              AF.Identity, bias=bc_odd))
                    mk = ctx.act.inc(sc.activation(dst_fm[0:64, cs],
                                                   pctx[bank][0:64, 0:512], AF.Identity))
                    marks["pk_free"][bank] = ("act", mk)
                    marks["rot_free"][bank] = ("act", mk)

            for c in range(4):
                cs = slice(c * 512, (c + 1) * 512)
                @blk.tensor
                def _(t, c=c, cs=cs):
                    proj_chunk_m(t, "k", cs, c % 2)
                bc_full = bkc[:, h:h + 1]
                rope_emit(kfm, bkc[0:64, h:h + 1], bkc[64:128, h:h + 1], cs, c % 2,
                          "kfm_free" if c == 0 else None)
            for c in range(2):
                cs = slice(c * 512, (c + 1) * 512)
                @blk.tensor
                def _(t, c=c, cs=cs):
                    proj_chunk_m(t, "q", cs, c % 2)
                bc_full = bqc[:, h:h + 1]
                rope_emit(qfm, bqc[0:64, h:h + 1], bqc[64:128, h:h + 1], cs, c % 2,
                          "qfm_free" if c == 0 else None)
            for c in range(4):
                cs = slice(c * 512, (c + 1) * 512)
                @blk.tensor
                def _(t, c=c, cs=cs):
                    proj_chunk_m(t, "v", cs, c % 2)
                @blk.scalar
                def _(sc, c=c, cs=cs, h=h):
                    ctx.pe.wait(sc)
                    if h > 0 and c == 0:
                        sc.wait_ge(ctx.pe.h, marks["vfm_free"])
                    mk = ctx.act.inc(sc.activation(vfm[:, cs], pk[c % 2][:, :],
                                                   AF.Identity, bias=bvc[:, h:h + 1]))
                    marks["pk_free"][c % 2] = ("act", mk)
                    marks["vfm_act"] = mk
            for bi in range(2):
                @blk.tensor
                def _(t, bi=bi, h=h):
                    t.wait_ge(ctx.act.h, marks["vfm_act"])
                    wait_mark(t, marks["ptr_free"][bi % 2])
                    if h > 0 and bi == 0:
                        t.wait_ge(ctx.pe.h, marks["pv_last"])
                    for j in range(8):
                        jj = bi * 8 + j
                        ins = t.transpose(ptr[bi % 2][:, j * 128:(j + 1) * 128],
                                          vfm[:, jj * 128:(jj + 1) * 128], ident[:, :])
                    mk = ctx.pe.inc(ins)
                    if bi == 1:
                        marks["vfm_free"] = mk
                @blk.vector
                def _(v, bi=bi):
                    ctx.pe.wait(v)
                    mk = ctx.dve.inc(v.tensor_copy(
                        vone_v[:, bi * 8:(bi + 1) * 8, 0:128],
                        ptr[bi % 2][:, :].rearrange("p (j c) -> p j c", c=128)))
                    marks["ptr_free"][bi % 2] = ("dve", mk)
                    marks["vone_dve"] = mk

            if h + 1 < H:
                @blk.gpsimd
                def _(g, h=h):
                    for m in range(3):
                        g.wait_ge(ctx.pe.h, marks["wb_free"][m])
                        ldw[m].inc(g.dma_start(
                            wbf3[m][:, :], io[wnames[m]][(h + 1) * 128:(h + 2) * 128, :]), 16)

            for qc in range(2):
                qs = slice(qc * 512, (qc + 1) * 512)
                s_marks, e_marks, pv_marks = {}, {}, {}

                def emit_pv(t, tk):
                    pbuf = ptil[tk % 2]
                    if tk == 0:
                        for sub in range(4):
                            wait_mark(t, marks["ctx_free"][sub])
                    for sub in range(4):
                        ins = t.matmul(pctx[sub][:, 0:129], pbuf[:, sub * 128:(sub + 1) * 128],
                                       vone_v[:, tk, :],
                                       start=(tk == 0), stop=(tk == NTS - 1))
                    pv_marks[tk] = ctx.pe.inc(ins)

                for tk in range(NTS):
                    @blk.tensor
                    def _(t, tk=tk, qs=qs, qc=qc):
                        if tk >= 2:
                            t.wait_ge(ctx.act.h, e_marks[tk - 2])
                        else:
                            wait_mark(t, marks["pk_free"][tk % 2])
                        if tk == 0 and qc == 0:
                            t.wait_ge(ctx.dve.h, marks["vone_dve"])
                        ins = t.matmul(pk[tk % 2][:, :], kfm[:, tk * 128:(tk + 1) * 128],
                                       qfm[:, qs], start=True, stop=True)
                        s_marks[tk] = ctx.pe.inc(ins)
                        if tk >= 1:
                            t.wait_ge(ctx.act.h, e_marks[tk - 1])
                            emit_pv(t, tk - 1)
                    @blk.scalar
                    def _(sc, tk=tk, qc=qc):
                        sc.wait_ge(ctx.pe.h, s_marks[tk])
                        if tk >= 2:
                            sc.wait_ge(ctx.pe.h, pv_marks[tk - 2])
                        elif qc == 0:
                            sc.wait_ge(ctx.pe.h, marks["pv_last"])
                        ins = sc.activation(ptil[tk % 2][:, :], pk[tk % 2][:, :], AF.Exp,
                                            scale=ATTN_SCALE)
                        e_marks[tk] = ctx.act.inc(ins)
                        if tk == NTS - 1:
                            marks["pk_free"][0] = ("act", e_marks[NTS - 2])
                            marks["pk_free"][1] = ("act", e_marks[NTS - 1])
                @blk.tensor
                def _(t, qc=qc):
                    t.wait_ge(ctx.act.h, e_marks[NTS - 1])
                    emit_pv(t, NTS - 1)
                marks["pv_last"] = ctx.pe.n
                @blk.vector
                def _(v, qc=qc):
                    v.wait_ge(ctx.pe.h, pv_marks[NTS - 1])
                    v.wait_ge(ctx.pe.h, marks["ctxtm_free"])
                    ctx.dve.wait(v)
                    for sub in range(4):
                        ctx.dve.inc(v.reciprocal(recips[:, sub:sub + 1],
                                                 pctx[sub][:, 128:129]))
                    ctx.dve.wait(v)
                    for sub in range(4):
                        mk = ctx.dve.inc(v.tensor_scalar(ctx_tm[:, sub * 128:(sub + 1) * 128],
                                                         pctx[sub][:, 0:128],
                                                         recips[:, sub:sub + 1], None, ALU.mult))
                        marks["ctx_free"][sub] = ("dve", mk)
                @blk.tensor
                def _(t, qc=qc):
                    ctx.dve.wait(t)
                    wait_mark(t, marks["ptr_free"][0])
                    for sub in range(4):
                        ins = t.transpose(ptr[0][:, sub * 128:(sub + 1) * 128],
                                          ctx_tm[:, sub * 128:(sub + 1) * 128], ident[:, :])
                    marks["ctxtm_free"] = ctx.pe.inc(ins)
                @blk.vector
                def _(v, qc=qc, h=h):
                    ctx.pe.wait(v)
                    mk = ctx.dve.inc(v.tensor_copy(
                        ctx_fm[:, h * TOWN + qc * 512:h * TOWN + (qc + 1) * 512],
                        ptr[0][:, 0:512]))
                    marks["ptr_free"][0] = ("dve", mk)
            marks["kfm_free"] = marks["pv_last"]
            marks["qfm_free"] = marks["pv_last"]

    ctx_v = ctx_fm[:, :].rearrange("p (i c) -> p i c", c=TOWN)
    for oc in range(4):
        with nc.named_scope("a_oproj"), blk_new() as blk:
            for sub in range(4):
                @blk.gpsimd
                def _(g, sub=sub, oc=oc):
                    g.wait_ge(ctx.pe.h, ctx.pe.n)
                    ld_wo.inc(g.dma_start(wbf3[sub % 2][:, :],
                                          io["wo"][(oc * 4 + sub) * 128:(oc * 4 + sub + 1) * 128, :]), 16)
                for bi in range(2):
                    @blk.tensor
                    def _(t, sub=sub, bi=bi):
                        ld_wo.wait(t)
                        t.wait_ge(ctx.dve.h, ctx.dve.n)
                        for j in range(8):
                            jj = bi * 8 + j
                            ins = t.transpose(ptr[bi % 2][:, j * 128:(j + 1) * 128],
                                              wbf3[sub % 2][:, jj * 128:(jj + 1) * 128],
                                              ident[:, :])
                        ctx.pe.inc(ins)
                    @blk.vector
                    def _(v, sub=sub, bi=bi):
                        ctx.pe.wait(v)
                        ctx.dve.inc(v.tensor_copy(
                            woT_v[:, bi * 8:(bi + 1) * 8, sub * 128:sub * 128 + 128],
                            ptr[bi % 2][:, :].rearrange("p (j c) -> p j c", c=128)))
        with blk_new() as blk:
            t_marks = {}
            for tt in range(NT):
                bank = tt % 4
                @blk.sync
                def _(s, tt=tt, oc=oc):
                    if tt >= 2:
                        s.wait_ge(ctx.dve.h, t_marks[tt - 2])
                    ld_xr[tt % 2].inc(s.dma_start(xres[tt % 2][:, :],
                                                  io["x_own"][tt * 128:(tt + 1) * 128,
                                                              oc * 512:(oc + 1) * 512]), 16)
                @blk.tensor
                def _(t, tt=tt, bank=bank, oc=oc):
                    if tt >= 4:
                        t.wait_ge(ctx.dve.h, t_marks[tt - 4])
                    t.matmul(pctx[bank][:, :], ones_row[0:1, 0:128],
                             bo_row[0:1, oc * 512:(oc + 1) * 512], start=True, stop=False)
                    for j in range(NI):
                        ins = t.matmul(pctx[bank][:, :], ctx_v[:, j, tt * 128:(tt + 1) * 128],
                                       woT_v[:, j, :], start=False, stop=(j == NI - 1))
                    ctx.pe.inc(ins)
                @blk.vector
                def _(v, tt=tt, bank=bank):
                    ctx.pe.wait(v)
                    ld_xr[tt % 2].wait(v)
                    v.wait_ge(ctx.ld2.h, ctx.ld2.n)
                    t_marks[tt] = ctx.dve.inc(v.tensor_add(ysb[tt % 2][:, :],
                                                           xres[tt % 2][:, :],
                                                           pctx[bank][:, :]))
                @blk.scalar
                def _(sc, tt=tt, oc=oc):
                    sc.wait_ge(ctx.dve.h, t_marks[tt])
                    ctx.ld2.inc(sc.dma_start(io["y"][tt * 128:(tt + 1) * 128,
                                                     oc * 512:(oc + 1) * 512],
                                             ysb[tt % 2][:, :]), 16)
    with blk_new() as blk:
        @blk.scalar
        def _(sc):
            ctx.ld2.wait(sc)


def emit_l1_attn(ctx, blk_new, io):
    nc = ctx.nc
    TK = 1280
    NTK = TK // 128
    W = 256
    pk = [ctx.ps("c_pk%d" % i, [128, 512], F32) for i in range(2)]
    pctx = [ctx.ps("c_pctx%d" % i, [128, 512], F32) for i in range(4)]
    ptr = [ctx.ps("c_ptr%d" % i, [128, 1024], BF16) for i in range(2)]

    ident, ones_row = emit_common_consts(ctx, blk_new)
    ld_x = Trk(nc, "c_ld_x")
    ld_w = Trk(nc, "c_ld_w")
    ld_wo = Trk(nc, "c_ld_wo")
    ld_b = Trk(nc, "c_ld_b")
    ld_xr = [Trk(nc, "c_ld_xr%d" % i) for i in range(2)]

    xln = ctx.sb("c_xln", [128, NI * TK], BF16)
    wraw = [ctx.sb("c_wraw%d" % i, [128, D], F32) for i in range(2)]
    wpool = ctx.sb("c_wpool", [128, 4 * D], BF16)
    kfm = ctx.sb("c_kfm", [128, TK], BF16)
    qfm = ctx.sb("c_qfm", [128, TOWN], BF16)
    vfm = ctx.sb("c_vfm", [128, TK], BF16)
    vone = ctx.sb("c_vone", [128, NTK * 129], BF16)
    ptil = ctx.sb("c_ptil", [128, 4 * 256], BF16)
    ropeT = ctx.sb("c_ropeT", [128, 1024], BF16)
    csub = ctx.sb("c_csub", [128, 64], BF16)
    cadd = ctx.sb("c_cadd", [128, 64], BF16)
    maskA = ctx.sb("c_maskA", [128, 256], BF16)
    maskB = ctx.sb("c_maskB", [128, 256], BF16)
    prevflag = ctx.sb("c_prevflag", [128, 1], F32)
    ctx_tm = ctx.sb("c_ctxtm", [128, 512], BF16)
    ctx_fm = ctx.sb("c_ctxfm", [128, H * TOWN], BF16)
    recips = ctx.sb("c_recips", [128, 4], F32)
    xres = [ctx.sb("c_xres%d" % i, [128, 512], F32) for i in range(2)]
    ysb = [ctx.sb("c_ysb%d" % i, [128, 512], F32) for i in range(2)]

    cs_t = emit_rope_tables(ctx, blk_new, pctx[0], wraw[0], wraw[1],
                            io["pos_all"], io["th_hi"], io["th_lo"], 1536, "rt1")
    sc_t = emit_rope_tables(ctx, blk_new, pctx[0], wraw[0], wraw[1],
                            io["pos_all"], io["th_hi"], io["th_lo"], 1536, "rt1s",
                            swap=True)

    xln_v = xln[:, :].rearrange("p (i c) -> p i c", c=TK)
    vone_v = vone[:, :].rearrange("p (t c) -> p t c", c=129)
    wqT_v = wpool[:, 0 * D:1 * D].rearrange("p (j c) -> p j c", c=128)
    wkT_v = wpool[:, 1 * D:2 * D].rearrange("p (j c) -> p j c", c=128)
    wvT_v = wpool[:, 2 * D:3 * D].rearrange("p (j c) -> p j c", c=128)
    woT_v = wpool[:, 0:4 * D].rearrange("p (j c) -> p j c", c=512)
    wT_v = {"k": wkT_v, "q": wqT_v, "v": wvT_v}

    with blk_new() as blk:
        @blk.sync
        def _(s):
            ld_b.inc(s.dma_start(csub[:, :], io["csub"]), 16)
            ld_b.inc(s.dma_start(cadd[:, :], io["cadd"]), 16)
            ld_b.inc(s.dma_start(prevflag[:, :], io["prevflag"]), 16)
            ld_b.wait(s)
        @blk.vector
        def _(v):
            ctx.dve.inc(v.memset(vone_v[:, :, 128:129], 1.0))
        @blk.gpsimd
        def _(g):
            ctx.pool.inc(g.memset(maskA[:, :], 1.0))
            ctx.pool.inc(g.memset(maskB[:, :], 1.0))
            ctx.pool.wait(g)
            g.affine_select(maskA[:, :], maskA[:, :], [[1, 256]], ALU.is_ge, 0.0,
                            base=0, channel_multiplier=-1)
            ctx.pool.inc(g.affine_select(maskB[:, :], maskB[:, :], [[1, 256]],
                                         ALU.is_ge, 0.0, base=-128,
                                         channel_multiplier=-1))

    tilesC = [(io["x_prev"][t * 128:(t + 1) * 128, :], t * 128) for t in range(2)] + \
             [(io["x_own"][t * 128:(t + 1) * 128, :], 256 + t * 128) for t in range(NT)]
    emit_ln_pipe(ctx, blk_new, tilesC, xln, TK, BF16, ptr, 8, ident, ld_x,
                 scope="c_ln", xs=wraw)

    def proj_chunk(t, which, cs, bank, w):
        for j in range(NI):
            ins = t.matmul(pk[bank][:, 0:w], wT_v[which][:, j, :], xln_v[:, j, cs],
                           start=(j == 0), stop=(j == NI - 1))
        return ctx.pe.inc(ins)

    kq_chunks = [(0, 512), (512, 512), (1024, 256)]
    q_chunks = [(0, 512), (512, 512)]

    wnamesC = ["Wq_p", "Wk_p", "Wv"]
    ldw = [Trk(nc, "c_ldw%d" % m) for m in range(3)]
    wbf3 = [ctx.sb("c_wbf%d" % m, [128, D], BF16) for m in range(3)]
    with nc.named_scope("c_heads"), blk_new() as blk:
        marks = {
            "ptr_free": {0: ("dve", 0), 1: ("dve", 0)},
            "pk_free": {0: ("act", 0), 1: ("act", 0)},
            "rot_free": {0: ("act", 0), 1: ("act", 0)},
            "ctx_free": {i: ("dve", 0) for i in range(2)},
            "wb_free": {m: 0 for m in range(3)},
            "wT_ready": {m: 0 for m in range(3)},
            "kfm_free": 0, "qfm_free": 0, "pv_last": 0, "vfm_free": 0,
            "ctxtm_free": 0, "vone_dve": 0, "ropeT_dve": 0, "vfm_act": 0,
        }

        def wait_mark(eng, mk):
            nm, val = mk
            eng.wait_ge(getattr(ctx, nm).h, val)

        @blk.gpsimd
        def _(g):
            for m in range(3):
                ldw[m].inc(g.dma_start(wbf3[m][:, :], io[wnamesC[m]][0:128, :]), 16)

        for h in range(H):
            for m, dstv in enumerate([wqT_v, wkT_v, wvT_v]):
                for bi in range(2):
                    @blk.tensor
                    def _(t, m=m, bi=bi, h=h):
                        if bi == 0:
                            ldw[m].wait(t)
                        wait_mark(t, marks["ptr_free"][bi % 2])
                        for j in range(8):
                            jj = bi * 8 + j
                            ins = t.transpose(ptr[bi % 2][:, j * 128:(j + 1) * 128],
                                              wbf3[m][:, jj * 128:(jj + 1) * 128],
                                              ident[:, :])
                        mk = ctx.pe.inc(ins)
                        if bi == 1:
                            marks["wb_free"][m] = mk
                    @blk.vector
                    def _(v, m=m, bi=bi, dstv=dstv):
                        ctx.pe.wait(v)
                        mk = ctx.dve.inc(v.tensor_copy(
                            dstv[:, bi * 8:(bi + 1) * 8, 0:128],
                            ptr[bi % 2][:, :].rearrange("p (j c) -> p j c", c=128)))
                        marks["ptr_free"][bi % 2] = ("dve", mk)
                        if bi == 1:
                            marks["wT_ready"][m] = mk

            def proj_chunk_m(t, which, cs, bank, w):
                wait_mark(t, marks["pk_free"][bank])
                t.wait_ge(ctx.dve.h, marks["wT_ready"][{"q": 0, "k": 1, "v": 2}[which]])
                for j in range(NI):
                    ins = t.matmul(pk[bank][:, 0:w], wT_v[which][:, j, :], xln_v[:, j, cs],
                                   start=(j == 0), stop=(j == NI - 1))
                return ctx.pe.inc(ins)

            def rope_emit(dst_fm, dcol, tcol, w, bank, dst_gate, h=h):
                @blk.vector
                def _(v):
                    ctx.pe.wait(v)
                    ctx.dve.inc(v.tensor_mul(ropeT[:, 0:w], pk[bank][:, 0:w],
                                             cs_t[:, tcol:tcol + w]))
                    mk = ctx.dve.inc(v.tensor_mul(ropeT[:, 512:512 + w], pk[bank][:, 0:w],
                                                  sc_t[:, tcol:tcol + w]))
                    marks["ropeT_dve"] = mk
                @blk.tensor
                def _(t):
                    t.wait_ge(ctx.dve.h, marks["ropeT_dve"])
                    wait_mark(t, marks["rot_free"][bank])
                    t.matmul(pctx[bank][0:64, 0:w], csub[:, :], ropeT[:, 0:w],
                             start=True, stop=True)
                    ctx.pe.inc(t.matmul(pctx[bank][64:128, 0:w], cadd[:, :],
                                        ropeT[:, 512:512 + w],
                                        start=True, stop=True, tile_position=(0, 64)))
                @blk.scalar
                def _(sc):
                    ctx.pe.wait(sc)
                    if dst_gate is not None and h > 0:
                        sc.wait_ge(ctx.pe.h, marks[dst_gate])
                    mk = ctx.act.inc(sc.activation(dst_fm[:, dcol:dcol + w],
                                                   pctx[bank][:, 0:w], AF.Identity))
                    marks["pk_free"][bank] = ("act", mk)
                    marks["rot_free"][bank] = ("act", mk)

            for c, (col, w) in enumerate(kq_chunks):
                @blk.tensor
                def _(t, c=c, col=col, w=w):
                    proj_chunk_m(t, "k", slice(col, col + w), c % 2, w)
                rope_emit(kfm, col, col, w, c % 2, "kfm_free" if c == 0 else None)
            for c, (col, w) in enumerate(q_chunks):
                @blk.tensor
                def _(t, c=c, col=col, w=w):
                    proj_chunk_m(t, "q", slice(256 + col, 256 + col + w), c % 2, w)
                rope_emit(qfm, col, 256 + col, w, c % 2, "qfm_free" if c == 0 else None)
            for c, (col, w) in enumerate(kq_chunks):
                @blk.tensor
                def _(t, c=c, col=col, w=w):
                    proj_chunk_m(t, "v", slice(col, col + w), c % 2, w)
                @blk.scalar
                def _(sc, c=c, col=col, w=w, h=h):
                    ctx.pe.wait(sc)
                    if h > 0 and c == 0:
                        sc.wait_ge(ctx.pe.h, marks["vfm_free"])
                    mk = ctx.act.inc(sc.activation(vfm[:, col:col + w], pk[c % 2][:, 0:w],
                                                   AF.Identity))
                    marks["pk_free"][c % 2] = ("act", mk)
                    marks["vfm_act"] = mk
            for bi, j0 in enumerate(range(0, NTK, 8)):
                j1 = min(NTK, j0 + 8)
                @blk.tensor
                def _(t, j0=j0, j1=j1, bi=bi, h=h):
                    t.wait_ge(ctx.act.h, marks["vfm_act"])
                    wait_mark(t, marks["ptr_free"][bi % 2])
                    if h > 0 and bi == 0:
                        t.wait_ge(ctx.pe.h, marks["pv_last"])
                    for j in range(j0, j1):
                        ins = t.transpose(ptr[bi % 2][:, (j - j0) * 128:(j - j0 + 1) * 128],
                                          vfm[:, j * 128:(j + 1) * 128], ident[:, :])
                    mk = ctx.pe.inc(ins)
                    if bi == 1:
                        marks["vfm_free"] = mk
                @blk.vector
                def _(v, j0=j0, j1=j1, bi=bi):
                    ctx.pe.wait(v)
                    mk = ctx.dve.inc(v.tensor_copy(
                        vone_v[:, j0:j1, 0:128],
                        ptr[bi % 2][:, 0:(j1 - j0) * 128].rearrange("p (j c) -> p j c", c=128)))
                    marks["ptr_free"][bi % 2] = ("dve", mk)
                    marks["vone_dve"] = mk

            if h + 1 < H:
                @blk.gpsimd
                def _(g, h=h):
                    for m in range(3):
                        g.wait_ge(ctx.pe.h, marks["wb_free"][m])
                        ldw[m].inc(g.dma_start(
                            wbf3[m][:, :], io[wnamesC[m]][(h + 1) * 128:(h + 2) * 128, :]), 16)

            for lc in range(4):
                qs = slice(lc * 256, (lc + 1) * 256)
                s_marks, pv_marks = {}, {}
                p_ready = {}

                def emit_pv(t, j, lc):
                    if j == 0:
                        for sub in range(2):
                            wait_mark(t, marks["ctx_free"][sub])
                    for sub in range(2):
                        ins = t.matmul(pctx[sub][:, 0:129],
                                       ptil[:, j * 256 + sub * 128:j * 256 + (sub + 1) * 128],
                                       vone_v[:, 2 * lc + j, :],
                                       start=(j == 0), stop=(j == 3))
                    pv_marks[j] = ctx.pe.inc(ins)

                for j in range(4):
                    @blk.tensor
                    def _(t, j=j, lc=lc, qs=qs):
                        if j >= 2:
                            h2, n2 = p_ready[j - 2]
                            t.wait_ge(h2, n2)
                        else:
                            wait_mark(t, marks["pk_free"][j % 2])
                        if j == 0 and lc == 0:
                            t.wait_ge(ctx.dve.h, marks["vone_dve"])
                        ins = t.matmul(pk[j % 2][:, 0:256],
                                       kfm[:, (2 * lc + j) * 128:(2 * lc + j + 1) * 128],
                                       qfm[:, qs], start=True, stop=True)
                        s_marks[j] = ctx.pe.inc(ins)
                        if j >= 1:
                            h1, n1 = p_ready[j - 1]
                            t.wait_ge(h1, n1)
                            emit_pv(t, j - 1, lc)
                    @blk.scalar
                    def _(sc, j=j, lc=lc):
                        sc.wait_ge(ctx.pe.h, s_marks[j])
                        sc.wait_ge(ctx.pe.h, marks["pv_last"])
                        mk = ctx.act.inc(
                            sc.activation(ptil[:, j * 256:(j + 1) * 256],
                                          pk[j % 2][:, 0:256], AF.Exp, scale=ATTN_SCALE))
                        p_ready[j] = (ctx.act.h, mk)
                        if j >= 2:
                            marks["pk_free"][j % 2] = ("act", mk)
                    @blk.vector
                    def _(v, j=j, lc=lc):
                        h0, n0 = p_ready[j]
                        v.wait_ge(h0, n0)
                        pt = ptil[:, j * 256:(j + 1) * 256]
                        if j >= 2:
                            m = maskA if j == 2 else maskB
                            mk = ctx.dve.inc(v.tensor_mul(pt, pt, m[:, :]))
                            p_ready[j] = (ctx.dve.h, mk)
                        elif lc == 0:
                            mk = ctx.dve.inc(
                                v.tensor_scalar(pt, pt, prevflag[:, 0:1], None, ALU.mult))
                            p_ready[j] = (ctx.dve.h, mk)
                @blk.tensor
                def _(t, lc=lc):
                    h3, n3 = p_ready[3]
                    t.wait_ge(h3, n3)
                    emit_pv(t, 3, lc)
                marks["pv_last"] = ctx.pe.n
                @blk.vector
                def _(v, lc=lc):
                    v.wait_ge(ctx.pe.h, pv_marks[3])
                    v.wait_ge(ctx.pe.h, marks["ctxtm_free"])
                    ctx.dve.wait(v)
                    for sub in range(2):
                        ctx.dve.inc(v.reciprocal(recips[:, sub:sub + 1],
                                                 pctx[sub][:, 128:129]))
                    ctx.dve.wait(v)
                    for sub in range(2):
                        mk = ctx.dve.inc(v.tensor_scalar(ctx_tm[:, sub * 128:(sub + 1) * 128],
                                                         pctx[sub][:, 0:128],
                                                         recips[:, sub:sub + 1], None, ALU.mult))
                        marks["ctx_free"][sub] = ("dve", mk)
                @blk.tensor
                def _(t, lc=lc):
                    ctx.dve.wait(t)
                    wait_mark(t, marks["ptr_free"][0])
                    for sub in range(2):
                        ins = t.transpose(ptr[0][:, sub * 128:(sub + 1) * 128],
                                          ctx_tm[:, sub * 128:(sub + 1) * 128], ident[:, :])
                    marks["ctxtm_free"] = ctx.pe.inc(ins)
                @blk.vector
                def _(v, lc=lc, h=h):
                    ctx.pe.wait(v)
                    mk = ctx.dve.inc(v.tensor_copy(
                        ctx_fm[:, h * TOWN + lc * 256:h * TOWN + (lc + 1) * 256],
                        ptr[0][:, 0:256]))
                    marks["ptr_free"][0] = ("dve", mk)
            marks["kfm_free"] = marks["pv_last"]
            marks["qfm_free"] = marks["pv_last"]

    ctx_v = ctx_fm[:, :].rearrange("p (i c) -> p i c", c=TOWN)
    for oc in range(4):
        with nc.named_scope("c_oproj"), blk_new() as blk:
            for sub in range(4):
                @blk.gpsimd
                def _(g, sub=sub, oc=oc):
                    g.wait_ge(ctx.pe.h, ctx.pe.n)
                    ld_wo.inc(g.dma_start(wbf3[sub % 2][:, :],
                                          io["Wo"][(oc * 4 + sub) * 128:(oc * 4 + sub + 1) * 128, :]), 16)
                for bi in range(2):
                    @blk.tensor
                    def _(t, sub=sub, bi=bi):
                        ld_wo.wait(t)
                        t.wait_ge(ctx.dve.h, ctx.dve.n)
                        for j in range(8):
                            jj = bi * 8 + j
                            ins = t.transpose(ptr[bi % 2][:, j * 128:(j + 1) * 128],
                                              wbf3[sub % 2][:, jj * 128:(jj + 1) * 128],
                                              ident[:, :])
                        ctx.pe.inc(ins)
                    @blk.vector
                    def _(v, sub=sub, bi=bi):
                        ctx.pe.wait(v)
                        ctx.dve.inc(v.tensor_copy(
                            woT_v[:, bi * 8:(bi + 1) * 8, sub * 128:sub * 128 + 128],
                            ptr[bi % 2][:, :].rearrange("p (j c) -> p j c", c=128)))
        with blk_new() as blk:
            t_marks = {}
            for tt in range(NT):
                bank = tt % 4
                @blk.sync
                def _(s, tt=tt, oc=oc):
                    if tt >= 2:
                        s.wait_ge(ctx.dve.h, t_marks[tt - 2])
                    ld_xr[tt % 2].inc(s.dma_start(xres[tt % 2][:, :],
                                                  io["x_own"][tt * 128:(tt + 1) * 128,
                                                              oc * 512:(oc + 1) * 512]), 16)
                @blk.tensor
                def _(t, tt=tt, bank=bank):
                    if tt >= 4:
                        t.wait_ge(ctx.dve.h, t_marks[tt - 4])
                    for j in range(NI):
                        ins = t.matmul(pctx[bank][:, :], ctx_v[:, j, tt * 128:(tt + 1) * 128],
                                       woT_v[:, j, :], start=(j == 0), stop=(j == NI - 1))
                    ctx.pe.inc(ins)
                @blk.vector
                def _(v, tt=tt, bank=bank):
                    ctx.pe.wait(v)
                    ld_xr[tt % 2].wait(v)
                    v.wait_ge(ctx.ld2.h, ctx.ld2.n)
                    t_marks[tt] = ctx.dve.inc(v.tensor_add(ysb[tt % 2][:, :],
                                                           xres[tt % 2][:, :],
                                                           pctx[bank][:, :]))
                @blk.scalar
                def _(sc, tt=tt, oc=oc):
                    sc.wait_ge(ctx.dve.h, t_marks[tt])
                    ctx.ld2.inc(sc.dma_start(io["y"][tt * 128:(tt + 1) * 128,
                                                     oc * 512:(oc + 1) * 512],
                                             ysb[tt % 2][:, :]), 16)
    with blk_new() as blk:
        @blk.scalar
        def _(sc):
            ctx.ld2.wait(sc)


def emit_l0_ffn(ctx, blk_new, io, ncores=8):
    nc = ctx.nc
    DK = float(2 ** 20 + 1)
    pb = [ctx.ps("b_p%d" % i, [128, 512], F32) for i in range(8)]
    ld_x = Trk(nc, "b_ld_x")
    ld_w = Trk(nc, "b_ld_w")
    ld_wq = Trk(nc, "b_ld_wq")
    ld_xr = [Trk(nc, "b_ld_xr%d" % i) for i in range(2)]
    ld_cc = Trk(nc, "b_ld_cc")
    cc_sem = Trk(nc, "b_cc")

    identf = ctx.sb("b_identf", [128, 128], F32)
    ctx.eps_col = ctx.sb("b_eps", [128, 1], F32)
    ones_f = ctx.sb("b_onesf", [1, 128], F32)
    xln = ctx.sb("b_xln", [128, NI * TOWN], F32)
    xq = ctx.sb("b_xq", [128, NI * TOWN], BF16)
    scr = ctx.sb("b_scr", [128, D], F32)
    xln_tm = ctx.sb("b_xlntm", [128, D], F32)
    qs1 = ctx.sb("b_qs1", [128, D], F32)
    qs2 = ctx.sb("b_qs2", [128, D], F32)
    wstage = [ctx.sb("b_wst%d" % i, [128, D], BF16) for i in range(2)]
    wqf = ctx.sb("b_wqf", [128, D], F32)
    wqT = ctx.sb("b_wqT", [128, NI * 512], BF16)
    stats = ctx.sb("b_stats", [128, 8], F32)
    amx = ctx.sb("b_amx", [128, 2], F32)
    sca = ctx.sb("b_sca", [1, 8], F32)
    scol = ctx.sb("b_scol", [128, 4], F32)
    otmp = ctx.sb("b_otmp", [128, 4 * 512], BF16)
    xres = [ctx.sb("b_xres%d" % i, [128, 512], F32) for i in range(2)]
    ysb = [ctx.sb("b_ysb%d" % i, [128, 512], F32) for i in range(2)]

    xln_v = xln[:, :].rearrange("p (i c) -> p i c", c=TOWN)
    xq_v = xq[:, :].rearrange("p (i c) -> p i c", c=TOWN)
    wqT_v = wqT[:, :].rearrange("p (j c) -> p j c", c=512)

    def tr_f32(blk, src_f32_t, dst_v, j0_dst, col_off, ncol):
        for b4 in range(4):
            @blk.tensor
            def _(t, b4=b4):
                ctx.act.wait(t)
                ctx.dve.wait(t)
                for j in range(4):
                    jj = b4 * 4 + j
                    ins = t.transpose(pb[6 + b4 % 2][:, j * 128:(j + 1) * 128],
                                      src_f32_t[:, jj * 128:(jj + 1) * 128], identf[:, :])
                ctx.pe.inc(ins)
            @blk.vector
            def _(v, b4=b4):
                ctx.pe.wait(v)
                ctx.dve.inc(v.tensor_copy(
                    dst_v[:, j0_dst + b4 * 4:j0_dst + b4 * 4 + 4, col_off:col_off + 128],
                    pb[6 + b4 % 2][:, 0:512].rearrange("p (j c) -> p j c", c=128)))

    with blk_new() as blk:
        @blk.gpsimd
        def _(g):
            ctx.pool.inc(g.memset(identf[:, :], 1.0))
            g.memset(ctx.eps_col[:, :], 1e-5)
            g.memset(ones_f[:, :], 1.0)
            ctx.pool.wait(g)
            g.affine_select(identf[:, :], identf[:, :], [[1, 128]], ALU.is_equal, 0.0,
                            base=0, channel_multiplier=-1)
        @blk.vector
        def _(v):
            ctx.dve.inc(v.memset(amx[:, :], 0.0))

    def b_amax_cb(tt, blk, xtm_ap, marks):
        @blk.vector
        def _(v, tt=tt, xtm_ap=xtm_ap):
            v.wait_ge(ctx.act.h, marks["xlnw"][tt])
            ctx.dve.inc(v.tensor_reduce(stats[:, 0:1], xtm_ap[:, :], AX.X, ALU.max,
                                        apply_absolute_value=True))
            ctx.dve.wait(v)
            ctx.dve.inc(v.tensor_max(amx[:, 0:1], amx[:, 0:1], stats[:, 0:1]))

    tilesB = [(io["x_own"][t * 128:(t + 1) * 128, :], t * 128) for t in range(NT)]
    emit_ln_pipe(ctx, blk_new, tilesB, xln, TOWN, F32, [pb[6], pb[7]], 4, identf,
                 ld_x, post_cb=b_amax_cb, scope="b_ln", xs=[qs1, qs2],
                 tms=[scr, xln_tm])
    for t in range(NI):
        with blk_new() as blk:
            @blk.sync
            def _(s, t=t):
                s.wait_ge(ctx.dve.h, ctx.dve.n)
                ld_w.inc(s.dma_start(wstage[t % 2][:, :], io["w"][t * 128:(t + 1) * 128, :]), 16)
                ld_w.wait(s)
            @blk.vector
            def _(v, t=t):
                ld_w.wait(v)
                ctx.dve.inc(v.tensor_reduce(stats[:, 0:1], wstage[t % 2][:, :], AX.X,
                                            ALU.max, apply_absolute_value=True))
                ctx.dve.wait(v)
                ctx.dve.inc(v.tensor_max(amx[:, 1:2], amx[:, 1:2], stats[:, 0:1]))

    with blk_new() as blk:
        @blk.tensor
        def _(t):
            ctx.dve.wait(t)
            t.transpose(pb[6][0:1, 0:128], amx[:, 0:1], identf[:, :])
            ctx.pe.inc(t.transpose(pb[6][0:1, 128:256], amx[:, 1:2], identf[:, :]))
        @blk.vector
        def _(v):
            ctx.pe.wait(v)
            ctx.dve.inc(v.reduce_max(sca[0:1, 0:1], pb[6][0:1, 0:128], axis=AX.X))
            ctx.dve.inc(v.reduce_max(sca[0:1, 3:4], pb[6][0:1, 128:256], axis=AX.X))
    with blk_new() as blk:
        @blk.gpsimd
        def _(g):
            ctx.dve.wait(g)
            ld_cc.inc(g.dma_start(io["cc_in"], sca[0:1, 0:1]), 16)
            ld_cc.wait(g)
            cc_sem.inc(g.collective_compute("AllReduce", ALU.max,
                                            [list(range(ncores))],
                                            [io["cc_in"].opt()], [io["cc_out"].opt()]))
            cc_sem.wait(g)
            ld_cc.inc(g.dma_start(sca[0:1, 0:1], io["cc_out"]), 16)
            ld_cc.wait(g)
    with blk_new() as blk:
        @blk.vector
        def _(v):
            ld_cc.wait(v)
            def step(ins):
                ctx.dve.inc(ins)
                ctx.dve.wait(v)
            step(v.tensor_scalar(sca[0:1, 1:2], sca[0:1, 0:1], 1.0 / 448.0, 1e-12,
                                 ALU.mult, ALU.max))
            step(v.reciprocal(sca[0:1, 2:3], sca[0:1, 1:2]))
            step(v.tensor_scalar(sca[0:1, 4:5], sca[0:1, 3:4], 1.0 / 448.0, 1e-12,
                                 ALU.mult, ALU.max))
            step(v.reciprocal(sca[0:1, 5:6], sca[0:1, 4:5]))
            step(v.tensor_tensor(sca[0:1, 6:7], sca[0:1, 1:2], sca[0:1, 4:5], ALU.mult))
        @blk.tensor
        def _(t):
            ctx.dve.wait(t)
            t.matmul(pb[5][:, 0:1], ones_f[:, :], sca[0:1, 2:3], start=True, stop=True)
            t.matmul(pb[5][:, 1:2], ones_f[:, :], sca[0:1, 5:6], start=True, stop=True)
            ctx.pe.inc(t.matmul(pb[5][:, 2:3], ones_f[:, :], sca[0:1, 6:7], start=True, stop=True))
        @blk.vector
        def _(v):
            ctx.pe.wait(v)
            ctx.dve.inc(v.tensor_copy(scol[:, 0:3], pb[5][:, 0:3]))

    for i in range(NI):
        with blk_new() as blk:
            @blk.vector
            def _(v, i=i):
                ctx.act.wait(v)
                ctx.pool.wait(v)
                ctx.dve.inc(v.tensor_scalar(qs1[:, 0:TOWN], xln_v[:, i, :],
                                            scol[:, 0:1], None, ALU.mult))
            @blk.scalar
            def _(sc, i=i):
                ctx.dve.wait(sc)
                ctx.act.inc(sc.activation(qs2[:, 0:TOWN], qs1[:, 0:TOWN], AF.Identity,
                                          scale=DK))
            @blk.gpsimd
            def _(g, i=i):
                ctx.act.wait(g)
                ctx.pool.inc(g.tensor_sub(qs2[:, TOWN:2 * TOWN], qs2[:, 0:TOWN],
                                          qs1[:, 0:TOWN]))
            @blk.vector
            def _(v, i=i):
                ctx.pool.wait(v)
                ctx.dve.inc(v.tensor_sub(xq_v[:, i, :], qs2[:, 0:TOWN],
                                         qs2[:, TOWN:2 * TOWN]))

    for oc in range(4):
        with blk_new() as blk:
            for sub in range(4):
                @blk.sync
                def _(s, sub=sub, oc=oc):
                    s.wait_ge(ctx.dve.h, ctx.dve.n)
                    ld_wq.inc(s.dma_start(wstage[sub % 2][:, :],
                                          io["w"][(oc * 4 + sub) * 128:(oc * 4 + sub + 1) * 128, :]), 16)
                @blk.vector
                def _(v, sub=sub):
                    ld_wq.wait(v)
                    ctx.act.wait(v)
                    ctx.pool.wait(v)
                    ctx.dve.inc(v.tensor_scalar(qs1[:, :], wstage[sub % 2][:, :],
                                                scol[:, 1:2], None, ALU.mult))
                @blk.scalar
                def _(sc, sub=sub):
                    ctx.dve.wait(sc)
                    ctx.act.inc(sc.activation(qs2[:, :], qs1[:, :], AF.Identity, scale=DK))
                @blk.gpsimd
                def _(g, sub=sub):
                    ctx.act.wait(g)
                    ctx.pool.inc(g.tensor_sub(scr[:, :], qs2[:, :], qs1[:, :]))
                @blk.vector
                def _(v, sub=sub):
                    ctx.pool.wait(v)
                    ctx.dve.inc(v.tensor_sub(wqf[:, :], qs2[:, :], scr[:, :]))
                tr_f32(blk, wqf, wqT_v, 0, sub * 128, 512)
        with blk_new() as blk:
            t_marks = {}
            a_marks = {}
            for tt in range(NT):
                bank = tt % 4
                @blk.sync
                def _(s, tt=tt, oc=oc):
                    if tt >= 2:
                        s.wait_ge(ctx.dve.h, t_marks[tt - 2])
                    ld_xr[tt % 2].inc(s.dma_start(xres[tt % 2][:, :],
                                                  io["x_own"][tt * 128:(tt + 1) * 128,
                                                              oc * 512:(oc + 1) * 512]), 16)
                @blk.tensor
                def _(t, tt=tt, bank=bank):
                    if tt >= 4:
                        t.wait_ge(ctx.act.h, a_marks[tt - 4])
                    for j in range(NI):
                        ins = t.matmul(pb[bank][:, :], xq_v[:, j, tt * 128:(tt + 1) * 128],
                                       wqT_v[:, j, :], start=(j == 0), stop=(j == NI - 1))
                    ctx.pe.inc(ins)
                @blk.scalar
                def _(sc, tt=tt, bank=bank):
                    ctx.pe.wait(sc)
                    if tt >= 4:
                        sc.wait_ge(ctx.dve.h, t_marks[tt - 4])
                    a_marks[tt] = ctx.act.inc(
                        sc.activation(otmp[:, (tt % 4) * 512:(tt % 4 + 1) * 512],
                                      pb[bank][:, :], AF.Identity, scale=scol[:, 2:3]))
                @blk.vector
                def _(v, tt=tt):
                    v.wait_ge(ctx.act.h, a_marks[tt])
                    ld_xr[tt % 2].wait(v)
                    v.wait_ge(ctx.ld2.h, ctx.ld2.n)
                    t_marks[tt] = ctx.dve.inc(
                        v.tensor_add(ysb[tt % 2][:, :], xres[tt % 2][:, :],
                                     otmp[:, (tt % 4) * 512:(tt % 4 + 1) * 512]))
                @blk.scalar
                def _(sc, tt=tt, oc=oc):
                    sc.wait_ge(ctx.dve.h, t_marks[tt])
                    ctx.ld2.inc(sc.dma_start(io["y"][tt * 128:(tt + 1) * 128,
                                                     oc * 512:(oc + 1) * 512],
                                             ysb[tt % 2][:, :]), 16)
    with blk_new() as blk:
        @blk.scalar
        def _(sc):
            ctx.ld2.wait(sc)


def emit_l1_ffn(ctx, blk_new, io):
    nc = ctx.nc
    FS, FE, E = 1536, 768, 8
    pb = [ctx.ps("d_p%d" % i, [128, 512], F32) for i in range(8)]
    ld_x = Trk(nc, "d_ld_x")
    ld_r = Trk(nc, "d_ld_r")
    ld_w = [Trk(nc, "d_ld_w%d" % i) for i in range(4)]
    ld_xr = [Trk(nc, "d_ld_xr%d" % i) for i in range(2)]

    identf = ctx.sb("d_identf", [128, 128], F32)
    ctx.eps_col = ctx.sb("d_eps", [128, 1], F32)
    xln = ctx.sb("d_xln", [128, NI * TOWN], BF16)
    rtr = ctx.sb("d_rtr", [128, NI * 8], BF16)
    lgt = ctx.sb("d_lgt", [128, 8], F32)
    pstat = ctx.sb("d_pstat", [128, 8], F32)
    probs = ctx.sb("d_probs", [128, 8], F32)
    keep = ctx.sb("d_keep", [128, 8], F32)
    cw = ctx.sb("d_cw", [128, NT * 8], F32)
    gbuf = ctx.sb("d_gbuf", [128, NT * 512], F32)
    ubuf = ctx.sb("d_ubuf", [128, NT * 512], F32)
    xstage = ubuf[:, 0:D]
    scr = gbuf[:, 0:D]
    hsh = ctx.sb("d_hsh", [128, (FS // 128) * TOWN], BF16)
    hex_ = [ctx.sb("d_hex%d" % e, [128, (FE // 128) * TOWN], BF16) for e in range(E)]
    wslc = [ctx.sb("d_wslc%d" % i, [128, 512], BF16) for i in range(4)]
    xres = [ctx.sb("d_xres%d" % i, [128, 512], F32) for i in range(2)]
    ysb = [ctx.sb("d_ysb%d" % i, [128, 512], F32) for i in range(2)]

    xln_v = xln[:, :].rearrange("p (i c) -> p i c", c=TOWN)
    hsh_v = hsh[:, :].rearrange("p (f c) -> p f c", c=TOWN)
    hex_v = [h[:, :].rearrange("p (f c) -> p f c", c=TOWN) for h in hex_]

    with blk_new() as blk:
        @blk.gpsimd
        def _(g):
            ctx.pool.inc(g.memset(identf[:, :], 1.0))
            g.memset(ctx.eps_col[:, :], 1e-5)
            ctx.pool.wait(g)
            g.affine_select(identf[:, :], identf[:, :], [[1, 128]], ALU.is_equal, 0.0,
                            base=0, channel_multiplier=-1)
        @blk.gpsimd
        def _(g):
            ld_r.inc(g.dma_start(rtr[:, :].rearrange("p (i c) -> p i c", c=8),
                                 io["router"].rearrange("(i p) c -> p i c", p=128)), 16)
            ld_r.wait(g)

    tilesD = [(io["x_own"][t * 128:(t + 1) * 128, :], t * 128) for t in range(NT)]
    emit_ln_pipe(ctx, blk_new, tilesD, xln, TOWN, F32, [pb[6], pb[7]], 4, identf,
                 ld_x, scope="d_ln", xs=[ubuf[:, 0:D], ubuf[:, D:2 * D]],
                 tms=[gbuf[:, 0:D], gbuf[:, D:2 * D]])
    for t in range(NT):
        with blk_new() as blk:
            @blk.tensor
            def _(tt, t=t):
                ld_r.wait(tt)
                ctx.act.wait(tt)
                for j in range(NI):
                    ins = tt.matmul(pb[5][:, 0:8], xln_v[:, j, t * 128:(t + 1) * 128],
                                    rtr[:, j * 8:(j + 1) * 8],
                                    start=(j == 0), stop=(j == NI - 1))
                ctx.pe.inc(ins)
            @blk.vector
            def _(v, t=t):
                ctx.pe.wait(v)
                ctx.dve.wait(v)
                def step(ins):
                    ctx.dve.inc(ins)
                    ctx.dve.wait(v)
                step(v.tensor_copy(lgt[:, :], pb[5][:, 0:8]))
                step(v.reduce_max(pstat[:, 0:1], lgt[:, :], axis=AX.X))
                step(v.tensor_scalar(pstat[:, 1:2], pstat[:, 0:1], -1.0, None, ALU.mult))
            @blk.scalar
            def _(sc, t=t):
                ctx.dve.wait(sc)
                ctx.act.inc(sc.activation(probs[:, :], lgt[:, :], AF.Exp,
                                          bias=pstat[:, 1:2], accum_out=pstat[:, 2:3]))
            @blk.vector
            def _(v, t=t):
                ctx.act.wait(v)
                def step(ins):
                    ctx.dve.inc(ins)
                    ctx.dve.wait(v)
                step(v.reciprocal(pstat[:, 3:4], pstat[:, 2:3]))
                step(v.tensor_scalar(probs[:, :], probs[:, :], pstat[:, 3:4], None, ALU.mult))
                step(v.reduce_max(pstat[:, 4:5], probs[:, :], axis=AX.X))
                step(v.tensor_scalar(keep[:, :], probs[:, :], pstat[:, 4:5], None, ALU.is_ge))
                step(v.scalar_tensor_tensor(keep[:, :], keep[:, :], -1e30, probs[:, :],
                                            ALU.mult, ALU.add))
                step(v.reduce_max(pstat[:, 5:6], keep[:, :], axis=AX.X))
                step(v.tensor_scalar(keep[:, :], probs[:, :], pstat[:, 5:6], None, ALU.is_ge))
                step(v.tensor_mul(cw[:, t * 8:(t + 1) * 8], probs[:, :], keep[:, :]))

    def gated_block(w_gate_sl, w_up_sl, cw_col, hdst_v, f0_tiles, fcw, fc_off, tag):
        with nc.named_scope("d_gated"), blk_new() as blk:
            pem = {}
            for mat, (wsl, buf) in enumerate([(w_gate_sl, gbuf), (w_up_sl, ubuf)]):
                for d in range(NI):
                    si = mat * NI + d
                    @blk.gpsimd
                    def _(g, d=d, mat=mat, wsl=wsl, si=si):
                        par = si % 4
                        if si >= 4:
                            g.wait_ge(ctx.pe.h, pem[si - 4])
                        ld_w[par].inc(g.dma_start(wslc[par][:, 0:fcw], wsl(d)), 16)
                    @blk.tensor
                    def _(t, d=d, mat=mat, si=si):
                        par = si % 4
                        ld_w[par].wait(t)
                        if d == 0:
                            ctx.act.wait(t)
                        for tt in range(NT):
                            ins = t.matmul(pb[tt][:, 0:fcw], xln_v[:, d, tt * 128:(tt + 1) * 128],
                                           wslc[par][:, 0:fcw],
                                           start=(d == 0), stop=(d == NI - 1))
                        pem[si] = ctx.pe.inc(ins)
                @blk.scalar
                def _(sc, mat=mat, buf=buf):
                    ctx.pe.wait(sc)
                    ctx.dve.wait(sc)
                    for tt in range(NT):
                        ctx.act.inc(sc.activation(buf[:, tt * 512:tt * 512 + fcw],
                                                  pb[tt][:, 0:fcw],
                                                  AF.Silu if mat == 0 else AF.Identity))
            @blk.vector
            def _(v):
                ctx.act.wait(v)
                for tt in range(NT):
                    gsl = gbuf[:, tt * 512:tt * 512 + fcw]
                    usl = ubuf[:, tt * 512:tt * 512 + fcw]
                    if cw_col is None:
                        ctx.dve.inc(v.tensor_mul(gsl, gsl, usl))
                    else:
                        ctx.dve.inc(v.scalar_tensor_tensor(gsl, gsl, cw_col(tt), usl,
                                                           ALU.mult, ALU.mult))
            nfs = fcw // 128
            for tt in range(NT):
                @blk.tensor
                def _(t, tt=tt):
                    ctx.dve.wait(t)
                    if tt >= 2:
                        ctx.dve.wait(t)
                    for fsub in range(nfs):
                        ins = t.transpose(pb[6 + tt % 2][:, fsub * 128:(fsub + 1) * 128],
                                          gbuf[:, tt * 512 + fsub * 128:tt * 512 + (fsub + 1) * 128],
                                          identf[:, :])
                    ctx.pe.inc(ins)
                @blk.vector
                def _(v, tt=tt):
                    ctx.pe.wait(v)
                    ctx.dve.inc(v.tensor_copy(
                        hdst_v[:, f0_tiles:f0_tiles + nfs, tt * 128:(tt + 1) * 128],
                        pb[6 + tt % 2][:, 0:nfs * 128].rearrange("p (j c) -> p j c", c=128)))

    for fc in range(3):
        gated_block(lambda d, fc=fc: io["shared_gate"][d * 128:(d + 1) * 128,
                                                       fc * 512:(fc + 1) * 512],
                    lambda d, fc=fc: io["shared_up"][d * 128:(d + 1) * 128,
                                                     fc * 512:(fc + 1) * 512],
                    None, hsh_v, fc * 4, 512, fc, "sh%d" % fc)
    for e in range(E):
        for fc, fcw in [(0, 512), (1, 256)]:
            gated_block(lambda d, e=e, fc=fc, fcw=fcw: io["w_gate"][e][d * 128:(d + 1) * 128,
                                                                      fc * 512:fc * 512 + fcw],
                        lambda d, e=e, fc=fc, fcw=fcw: io["w_up"][e][d * 128:(d + 1) * 128,
                                                                    fc * 512:fc * 512 + fcw],
                        (lambda tt, e=e: cw[:, tt * 8 + e:tt * 8 + e + 1]),
                        hex_v[e], fc * 4, fcw, fc, "e%d_%d" % (e, fc))

    for oc in range(4):
        with nc.named_scope("d_down"), blk_new() as blk:
            slices = [("shared_down", None, f) for f in range(FS // 128)]
            for e in range(E):
                slices += [("w_down", e, f) for f in range(FE // 128)]
            pem = {}
            for si, (nm, e, f) in enumerate(slices):
                @blk.gpsimd
                def _(g, si=si, nm=nm, e=e, f=f, oc=oc):
                    par = si % 4
                    if si >= 4:
                        g.wait_ge(ctx.pe.h, pem[si - 4])
                    src = (io[nm][f * 128:(f + 1) * 128, oc * 512:(oc + 1) * 512]
                           if e is None else
                           io[nm][e][f * 128:(f + 1) * 128, oc * 512:(oc + 1) * 512])
                    ld_w[par].inc(g.dma_start(wslc[par][:, :], src), 16)
                @blk.tensor
                def _(t, si=si, e=e, f=f):
                    par = si % 4
                    ld_w[par].wait(t)
                    if si == 0:
                        ctx.dve.wait(t)
                    hsrc = hsh_v if e is None else hex_v[e]
                    for tt in range(NT):
                        ins = t.matmul(pb[tt][:, :], hsrc[:, f, tt * 128:(tt + 1) * 128],
                                       wslc[par][:, :],
                                       start=(si == 0), stop=(si == len(slices) - 1))
                    pem[si] = ctx.pe.inc(ins)
            pe_done = ctx.pe.n
            t_marks = {}
            for tt in range(NT):
                @blk.sync
                def _(s, tt=tt, oc=oc):
                    if tt >= 2:
                        s.wait_ge(ctx.dve.h, t_marks[tt - 2])
                    ld_xr[tt % 2].inc(s.dma_start(xres[tt % 2][:, :],
                                                  io["x_own"][tt * 128:(tt + 1) * 128,
                                                              oc * 512:(oc + 1) * 512]), 16)
                @blk.vector
                def _(v, tt=tt, oc=oc):
                    v.wait_ge(ctx.pe.h, pe_done)
                    ld_xr[tt % 2].wait(v)
                    v.wait_ge(ctx.ld2.h, ctx.ld2.n)
                    t_marks[tt] = ctx.dve.inc(v.tensor_add(ysb[tt % 2][:, :],
                                                           xres[tt % 2][:, :],
                                                           pb[tt][:, :]))
                @blk.scalar
                def _(sc, tt=tt, oc=oc):
                    sc.wait_ge(ctx.dve.h, t_marks[tt])
                    ctx.ld2.inc(sc.dma_start(io["y"][tt * 128:(tt + 1) * 128,
                                                     oc * 512:(oc + 1) * 512],
                                             ysb[tt % 2][:, :]), 16)
    with blk_new() as blk:
        @blk.scalar
        def _(sc):
            ctx.ld2.wait(sc)


def emit_ln_core(ctx, blk, src_f32, scr_f32, out_tm, ld, out_dt):
    if not hasattr(ctx, "_ln_st2"):
        ctx._ln_st2 = ctx.sb("ln_stats2", [128, 8], F32)
    st = ctx._ln_st2
    rs, mean, sumsq, var, rstd, nb = (st[:, 0:1], st[:, 1:2], st[:, 2:3], st[:, 3:4],
                                      st[:, 4:5], st[:, 5:6])

    @blk.scalar
    def _(sc):
        ld.wait(sc)
        ctx.dve.wait(sc)
        ctx.act.wait(sc)
        ctx.act.inc(sc.activation(scr_f32[:, :], src_f32[:, :], AF.Square,
                                  accum_out=sumsq))

    @blk.vector
    def _(v):
        ld.wait(v)
        ctx.dve.inc(v.reduce_sum(rs, src_f32[:, :], axis=AX.X))
        ctx.dve.wait(v)
        ctx.dve.inc(v.tensor_scalar(mean[:, :], rs, 1.0 / D, None, ALU.mult))
        ctx.act.wait(v)
        ctx.dve.wait(v)
        ctx.dve.inc(v.tensor_tensor(var[:, :], mean, mean, ALU.mult))
        ctx.dve.wait(v)
        ctx.dve.inc(v.scalar_tensor_tensor(var[:, :], sumsq, 1.0 / D, var,
                                           ALU.mult, ALU.subtract))

    @blk.scalar
    def _(sc):
        ctx.dve.wait(sc)
        ctx.act.inc(sc.activation(var[:, :], var[:, :], AF.Sqrt, bias=ctx.eps_col[:, :]))

    @blk.vector
    def _(v):
        ctx.act.wait(v)
        ctx.dve.inc(v.reciprocal(rstd[:, :], var[:, :]))
        ctx.dve.wait(v)
        ctx.dve.inc(v.scalar_tensor_tensor(nb[:, :], mean, -1.0, rstd, ALU.mult, ALU.mult))

    @blk.scalar
    def _(sc):
        ctx.dve.wait(sc)
        ctx.act.inc(sc.activation(out_tm[:, :], src_f32[:, :], AF.Identity,
                                  bias=nb, scale=rstd))



def _theta_split():
    th64 = 10000.0 ** (-np.arange(0, 128, 2, dtype=np.float64) / 128.0)
    th64 = np.concatenate([th64, th64])
    hi = th64.astype(np.float32)
    lo = (th64 - hi.astype(np.float64)).astype(np.float32)
    return hi.reshape(1, 128), lo.reshape(1, 128)


def _csub():
    c = np.zeros((128, 64), dtype=ml_dtypes.bfloat16)
    for j in range(64):
        c[j, j] = 1.0
        c[64 + j, j] = -1.0
    return c


def _bias_cols(b):
    return np.ascontiguousarray(np.asarray(b, dtype=np.float32).reshape(16, 128).T)


def _perm_rows(w):
    w = np.asarray(w)
    v = w.reshape(16, 64, 2, *w.shape[1:])
    return np.ascontiguousarray(v.transpose(0, 2, 1, *range(3, v.ndim)).reshape(w.shape))


def build_l0_attn():
    nc = bass.Bass("TRN2", target_bir_lowering=False)
    io = {}
    io["x_own"] = nc.dram_tensor("x_own", [TOWN, D], F32, kind="ExternalInput").ap()
    io["x_oth"] = nc.dram_tensor("x_oth", [TOWN, D], F32, kind="ExternalInput").ap()
    for nm in ["wq_p", "wk_p", "wv", "wo"]:
        io[nm] = nc.dram_tensor(nm, [D, D], F32, kind="ExternalInput").ap()
    for nm in ["bq_c", "bk_c", "bv_c"]:
        io[nm] = nc.dram_tensor(nm, [128, H], F32, kind="ExternalInput").ap()
    io["bo"] = nc.dram_tensor("bo", [D], F32, kind="ExternalInput").ap()
    io["pos_all"] = nc.dram_tensor("pos_all", [1, TSEQ], F32, kind="ExternalInput").ap()
    io["th_hi"] = nc.dram_tensor("th_hi", [1, 128], F32, kind="ExternalInput").ap()
    io["th_lo"] = nc.dram_tensor("th_lo", [1, 128], F32, kind="ExternalInput").ap()
    io["csub"] = nc.dram_tensor("csub", [128, 64], BF16, kind="ExternalInput").ap()
    io["y"] = nc.dram_tensor("y", [TOWN, D], F32, kind="ExternalOutput").ap()
    ctx = Ctx(nc)
    emit_l0_attn(ctx, nc.Block, io)
    return nc


def run_l0_attn(x_full, wq, bq, wk, bk, wv, bv, wo, bo):
    th_hi, th_lo = _theta_split()
    wq_p, wk_p = _perm_rows(wq), _perm_rows(wk)
    bq_c = _bias_cols(_perm_rows(bq))
    bk_c = _bias_cols(_perm_rows(bk))
    bv_c = _bias_cols(np.asarray(bv))
    nc = build_l0_attn()
    in_maps = []
    for c in range(8):
        own = x_full[c * 1024:(c + 1) * 1024]
        oth = x_full[(c ^ 1) * 1024:((c ^ 1) + 1) * 1024]
        pos_own = (c % 2) * 1024 + np.arange(1024, dtype=np.float32)
        pos_oth = ((c ^ 1) % 2) * 1024 + np.arange(1024, dtype=np.float32)
        pos_all = np.concatenate([pos_own, pos_oth]).reshape(1, 2048)
        in_maps.append(dict(x_own=own, x_oth=oth, wq_p=wq_p, wk_p=wk_p, wv=wv, wo=wo,
                            bq_c=bq_c, bk_c=bk_c, bv_c=bv_c, bo=bo, pos_all=pos_all,
                            th_hi=th_hi, th_lo=th_lo, csub=_csub()))
    res = run_bass_kernel_spmd(nc, in_maps, core_ids=list(range(8)))
    return np.concatenate([r["y"] for r in res.results], axis=0)


def build_l0_ffn(ncores=8):
    nc = bass.Bass("TRN2", target_bir_lowering=False)
    io = {}
    io["x_own"] = nc.dram_tensor("x_own", [TOWN, D], F32, kind="ExternalInput").ap()
    io["w"] = nc.dram_tensor("w", [D, D], BF16, kind="ExternalInput").ap()
    io["y"] = nc.dram_tensor("y", [TOWN, D], F32, kind="ExternalOutput").ap()
    io["cc_in"] = nc.dram_tensor("cc_in", [1, 1], F32).ap()
    io["cc_out"] = nc.dram_tensor("cc_out", [1, 1], F32).ap()
    ctx = Ctx(nc)
    emit_l0_ffn(ctx, nc.Block, io, ncores=ncores)
    return nc


def run_l0_ffn(x_full, w_bf16):
    nc = build_l0_ffn(8)
    in_maps = [dict(x_own=x_full[c * 1024:(c + 1) * 1024], w=w_bf16) for c in range(8)]
    res = run_bass_kernel_spmd(nc, in_maps, core_ids=list(range(8)))
    return np.concatenate([r["y"] for r in res.results], axis=0)


def _cadd():
    c = np.zeros((128, 64), dtype=ml_dtypes.bfloat16)
    for j in range(64):
        c[j, j] = 1.0
        c[64 + j, j] = 1.0
    return c


def build_l1_attn():
    nc = bass.Bass("TRN2", target_bir_lowering=False)
    io = {}
    io["x_own"] = nc.dram_tensor("x_own", [TOWN, D], F32, kind="ExternalInput").ap()
    io["x_prev"] = nc.dram_tensor("x_prev", [256, D], F32, kind="ExternalInput").ap()
    for nm in ["Wq_p", "Wk_p", "Wv", "Wo"]:
        io[nm] = nc.dram_tensor(nm, [D, D], F32, kind="ExternalInput").ap()
    io["pos_all"] = nc.dram_tensor("pos_all", [1, 1536], F32, kind="ExternalInput").ap()
    io["th_hi"] = nc.dram_tensor("th_hi", [1, 128], F32, kind="ExternalInput").ap()
    io["th_lo"] = nc.dram_tensor("th_lo", [1, 128], F32, kind="ExternalInput").ap()
    io["csub"] = nc.dram_tensor("csub", [128, 64], BF16, kind="ExternalInput").ap()
    io["cadd"] = nc.dram_tensor("cadd", [128, 64], BF16, kind="ExternalInput").ap()
    io["prevflag"] = nc.dram_tensor("prevflag", [128, 1], F32, kind="ExternalInput").ap()
    io["y"] = nc.dram_tensor("y", [TOWN, D], F32, kind="ExternalOutput").ap()
    ctx = Ctx(nc)
    emit_l1_attn(ctx, nc.Block, io)
    return nc


def _l1_in_map(x_full, c, Wq_p, Wk_p, Wv, Wo, th_hi, th_lo):
    own = x_full[c * 1024:(c + 1) * 1024]
    if c % 2 == 1:
        x_prev = x_full[c * 1024 - 256:c * 1024]
        pos_prev = 768.0 + np.arange(256, dtype=np.float32)
        flag = np.ones((128, 1), np.float32)
    else:
        x_prev = np.zeros((256, D), np.float32)
        pos_prev = np.arange(256, dtype=np.float32)
        flag = np.zeros((128, 1), np.float32)
    pos_own = (c % 2) * 1024 + np.arange(1024, dtype=np.float32)
    pos_all = np.concatenate([pos_prev, pos_own,
                              np.zeros(256, np.float32)]).reshape(1, 1536)
    return dict(x_own=own, x_prev=x_prev, Wq_p=Wq_p, Wk_p=Wk_p, Wv=Wv, Wo=Wo,
                pos_all=pos_all, th_hi=th_hi, th_lo=th_lo, csub=_csub(),
                cadd=_cadd(), prevflag=flag)


def run_l1_attn(x_full, Wq, Wk, Wv, Wo):
    th_hi, th_lo = _theta_split()
    Wq_p, Wk_p = _perm_rows(Wq), _perm_rows(Wk)
    nc = build_l1_attn()
    in_maps = [_l1_in_map(x_full, c, Wq_p, Wk_p, Wv, Wo, th_hi, th_lo) for c in range(8)]
    res = run_bass_kernel_spmd(nc, in_maps, core_ids=list(range(8)))
    return np.concatenate([r["y"] for r in res.results], axis=0)


def build_l1_ffn():
    nc = bass.Bass("TRN2", target_bir_lowering=False)
    io = {}
    io["x_own"] = nc.dram_tensor("x_own", [TOWN, D], F32, kind="ExternalInput").ap()
    io["router"] = nc.dram_tensor("router", [D, 8], F32, kind="ExternalInput").ap()
    io["shared_gate"] = nc.dram_tensor("shared_gate", [D, 1536], F32, kind="ExternalInput").ap()
    io["shared_up"] = nc.dram_tensor("shared_up", [D, 1536], F32, kind="ExternalInput").ap()
    io["shared_down"] = nc.dram_tensor("shared_down", [1536, D], F32, kind="ExternalInput").ap()
    io["w_gate"] = nc.dram_tensor("w_gate", [8, D, 768], F32, kind="ExternalInput").ap()
    io["w_up"] = nc.dram_tensor("w_up", [8, D, 768], F32, kind="ExternalInput").ap()
    io["w_down"] = nc.dram_tensor("w_down", [8, 768, D], F32, kind="ExternalInput").ap()
    io["y"] = nc.dram_tensor("y", [TOWN, D], F32, kind="ExternalOutput").ap()
    ctx = Ctx(nc)
    emit_l1_ffn(ctx, nc.Block, io)
    return nc


def run_l1_ffn(x_full, router, sg, su, sd, wg, wu, wd):
    nc = build_l1_ffn()
    in_maps = [dict(x_own=x_full[c * 1024:(c + 1) * 1024], router=router,
                    shared_gate=sg, shared_up=su, shared_down=sd,
                    w_gate=wg, w_up=wu, w_down=wd) for c in range(8)]
    res = run_bass_kernel_spmd(nc, in_maps, core_ids=list(range(8)))
    return np.concatenate([r["y"] for r in res.results], axis=0)


def build_fused(ncores=8):
    nc = bass.Bass("TRN2", target_bir_lowering=False)
    EI = lambda nm, sh, dt=F32: nc.dram_tensor(nm, sh, dt, kind="ExternalInput").ap()
    x_own = EI("x_own", [TOWN, D])
    x_oth = EI("x_oth", [TOWN, D])
    y1 = nc.dram_tensor("y1", [TOWN, D], F32).ap()
    y2 = nc.dram_tensor("y2", [TOWN, D], F32).ap()
    y3 = nc.dram_tensor("y3", [TOWN, D], F32).ap()
    prevg = nc.dram_tensor("prevg", [512, D], F32).ap()
    yout = nc.dram_tensor("y", [TOWN, D], F32, kind="ExternalOutput").ap()
    cc_in = nc.dram_tensor("cc_in", [1, 1], F32).ap()
    cc_out = nc.dram_tensor("cc_out", [1, 1], F32).ap()

    ioA = dict(x_own=x_own, x_oth=x_oth, y=y1,
               wq_p=EI("wq_p", [D, D]), wk_p=EI("wk_p", [D, D]),
               wv=EI("wv", [D, D]), wo=EI("wo", [D, D]),
               bq_c=EI("bq_c", [128, H]), bk_c=EI("bk_c", [128, H]),
               bv_c=EI("bv_c", [128, H]), bo=EI("bo", [D]),
               pos_all=EI("pos_allA", [1, TSEQ]),
               th_hi=EI("th_hiA", [1, 128]), th_lo=EI("th_loA", [1, 128]),
               csub=EI("csubA", [128, 64], BF16))
    ioB = dict(x_own=y1, w=EI("w8", [D, D], BF16), y=y2, cc_in=cc_in, cc_out=cc_out)
    ioC = dict(x_own=y2, x_prev=prevg[0:256, :], y=y3,
               Wq_p=EI("Wq_p", [D, D]), Wk_p=EI("Wk_p", [D, D]),
               Wv=EI("Wv1", [D, D]), Wo=EI("Wo1", [D, D]),
               pos_all=EI("pos_allC", [1, 1536]),
               th_hi=EI("th_hiC", [1, 128]), th_lo=EI("th_loC", [1, 128]),
               csub=EI("csubC", [128, 64], BF16), cadd=EI("caddC", [128, 64], BF16),
               prevflag=EI("prevflag", [128, 1]))
    ioD = dict(x_own=y3, y=yout, router=EI("router", [D, 8]),
               shared_gate=EI("shared_gate", [D, 1536]),
               shared_up=EI("shared_up", [D, 1536]),
               shared_down=EI("shared_down", [1536, D]),
               w_gate=EI("w_gate", [8, D, 768]), w_up=EI("w_up", [8, D, 768]),
               w_down=EI("w_down", [8, 768, D]))

    ctx = Ctx(nc)

    blkno = [0]

    def run_block(emit, io, **kw):
        ctx.prefix = "blk%d_" % blkno[0]
        blkno[0] += 1
        with ExitStack() as st:
            ctx.stack = st
            with nc.named_scope(ctx.prefix + emit.__name__):
                emit(ctx, nc.Block, io, **kw)
        ctx.stack = None
        ctx.prefix = ""
        for attr in ("_ln_st", "_ln_st2", "eps_col"):
            if hasattr(ctx, attr):
                delattr(ctx, attr)

    run_block(emit_l0_attn, ioA)
    run_block(emit_l0_ffn, ioB, ncores=ncores)
    ccg = Trk(nc, "ccg")
    groups = [[2 * i, 2 * i + 1] for i in range(ncores // 2)]
    with nc.Block() as blk:
        @blk.gpsimd
        def _(g):
            g.wait_ge(ctx.ld2.h, ctx.ld2.n)
            ccg.inc(g.collective_compute("AllGather", ALU.bypass, groups,
                                         [y2[768:1024, :].opt()], [prevg.opt()]))
            ccg.wait(g)
    run_block(emit_l1_attn, ioC)
    run_block(emit_l1_ffn, ioD)
    return nc


def kernel(L0_attn_x, L0_attn_wq, L0_attn_bq, L0_attn_wk, L0_attn_bk, L0_attn_wv,
           L0_attn_bv, L0_attn_wo, L0_attn_bo, L0_ffn_weight, L1_attn_Wq, L1_attn_Wk,
           L1_attn_Wv, L1_attn_Wo, L1_ffn_router, L1_ffn_shared_gate, L1_ffn_shared_up,
           L1_ffn_shared_down, L1_ffn_w_gate, L1_ffn_w_up, L1_ffn_w_down):
    x = np.ascontiguousarray(np.asarray(L0_attn_x, dtype=np.float32).reshape(8192, 2048))
    th_hi, th_lo = _theta_split()
    wq_p, wk_p = _perm_rows(L0_attn_wq), _perm_rows(L0_attn_wk)
    bq_c = _bias_cols(_perm_rows(L0_attn_bq))
    bk_c = _bias_cols(_perm_rows(L0_attn_bk))
    bv_c = _bias_cols(L0_attn_bv)
    Wq_p, Wk_p = _perm_rows(L1_attn_Wq), _perm_rows(L1_attn_Wk)
    nc = build_fused(8)
    in_maps = []
    for c in range(8):
        pos_own = (c % 2) * 1024 + np.arange(1024, dtype=np.float32)
        pos_oth = ((c ^ 1) % 2) * 1024 + np.arange(1024, dtype=np.float32)
        posA = np.concatenate([pos_own, pos_oth]).reshape(1, 2048)
        if c % 2 == 1:
            pos_prev = 768.0 + np.arange(256, dtype=np.float32)
            flag = np.ones((128, 1), np.float32)
        else:
            pos_prev = np.arange(256, dtype=np.float32)
            flag = np.zeros((128, 1), np.float32)
        posC = np.concatenate([pos_prev, pos_own,
                               np.zeros(256, np.float32)]).reshape(1, 1536)
        in_maps.append(dict(
            x_own=x[c * 1024:(c + 1) * 1024],
            x_oth=x[(c ^ 1) * 1024:((c ^ 1) + 1) * 1024],
            wq_p=wq_p, wk_p=wk_p, wv=np.asarray(L0_attn_wv), wo=np.asarray(L0_attn_wo),
            bq_c=bq_c, bk_c=bk_c, bv_c=bv_c, bo=np.asarray(L0_attn_bo),
            pos_allA=posA, th_hiA=th_hi, th_loA=th_lo, csubA=_csub(),
            w8=np.asarray(L0_ffn_weight),
            Wq_p=Wq_p, Wk_p=Wk_p, Wv1=np.asarray(L1_attn_Wv), Wo1=np.asarray(L1_attn_Wo),
            pos_allC=posC, th_hiC=th_hi, th_loC=th_lo, csubC=_csub(), caddC=_cadd(),
            prevflag=flag,
            router=np.asarray(L1_ffn_router),
            shared_gate=np.asarray(L1_ffn_shared_gate),
            shared_up=np.asarray(L1_ffn_shared_up),
            shared_down=np.asarray(L1_ffn_shared_down),
            w_gate=np.asarray(L1_ffn_w_gate), w_up=np.asarray(L1_ffn_w_up),
            w_down=np.asarray(L1_ffn_w_down)))
    res = run_bass_kernel_spmd(nc, in_maps, core_ids=list(range(8)))
    y = np.concatenate([r["y"] for r in res.results], axis=0)
    return y.reshape(4, 2048, 2048)


# version 07
# speedup vs baseline: 1.5377x; 1.0739x over previous
import math
from contextlib import ExitStack
import numpy as np
import ml_dtypes
import concourse.bass as bass
import concourse.mybir as mybir
from concourse.bass_utils import run_bass_kernel_spmd

F32 = mybir.dt.float32
BF16 = mybir.dt.bfloat16
I32 = mybir.dt.int32
AF = mybir.ActivationFunctionType
ALU = mybir.AluOpType
AX = mybir.AxisListType

D = 2048
H = 16
TOWN = 1024
TSEQ = 2048
NT = TOWN // 128
NTS = TSEQ // 128
NI = D // 128

TWO_PI = 6.283185307179586
INV_2PI = 1.0 / TWO_PI
C_HI = 6.28125
C_LO = 0.00193530717958647692
PI_C = float(np.nextafter(np.float32(np.pi), np.float32(0.0)))
HALF_PI = float(np.float32(np.pi / 2))
ATTN_SCALE = 1.0 / math.sqrt(128.0)


_TRKN = [0]


class Trk:
    def __init__(self, nc, name):
        _TRKN[0] += 1
        self.h = nc.alloc_semaphore("%s_%d" % (name, _TRKN[0]))
        self.n = 0

    def inc(self, instr, k=1):
        instr.then_inc(self.h, k)
        self.n += k
        return self.n

    def wait(self, eng, val=None):
        eng.wait_ge(self.h, self.n if val is None else val)


class Ctx:
    def __init__(self, nc):
        self.nc = nc
        self.pe = Trk(nc, "t_pe")
        self.act = Trk(nc, "t_act")
        self.dve = Trk(nc, "t_dve")
        self.pool = Trk(nc, "t_pool")
        self.ld = Trk(nc, "t_ld")
        self.ld2 = Trk(nc, "t_ld2")
        self.ld3 = Trk(nc, "t_ld3")

        self.stack = None
        self.prefix = ""

    def sb(self, name, shape, dt):
        name = self.prefix + name
        if self.stack is not None:
            return self.stack.enter_context(self.nc.sbuf_tensor(name, shape, dt))
        return self.nc.alloc_sbuf_tensor(name, shape, dt)

    def ps(self, name, shape, dt):
        name = self.prefix + name
        if self.stack is not None:
            return self.stack.enter_context(self.nc.psum_tensor(name, shape, dt))
        return self.nc.alloc_psum_tensor(name, shape, dt)


def emit_common_consts(ctx, blk_new):
    nc = ctx.nc
    ident = ctx.sb("c_ident", [128, 128], BF16)
    ones_row = ctx.sb("c_ones_row", [1, 512], BF16)
    ctx.eps_col = ctx.sb("c_eps", [128, 1], F32)
    with blk_new() as blk:
        @blk.gpsimd
        def _(g):
            ctx.pool.inc(g.memset(ident[:, :], 1.0))
            g.memset(ones_row[:, :], 1.0)
            g.memset(ctx.eps_col[:, :], 1e-5)
            ctx.pool.wait(g)
            g.affine_select(ident[:, :], ident[:, :], [[1, 128]], ALU.is_equal, 0.0,
                            base=0, channel_multiplier=-1)
    return ident, ones_row


def emit_rope_tables(ctx, blk_new, pfr, scrA, scrB, pos_ap, theta_hi_ap, theta_lo_ap,
                     ncols, name, swap=False):
    nc = ctx.nc
    cs_t = ctx.sb(name + "_cs", [128, ncols], F32)
    s_pos = scrB[0:1, 0:ncols]
    s_th = ctx.sb(name + "_th", [1, 128], F32)
    s_tl = ctx.sb(name + "_tl", [1, 128], F32)
    s_u = scrA[:, 0:512]
    s_n = ctx.sb(name + "_n", [128, 512], I32)
    s_nf = scrA[:, 512:1024]
    s_m = scrA[:, 1024:1536]
    s_w = scrA[:, 1536:2048]
    cos_half = slice(64, 128) if swap else slice(0, 64)
    ldm = Trk(ctx.nc, name + "_ld")
    with blk_new() as blk:
        @blk.sync
        def _(s):
            ldm.inc(s.dma_start(s_pos, pos_ap), 16)
            ldm.inc(s.dma_start(s_th[:, :], theta_hi_ap), 16)
            ldm.inc(s.dma_start(s_tl[:, :], theta_lo_ap), 16)
            ldm.wait(s)
    nch = ncols // 512
    for c in range(nch):
        cs = slice(c * 512, (c + 1) * 512)
        with blk_new() as blk:
            @blk.tensor
            def _(t):
                ldm.wait(t)
                t.matmul(pfr[:, :], s_th[:, :], scrB[0:1, c * 512:(c + 1) * 512],
                         start=True, stop=False)
                ctx.pe.inc(t.matmul(pfr[:, :], s_tl[:, :], scrB[0:1, c * 512:(c + 1) * 512],
                                    start=False, stop=True))
            @blk.vector
            def _(v):
                ctx.pe.wait(v)
                def step(ins):
                    ctx.dve.inc(ins)
                    ctx.dve.wait(v)
                step(v.tensor_scalar(s_u, pfr[:, :], INV_2PI, None, ALU.mult))
                step(v.tensor_copy(s_n[:, :], s_u))
                step(v.tensor_copy(s_nf, s_n[:, :]))
                step(v.scalar_tensor_tensor(s_m, s_nf, -C_HI, pfr[:, :], ALU.mult, ALU.add))
                step(v.scalar_tensor_tensor(s_m, s_nf, -C_LO, s_m, ALU.mult, ALU.add))
                step(v.tensor_scalar(s_w, s_m, -PI_C, TWO_PI, ALU.is_lt, ALU.mult))
                step(v.tensor_add(s_m, s_m, s_w))
                step(v.tensor_scalar(s_w, s_m, PI_C, TWO_PI, ALU.is_gt, ALU.mult))
                step(v.tensor_sub(s_m, s_m, s_w))
                step(v.tensor_scalar(s_m, s_m, -PI_C, PI_C, ALU.max, ALU.min))
                mh = s_m[cos_half, :]
                wh = s_w[cos_half, :]
                step(v.tensor_scalar(mh, mh, HALF_PI, None, ALU.add))
                step(v.tensor_scalar(wh, mh, PI_C, TWO_PI, ALU.is_gt, ALU.mult))
                step(v.tensor_sub(mh, mh, wh))
                step(v.tensor_scalar(mh, mh, -PI_C, PI_C, ALU.max, ALU.min))
            @blk.scalar
            def _(sc):
                ctx.dve.wait(sc)
                ctx.act.inc(sc.activation(cs_t[:, cs], s_m, AF.Sin))
    return cs_t


def emit_ln_pipe(ctx, blk_new, tiles, xln_fm, fm_cols, tm_dt, banks, bank_cap,
                 ident, ld_unused, post_cb=None, scope=None, xs=None, tms=None):
    nc = ctx.nc
    xstage = xs if xs is not None else [ctx.sb("lnp_xs%d" % i, [128, D], F32)
                                        for i in range(2)]
    xtm = tms if tms is not None else [ctx.sb("lnp_tm%d" % i, [128, D], tm_dt)
                                       for i in range(2)]
    st = ctx.sb("lnp_st", [128, 16], F32)
    ldp = [Trk(ctx.nc, "lnp_ld%d" % i) for i in range(2)]
    nbatch = D // 128 // bank_cap
    xv = xln_fm[:, :].rearrange("p (i c) -> p i c", c=fm_cols)
    import contextlib
    scope_cm = nc.named_scope(scope) if scope else contextlib.nullcontext()
    with scope_cm, blk_new() as blk:
        marks = {"xlnw": {}, "red": {}, "tr": {}, "bank": {0: 0, 1: 0}, "ev": {}}
        for tt, (src, tcol0) in enumerate(tiles):
            par = tt % 2
            s0 = 8 * par
            rs, mean, sumsq, var, rstd, nb = (st[:, s0:s0 + 1], st[:, s0 + 1:s0 + 2],
                                              st[:, s0 + 2:s0 + 3], st[:, s0 + 3:s0 + 4],
                                              st[:, s0 + 4:s0 + 5], st[:, s0 + 5:s0 + 6])
            @blk.sync
            def _(s, tt=tt, par=par, src=src):
                if tt >= 2:
                    s.wait_ge(ctx.dve.h, marks["xlnw"][tt - 2])
                    s.wait_ge(ctx.dve.h, marks["red"][tt - 2])
                ldp[par].inc(s.dma_start(xstage[par][:, :], src), 16)

            @blk.scalar
            def _(sc, tt=tt, par=par, rs=rs, sumsq=sumsq):
                ldp[par].wait(sc)
                if tt >= 2:
                    sc.wait_ge(ctx.pe.h, marks["tr"][tt - 2])
                ctx.act.inc(sc.activation(xtm[par][:, :], xstage[par][:, :], AF.Square,
                                          accum_out=sumsq))

            @blk.vector
            def _(v, tt=tt, par=par, rs=rs, mean=mean, sumsq=sumsq, var=var):
                ldp[par].wait(v)
                mk = ctx.dve.inc(v.reduce_sum(rs, xstage[par][:, :], axis=AX.X))
                marks["red"][tt] = mk
                ctx.dve.wait(v)
                ctx.dve.inc(v.tensor_scalar(mean, rs, 1.0 / D, None, ALU.mult))
                ctx.act.wait(v)
                ctx.dve.wait(v)
                ctx.dve.inc(v.tensor_tensor(var, mean, mean, ALU.mult))
                ctx.dve.wait(v)
                ctx.dve.inc(v.scalar_tensor_tensor(var, sumsq, 1.0 / D, var,
                                                   ALU.mult, ALU.subtract))

            @blk.scalar
            def _(sc, tt=tt, var=var):
                ctx.dve.wait(sc)
                ctx.act.inc(sc.activation(var, var, AF.Sqrt, bias=ctx.eps_col[:, :]))

            @blk.vector
            def _(v, tt=tt, var=var, rstd=rstd, mean=mean, nb=nb):
                ctx.act.wait(v)
                ctx.dve.inc(v.reciprocal(rstd, var))
                ctx.dve.wait(v)
                ctx.dve.inc(v.scalar_tensor_tensor(nb, mean, -1.0, rstd, ALU.mult, ALU.mult))

            @blk.vector
            def _(v, tt=tt, par=par, rstd=rstd, nb=nb):
                ctx.dve.wait(v)
                mk = ctx.dve.inc(v.tensor_scalar(xtm[par][:, :], xstage[par][:, :],
                                                 rstd, nb, ALU.mult, ALU.add))
                marks["xlnw"][tt] = mk

            for bi in range(nbatch):
                bank = (tt * nbatch + bi) % 2
                @blk.tensor
                def _(t, tt=tt, par=par, bi=bi, bank=bank):
                    t.wait_ge(ctx.dve.h, marks["xlnw"][tt])
                    t.wait_ge(ctx.dve.h, marks["bank"][bank])
                    for j in range(bank_cap):
                        jj = bi * bank_cap + j
                        ins = t.transpose(banks[bank][:, j * 128:(j + 1) * 128],
                                          xtm[par][:, jj * 128:(jj + 1) * 128],
                                          ident[:, :])
                    mk = ctx.pe.inc(ins)
                    if bi == nbatch - 1:
                        marks["tr"][tt] = mk
                @blk.vector
                def _(v, tt=tt, bi=bi, bank=bank, tcol0=tcol0):
                    ctx.pe.wait(v)
                    mk = ctx.dve.inc(v.tensor_copy(
                        xv[:, bi * bank_cap:(bi + 1) * bank_cap, tcol0:tcol0 + 128],
                        banks[bank][:, 0:bank_cap * 128].rearrange("p (j c) -> p j c", c=128)))
                    marks["bank"][bank] = mk
                    marks["ev"][tt] = mk
            if post_cb is not None:
                post_cb(tt, blk, xtm[par], marks)
    return marks


def emit_ln_fm(ctx, blk, src_f32, xln_fm, tcol0, pbank_t, scr_f32, xln_tm, ident,
               fm_cols, ld):
    if not hasattr(ctx, "_ln_st"):
        ctx._ln_st = ctx.sb("ln_stats", [128, 8], F32)
    st = ctx._ln_st
    rs, mean, sumsq, var, rstd, nb = (st[:, 0:1], st[:, 1:2], st[:, 2:3], st[:, 3:4],
                                      st[:, 4:5], st[:, 5:6])

    @blk.scalar
    def _(sc):
        ld.wait(sc)
        ctx.dve.wait(sc)
        ctx.act.wait(sc)
        ctx.act.inc(sc.activation(scr_f32[:, :], src_f32[:, :], AF.Square,
                                  accum_out=sumsq))

    @blk.vector
    def _(v):
        ld.wait(v)
        ctx.dve.inc(v.reduce_sum(rs, src_f32[:, :], axis=AX.X))
        ctx.dve.wait(v)
        ctx.dve.inc(v.tensor_scalar(mean[:, :], rs, 1.0 / D, None, ALU.mult))
        ctx.act.wait(v)
        ctx.dve.wait(v)
        ctx.dve.inc(v.tensor_tensor(var[:, :], mean, mean, ALU.mult))
        ctx.dve.wait(v)
        ctx.dve.inc(v.scalar_tensor_tensor(var[:, :], sumsq, 1.0 / D, var, ALU.mult, ALU.subtract))

    @blk.scalar
    def _(sc):
        ctx.dve.wait(sc)
        ctx.act.inc(sc.activation(var[:, :], var[:, :], AF.Sqrt, bias=ctx.eps_col[:, :]))

    @blk.vector
    def _(v):
        ctx.act.wait(v)
        ctx.dve.inc(v.reciprocal(rstd[:, :], var[:, :]))
        ctx.dve.wait(v)
        ctx.dve.inc(v.scalar_tensor_tensor(nb[:, :], mean, -1.0, rstd, ALU.mult, ALU.mult))

    @blk.scalar
    def _(sc):
        ctx.dve.wait(sc)
        ctx.act.inc(sc.activation(xln_tm[:, :], src_f32[:, :], AF.Identity,
                                  bias=nb, scale=rstd))

    n_feat_tiles = D // 128
    for half in range(n_feat_tiles // 8):
        @blk.tensor
        def _(t, half=half):
            ctx.act.wait(t)
            ctx.dve.wait(t)
            for j in range(8):
                jj = half * 8 + j
                ins = t.transpose(pbank_t[:, j * 128:(j + 1) * 128],
                                  xln_tm[:, jj * 128:(jj + 1) * 128], ident[:, :])
            ctx.pe.inc(ins)

        @blk.vector
        def _(v, half=half):
            ctx.pe.wait(v)
            dst = xln_fm[:, :].rearrange("p (i c) -> p i c", c=fm_cols)
            ctx.dve.inc(v.tensor_copy(
                dst[:, half * 8:(half + 1) * 8, tcol0:tcol0 + 128],
                pbank_t[:, :].rearrange("p (i c) -> p i c", c=128)))


def emit_l0_attn(ctx, blk_new, io):
    nc = ctx.nc
    pk = [ctx.ps("a_pk%d" % i, [128, 512], F32) for i in range(2)]
    pctx = [ctx.ps("a_pctx%d" % i, [128, 512], F32) for i in range(4)]
    ptr = [ctx.ps("a_ptr%d" % i, [128, 1024], BF16) for i in range(2)]

    ident, ones_row = emit_common_consts(ctx, blk_new)
    ld_x = Trk(nc, "a_ld_x")
    ld_w = Trk(nc, "a_ld_w")
    ld_wo = Trk(nc, "a_ld_wo")
    ld_b = Trk(nc, "a_ld_b")
    ld_xr = [Trk(nc, "a_ld_xr%d" % i) for i in range(2)]

    xln = ctx.sb("a_xln", [128, NI * TSEQ], BF16)
    wraw = [ctx.sb("a_wraw%d" % i, [128, D], F32) for i in range(2)]
    wpool = ctx.sb("a_wpool", [128, 8 * D], BF16)
    kfm = ctx.sb("a_kfm", [128, TSEQ], BF16)
    qfm = ctx.sb("a_qfm", [128, TOWN], BF16)
    vfm = ctx.sb("a_vfm", [128, TSEQ], BF16)
    vone = ctx.sb("a_vone", [128, NTS * 129], BF16)
    ptil = [ctx.sb("a_ptil%d" % i, [128, 512], BF16) for i in range(2)]
    ropeT = ctx.sb("a_ropeT", [128, 512], BF16)
    csub = ctx.sb("a_csub", [128, 64], BF16)
    ctx_tm = ctx.sb("a_ctxtm", [128, 512], BF16)
    ctx_fm = ctx.sb("a_ctxfm", [128, H * TOWN], BF16)
    recips = ctx.sb("a_recips", [128, 4], F32)
    bqc = ctx.sb("a_bqc", [128, H], F32)
    bkc = ctx.sb("a_bkc", [128, H], F32)
    bvc = ctx.sb("a_bvc", [128, H], F32)
    bo_row = ctx.sb("a_borow", [1, D], BF16)
    xres = [ctx.sb("a_xres%d" % i, [128, 512], F32) for i in range(2)]
    ysb = [ctx.sb("a_ysb%d" % i, [128, 512], F32) for i in range(2)]

    cs_t = emit_rope_tables(ctx, blk_new, pctx[0], wraw[0], wraw[1],
                            io["pos_all"], io["th_hi"], io["th_lo"], TSEQ, "rt0")

    xln_v = xln[:, :].rearrange("p (i c) -> p i c", c=TSEQ)
    vone_v = vone[:, :].rearrange("p (t c) -> p t c", c=129)
    wT6_v = [[wpool[:, (m * 2 + p) * D:(m * 2 + p + 1) * D].rearrange("p (j c) -> p j c", c=128)
              for m in range(3)] for p in range(2)]
    woTd = [wpool[:, p * 4 * D:(p + 1) * 4 * D].rearrange("p (j c) -> p j c", c=512)
            for p in range(2)]

    with blk_new() as blk:
        @blk.sync
        def _(s):
            ld_b.inc(s.dma_start(bqc[:, :], io["bq_c"]), 16)
            ld_b.inc(s.dma_start(bkc[:, :], io["bk_c"]), 16)
            ld_b.inc(s.dma_start(bvc[:, :], io["bv_c"]), 16)
            ld_b.inc(s.dma_start(csub[:, :], io["csub"]), 16)
            ld_b.wait(s)
        @blk.gpsimd
        def _(g):
            ctx.ld3.inc(g.dma_start(bo_row[:, :], io["bo"].rearrange("(a d) -> a d", a=1)), 16)
            ctx.ld3.wait(g)
        @blk.vector
        def _(v):
            ctx.dve.inc(v.memset(vone_v[:, :, 128:129], 1.0))


    tiles = [(io["x_own"][t * 128:(t + 1) * 128, :], t * 128) for t in range(NT)] + \
            [(io["x_oth"][t * 128:(t + 1) * 128, :], TOWN + t * 128) for t in range(NT)]
    emit_ln_pipe(ctx, blk_new, tiles, xln, TSEQ, BF16, ptr, 8, ident, ld_x,
                 scope="a_ln", xs=wraw)

    wnames = ["wqT_p", "wkT_p", "wvT"]
    ldw = [Trk(nc, "a_ldw%d" % m) for m in range(6)]
    with nc.named_scope("a_heads"), blk_new() as blk:
        marks = {
            "ptr_free": {0: ("dve", 0), 1: ("dve", 0)},
            "pk_free": {0: ("act", 0), 1: ("act", 0)},
            "rot_free": {0: ("act", 0), 1: ("act", 0)},
            "ctx_free": {i: ("dve", 0) for i in range(4)},
            "proj_done": {0: 0, 1: 0},
            "kfm_free": 0, "qfm_free": 0, "pv_last": 0, "vfm_free": 0,
            "ctxtm_free": 0, "vone_dve": 0, "ropeT_dve": 0,
            "vfm_act": ("act", 0),
        }

        def wait_mark(eng, mk):
            nm, val = mk
            eng.wait_ge(getattr(ctx, nm).h, val)

        @blk.gpsimd
        def _(g):
            for m in range(3):
                ldw[m * 2].inc(g.dma_start(
                    wT6_v[0][m], io[wnames[m]][0].rearrange("(j p) d -> p j d", p=128)), 16)

        for h in range(H):
            par = h % 2
            wT_v = {"q": wT6_v[par][0], "k": wT6_v[par][1], "v": wT6_v[par][2]}
            if h + 1 < H:
                @blk.gpsimd
                def _(g, h=h, par=par):
                    g.wait_ge(ctx.pe.h, marks["proj_done"][1 - par])
                    for m in range(3):
                        ldw[m * 2 + (1 - par)].inc(g.dma_start(
                            wT6_v[1 - par][m],
                            io[wnames[m]][h + 1].rearrange("(j p) d -> p j d", p=128)), 16)

            def proj_chunk_m(t, which, cs, bank, par=par):
                wait_mark(t, marks["pk_free"][bank])
                m = {"q": 0, "k": 1, "v": 2}[which]
                ldw[m * 2 + par].wait(t)
                for j in range(NI):
                    ins = t.matmul(pk[bank][:, :], wT_v[which][:, j, :], xln_v[:, j, cs],
                                   start=(j == 0), stop=(j == NI - 1))
                return ctx.pe.inc(ins)

            def rope_emit(dst_fm, bc_even, bc_odd, cs, bank, dst_gate, h=h):
                @blk.vector
                def _(v):
                    ctx.pe.wait(v)
                    mk = ctx.dve.inc(v.scalar_tensor_tensor(ropeT[:, :], pk[bank][:, :],
                                                            bc_full, cs_t[:, cs],
                                                            ALU.add, ALU.mult))
                    marks["ropeT_dve"] = mk
                @blk.tensor
                def _(t):
                    t.wait_ge(ctx.dve.h, marks["ropeT_dve"])
                    wait_mark(t, marks["rot_free"][bank])
                    ctx.pe.inc(t.matmul(pctx[bank][0:64, 0:512], csub[:, :], ropeT[:, :],
                                        start=True, stop=True))
                @blk.vector
                def _(v):
                    ctx.pe.wait(v)
                    if dst_gate is not None and h > 0:
                        v.wait_ge(ctx.pe.h, marks[dst_gate])
                    mk = ctx.dve.inc(v.tensor_scalar(dst_fm[64:128, cs],
                                                     pk[bank][64:128, :], bc_odd, None,
                                                     ALU.add))
                    marks["pk_free"][bank] = ("dve", mk)
                    marks["last_odd"] = ("dve", mk)
                @blk.scalar
                def _(sc):
                    ctx.pe.wait(sc)
                    if dst_gate is not None and h > 0:
                        sc.wait_ge(ctx.pe.h, marks[dst_gate])
                    mk = ctx.act.inc(sc.activation(dst_fm[0:64, cs],
                                                   pctx[bank][0:64, 0:512], AF.Identity))
                    marks["rot_free"][bank] = ("act", mk)
                    marks["last_rot"] = ("act", mk)

            for c in range(4):
                cs = slice(c * 512, (c + 1) * 512)
                @blk.tensor
                def _(t, c=c, cs=cs):
                    proj_chunk_m(t, "k", cs, c % 2)
                bc_full = bkc[:, h:h + 1]
                rope_emit(kfm, bkc[0:64, h:h + 1], bkc[64:128, h:h + 1], cs, c % 2,
                          "kfm_free" if c == 0 else None)
            marks["k_ready"] = [marks["last_odd"], marks["last_rot"]]
            for c in range(2):
                cs = slice(c * 512, (c + 1) * 512)
                @blk.tensor
                def _(t, c=c, cs=cs):
                    proj_chunk_m(t, "q", cs, c % 2)
                bc_full = bqc[:, h:h + 1]
                rope_emit(qfm, bqc[0:64, h:h + 1], bqc[64:128, h:h + 1], cs, c % 2,
                          "qfm_free" if c == 0 else None)
            marks["q_ready"] = [marks["last_odd"], marks["last_rot"]]
            for c in range(4):
                cs = slice(c * 512, (c + 1) * 512)
                @blk.tensor
                def _(t, c=c, cs=cs):
                    proj_chunk_m(t, "v", cs, c % 2)
                @blk.vector
                def _(v, c=c, cs=cs, h=h):
                    ctx.pe.wait(v)
                    if h > 0 and c == 0:
                        v.wait_ge(ctx.pe.h, marks["vfm_free"])
                    mk = ctx.dve.inc(v.tensor_scalar(vfm[:, cs], pk[c % 2][:, :],
                                                     bvc[:, h:h + 1], None, ALU.add))
                    marks["pk_free"][c % 2] = ("dve", mk)
                    marks["vfm_act"] = ("dve", mk)
            marks["proj_done"][par] = ctx.pe.n
            for bi in range(2):
                @blk.tensor
                def _(t, bi=bi, h=h):
                    wait_mark(t, marks["vfm_act"])
                    wait_mark(t, marks["ptr_free"][bi % 2])
                    if h > 0 and bi == 0:
                        t.wait_ge(ctx.pe.h, marks["pv_last"])
                    for j in range(8):
                        jj = bi * 8 + j
                        ins = t.transpose(ptr[bi % 2][:, j * 128:(j + 1) * 128],
                                          vfm[:, jj * 128:(jj + 1) * 128], ident[:, :])
                    mk = ctx.pe.inc(ins)
                    if bi == 1:
                        marks["vfm_free"] = mk
                @blk.vector
                def _(v, bi=bi):
                    ctx.pe.wait(v)
                    mk = ctx.dve.inc(v.tensor_copy(
                        vone_v[:, bi * 8:(bi + 1) * 8, 0:128],
                        ptr[bi % 2][:, :].rearrange("p (j c) -> p j c", c=128)))
                    marks["ptr_free"][bi % 2] = ("dve", mk)
                    marks["vone_dve"] = mk

            for qc in range(2):
                qs = slice(qc * 512, (qc + 1) * 512)
                s_marks, e_marks, pv_marks = {}, {}, {}

                def emit_pv(t, tk):
                    pbuf = ptil[tk % 2]
                    if tk == 0:
                        for sub in range(4):
                            wait_mark(t, marks["ctx_free"][sub])
                    for sub in range(4):
                        ins = t.matmul(pctx[sub][:, 0:129], pbuf[:, sub * 128:(sub + 1) * 128],
                                       vone_v[:, tk, :],
                                       start=(tk == 0), stop=(tk == NTS - 1))
                    pv_marks[tk] = ctx.pe.inc(ins)

                for tk in range(NTS):
                    @blk.tensor
                    def _(t, tk=tk, qs=qs, qc=qc):
                        if tk >= 2:
                            t.wait_ge(ctx.act.h, e_marks[tk - 2])
                        else:
                            wait_mark(t, marks["pk_free"][tk % 2])
                        if tk == 0 and qc == 0:
                            t.wait_ge(ctx.dve.h, marks["vone_dve"])
                            for mk_ in marks["k_ready"] + marks["q_ready"]:
                                wait_mark(t, mk_)
                        ins = t.matmul(pk[tk % 2][:, :], kfm[:, tk * 128:(tk + 1) * 128],
                                       qfm[:, qs], start=True, stop=True)
                        s_marks[tk] = ctx.pe.inc(ins)
                        if tk >= 1:
                            t.wait_ge(ctx.act.h, e_marks[tk - 1])
                            emit_pv(t, tk - 1)
                    @blk.scalar
                    def _(sc, tk=tk, qc=qc):
                        sc.wait_ge(ctx.pe.h, s_marks[tk])
                        if tk >= 2:
                            sc.wait_ge(ctx.pe.h, pv_marks[tk - 2])
                        elif qc == 0:
                            sc.wait_ge(ctx.pe.h, marks["pv_last"])
                        ins = sc.activation(ptil[tk % 2][:, :], pk[tk % 2][:, :], AF.Exp,
                                            scale=ATTN_SCALE)
                        e_marks[tk] = ctx.act.inc(ins)
                        if tk == NTS - 1:
                            marks["pk_free"][0] = ("act", e_marks[NTS - 2])
                            marks["pk_free"][1] = ("act", e_marks[NTS - 1])
                @blk.tensor
                def _(t, qc=qc):
                    t.wait_ge(ctx.act.h, e_marks[NTS - 1])
                    emit_pv(t, NTS - 1)
                marks["pv_last"] = ctx.pe.n
                @blk.vector
                def _(v, qc=qc):
                    v.wait_ge(ctx.pe.h, pv_marks[NTS - 1])
                    v.wait_ge(ctx.pe.h, marks["ctxtm_free"])
                    ctx.dve.wait(v)
                    for sub in range(4):
                        ctx.dve.inc(v.reciprocal(recips[:, sub:sub + 1],
                                                 pctx[sub][:, 128:129]))
                    ctx.dve.wait(v)
                    for sub in range(4):
                        mk = ctx.dve.inc(v.tensor_scalar(ctx_tm[:, sub * 128:(sub + 1) * 128],
                                                         pctx[sub][:, 0:128],
                                                         recips[:, sub:sub + 1], None, ALU.mult))
                        marks["ctx_free"][sub] = ("dve", mk)
                @blk.tensor
                def _(t, qc=qc):
                    ctx.dve.wait(t)
                    wait_mark(t, marks["ptr_free"][0])
                    for sub in range(4):
                        ins = t.transpose(ptr[0][:, sub * 128:(sub + 1) * 128],
                                          ctx_tm[:, sub * 128:(sub + 1) * 128], ident[:, :])
                    marks["ctxtm_free"] = ctx.pe.inc(ins)
                @blk.vector
                def _(v, qc=qc, h=h):
                    ctx.pe.wait(v)
                    mk = ctx.dve.inc(v.tensor_copy(
                        ctx_fm[:, h * TOWN + qc * 512:h * TOWN + (qc + 1) * 512],
                        ptr[0][:, 0:512]))
                    marks["ptr_free"][0] = ("dve", mk)
            marks["kfm_free"] = marks["pv_last"]
            marks["qfm_free"] = marks["pv_last"]

    ctx_v = ctx_fm[:, :].rearrange("p (i c) -> p i c", c=TOWN)
    ld_oc = [Trk(nc, "a_ldoc%d" % i) for i in range(2)]
    opj_done = {0: 0, 1: 0}
    for oc in range(4):
        with nc.named_scope("a_oproj"), blk_new() as blk:
            @blk.gpsimd
            def _(g, oc=oc):
                g.wait_ge(ctx.pe.h, opj_done[oc % 2])
                ld_oc[oc % 2].inc(g.dma_start(
                    woTd[oc % 2], io["woT_r"][oc].rearrange("(j p) d -> p j d", p=128)), 16)
        with blk_new() as blk:
            t_marks = {}
            for tt in range(NT):
                bank = tt % 4
                @blk.sync
                def _(s, tt=tt, oc=oc):
                    if tt >= 2:
                        s.wait_ge(ctx.dve.h, t_marks[tt - 2])
                    ld_xr[tt % 2].inc(s.dma_start(xres[tt % 2][:, :],
                                                  io["x_own"][tt * 128:(tt + 1) * 128,
                                                              oc * 512:(oc + 1) * 512]), 16)
                @blk.tensor
                def _(t, tt=tt, bank=bank, oc=oc):
                    if tt == 0:
                        ld_oc[oc % 2].wait(t)
                    if tt >= 4:
                        t.wait_ge(ctx.dve.h, t_marks[tt - 4])
                    t.matmul(pctx[bank][:, :], ones_row[0:1, 0:128],
                             bo_row[0:1, oc * 512:(oc + 1) * 512], start=True, stop=False)
                    for j in range(NI):
                        ins = t.matmul(pctx[bank][:, :], ctx_v[:, j, tt * 128:(tt + 1) * 128],
                                       woTd[oc % 2][:, j, :], start=False, stop=(j == NI - 1))
                    mk = ctx.pe.inc(ins)
                    if tt == NT - 1:
                        opj_done[oc % 2] = mk
                @blk.vector
                def _(v, tt=tt, bank=bank):
                    ctx.pe.wait(v)
                    ld_xr[tt % 2].wait(v)
                    v.wait_ge(ctx.ld2.h, ctx.ld2.n)
                    t_marks[tt] = ctx.dve.inc(v.tensor_add(ysb[tt % 2][:, :],
                                                           xres[tt % 2][:, :],
                                                           pctx[bank][:, :]))
                @blk.scalar
                def _(sc, tt=tt, oc=oc):
                    sc.wait_ge(ctx.dve.h, t_marks[tt])
                    ctx.ld2.inc(sc.dma_start(io["y"][tt * 128:(tt + 1) * 128,
                                                     oc * 512:(oc + 1) * 512],
                                             ysb[tt % 2][:, :]), 16)
    with blk_new() as blk:
        @blk.scalar
        def _(sc):
            ctx.ld2.wait(sc)


def emit_l1_attn(ctx, blk_new, io):
    nc = ctx.nc
    TK = 1280
    NTK = TK // 128
    W = 256
    pk = [ctx.ps("c_pk%d" % i, [128, 512], F32) for i in range(2)]
    pctx = [ctx.ps("c_pctx%d" % i, [128, 512], F32) for i in range(4)]
    ptr = [ctx.ps("c_ptr%d" % i, [128, 1024], BF16) for i in range(2)]

    ident, ones_row = emit_common_consts(ctx, blk_new)
    ld_x = Trk(nc, "c_ld_x")
    ld_w = Trk(nc, "c_ld_w")
    ld_wo = Trk(nc, "c_ld_wo")
    ld_b = Trk(nc, "c_ld_b")
    ld_xr = [Trk(nc, "c_ld_xr%d" % i) for i in range(2)]

    xln = ctx.sb("c_xln", [128, NI * TK], BF16)
    wraw = [ctx.sb("c_wraw%d" % i, [128, D], F32) for i in range(2)]
    wpool = ctx.sb("c_wpool", [128, 8 * D], BF16)
    kfm = ctx.sb("c_kfm", [128, TK], BF16)
    qfm = ctx.sb("c_qfm", [128, TOWN], BF16)
    vfm = ctx.sb("c_vfm", [128, TK], BF16)
    vone = ctx.sb("c_vone", [128, NTK * 129], BF16)
    ptil = ctx.sb("c_ptil", [128, 4 * 256], BF16)
    ropeT = ctx.sb("c_ropeT", [128, 1024], BF16)
    csub = ctx.sb("c_csub", [128, 64], BF16)
    cadd = ctx.sb("c_cadd", [128, 64], BF16)
    maskA = ctx.sb("c_maskA", [128, 256], BF16)
    maskB = ctx.sb("c_maskB", [128, 256], BF16)
    prevflag = ctx.sb("c_prevflag", [128, 1], F32)
    ctx_tm = ctx.sb("c_ctxtm", [128, 512], BF16)
    ctx_fm = ctx.sb("c_ctxfm", [128, H * TOWN], BF16)
    recips = ctx.sb("c_recips", [128, 4], F32)
    xres = [ctx.sb("c_xres%d" % i, [128, 512], F32) for i in range(2)]
    ysb = [ctx.sb("c_ysb%d" % i, [128, 512], F32) for i in range(2)]

    cs_t = emit_rope_tables(ctx, blk_new, pctx[0], wraw[0], wraw[1],
                            io["pos_all"], io["th_hi"], io["th_lo"], 1536, "rt1")
    sc_t = emit_rope_tables(ctx, blk_new, pctx[0], wraw[0], wraw[1],
                            io["pos_all"], io["th_hi"], io["th_lo"], 1536, "rt1s",
                            swap=True)

    xln_v = xln[:, :].rearrange("p (i c) -> p i c", c=TK)
    vone_v = vone[:, :].rearrange("p (t c) -> p t c", c=129)
    wT6_v = [[wpool[:, (m * 2 + p) * D:(m * 2 + p + 1) * D].rearrange("p (j c) -> p j c", c=128)
              for m in range(3)] for p in range(2)]
    woTd = [wpool[:, p * 4 * D:(p + 1) * 4 * D].rearrange("p (j c) -> p j c", c=512)
            for p in range(2)]

    with blk_new() as blk:
        @blk.sync
        def _(s):
            ld_b.inc(s.dma_start(csub[:, :], io["csub"]), 16)
            ld_b.inc(s.dma_start(cadd[:, :], io["cadd"]), 16)
            ld_b.inc(s.dma_start(prevflag[:, :], io["prevflag"]), 16)
            ld_b.wait(s)
        @blk.vector
        def _(v):
            ctx.dve.inc(v.memset(vone_v[:, :, 128:129], 1.0))
        @blk.gpsimd
        def _(g):
            ctx.pool.inc(g.memset(maskA[:, :], 1.0))
            ctx.pool.inc(g.memset(maskB[:, :], 1.0))
            ctx.pool.wait(g)
            g.affine_select(maskA[:, :], maskA[:, :], [[1, 256]], ALU.is_ge, 0.0,
                            base=0, channel_multiplier=-1)
            ctx.pool.inc(g.affine_select(maskB[:, :], maskB[:, :], [[1, 256]],
                                         ALU.is_ge, 0.0, base=-128,
                                         channel_multiplier=-1))

    tilesC = [(io["x_prev"][t * 128:(t + 1) * 128, :], t * 128) for t in range(2)] + \
             [(io["x_own"][t * 128:(t + 1) * 128, :], 256 + t * 128) for t in range(NT)]
    emit_ln_pipe(ctx, blk_new, tilesC, xln, TK, BF16, ptr, 8, ident, ld_x,
                 scope="c_ln", xs=wraw)

    def proj_chunk(t, which, cs, bank, w):
        for j in range(NI):
            ins = t.matmul(pk[bank][:, 0:w], wT_v[which][:, j, :], xln_v[:, j, cs],
                           start=(j == 0), stop=(j == NI - 1))
        return ctx.pe.inc(ins)

    kq_chunks = [(0, 512), (512, 512), (1024, 256)]
    q_chunks = [(0, 512), (512, 512)]

    wnamesC = ["WqT_p", "WkT_p", "WvT1"]
    ldw = [Trk(nc, "c_ldw%d" % m) for m in range(6)]
    with nc.named_scope("c_heads"), blk_new() as blk:
        marks = {
            "ptr_free": {0: ("dve", 0), 1: ("dve", 0)},
            "pk_free": {0: ("act", 0), 1: ("act", 0)},
            "rot_free": {0: ("act", 0), 1: ("act", 0)},
            "ctx_free": {i: ("dve", 0) for i in range(2)},
            "proj_done": {0: 0, 1: 0},
            "kfm_free": 0, "qfm_free": 0, "pv_last": 0, "vfm_free": 0,
            "ctxtm_free": 0, "vone_dve": 0, "ropeT_dve": 0,
            "vfm_act": ("act", 0),
        }

        def wait_mark(eng, mk):
            nm, val = mk
            eng.wait_ge(getattr(ctx, nm).h, val)

        @blk.gpsimd
        def _(g):
            for m in range(3):
                ldw[m * 2].inc(g.dma_start(
                    wT6_v[0][m], io[wnamesC[m]][0].rearrange("(j p) d -> p j d", p=128)), 16)

        for h in range(H):
            par = h % 2
            wT_v = {"q": wT6_v[par][0], "k": wT6_v[par][1], "v": wT6_v[par][2]}
            if h + 1 < H:
                @blk.gpsimd
                def _(g, h=h, par=par):
                    g.wait_ge(ctx.pe.h, marks["proj_done"][1 - par])
                    for m in range(3):
                        ldw[m * 2 + (1 - par)].inc(g.dma_start(
                            wT6_v[1 - par][m],
                            io[wnamesC[m]][h + 1].rearrange("(j p) d -> p j d", p=128)), 16)

            def proj_chunk_m(t, which, cs, bank, w, par=par):
                wait_mark(t, marks["pk_free"][bank])
                m = {"q": 0, "k": 1, "v": 2}[which]
                ldw[m * 2 + par].wait(t)
                for j in range(NI):
                    ins = t.matmul(pk[bank][:, 0:w], wT_v[which][:, j, :], xln_v[:, j, cs],
                                   start=(j == 0), stop=(j == NI - 1))
                return ctx.pe.inc(ins)

            def rope_emit(dst_fm, dcol, tcol, w, bank, dst_gate, h=h):
                @blk.vector
                def _(v):
                    ctx.pe.wait(v)
                    ctx.dve.inc(v.tensor_mul(ropeT[:, 0:w], pk[bank][:, 0:w],
                                             cs_t[:, tcol:tcol + w]))
                    mk = ctx.dve.inc(v.tensor_mul(ropeT[:, 512:512 + w], pk[bank][:, 0:w],
                                                  sc_t[:, tcol:tcol + w]))
                    marks["ropeT_dve"] = mk
                @blk.tensor
                def _(t):
                    t.wait_ge(ctx.dve.h, marks["ropeT_dve"])
                    wait_mark(t, marks["rot_free"][bank])
                    t.matmul(pctx[bank][0:64, 0:w], csub[:, :], ropeT[:, 0:w],
                             start=True, stop=True)
                    ctx.pe.inc(t.matmul(pctx[bank][64:128, 0:w], cadd[:, :],
                                        ropeT[:, 512:512 + w],
                                        start=True, stop=True, tile_position=(0, 64)))
                @blk.vector
                def _(v):
                    ctx.pe.wait(v)
                    if dst_gate is not None and h > 0:
                        v.wait_ge(ctx.pe.h, marks[dst_gate])
                    mk = ctx.dve.inc(v.tensor_copy(dst_fm[:, dcol:dcol + w],
                                                   pctx[bank][:, 0:w]))
                    marks["pk_free"][bank] = ("dve", mk)
                    marks["rot_free"][bank] = ("dve", mk)
                    marks["kq_last"] = ("dve", mk)

            for c, (col, w) in enumerate(kq_chunks):
                @blk.tensor
                def _(t, c=c, col=col, w=w):
                    proj_chunk_m(t, "k", slice(col, col + w), c % 2, w)
                rope_emit(kfm, col, col, w, c % 2, "kfm_free" if c == 0 else None)
            for c, (col, w) in enumerate(q_chunks):
                @blk.tensor
                def _(t, c=c, col=col, w=w):
                    proj_chunk_m(t, "q", slice(256 + col, 256 + col + w), c % 2, w)
                rope_emit(qfm, col, 256 + col, w, c % 2, "qfm_free" if c == 0 else None)
            marks["kq_ready"] = marks["kq_last"]
            for c, (col, w) in enumerate(kq_chunks):
                @blk.tensor
                def _(t, c=c, col=col, w=w):
                    proj_chunk_m(t, "v", slice(col, col + w), c % 2, w)
                @blk.vector
                def _(v, c=c, col=col, w=w, h=h):
                    ctx.pe.wait(v)
                    if h > 0 and c == 0:
                        v.wait_ge(ctx.pe.h, marks["vfm_free"])
                    mk = ctx.dve.inc(v.tensor_copy(vfm[:, col:col + w], pk[c % 2][:, 0:w]))
                    marks["pk_free"][c % 2] = ("dve", mk)
                    marks["vfm_act"] = ("dve", mk)
            marks["proj_done"][par] = ctx.pe.n
            for bi, j0 in enumerate(range(0, NTK, 8)):
                j1 = min(NTK, j0 + 8)
                @blk.tensor
                def _(t, j0=j0, j1=j1, bi=bi, h=h):
                    wait_mark(t, marks["vfm_act"])
                    wait_mark(t, marks["ptr_free"][bi % 2])
                    if h > 0 and bi == 0:
                        t.wait_ge(ctx.pe.h, marks["pv_last"])
                    for j in range(j0, j1):
                        ins = t.transpose(ptr[bi % 2][:, (j - j0) * 128:(j - j0 + 1) * 128],
                                          vfm[:, j * 128:(j + 1) * 128], ident[:, :])
                    mk = ctx.pe.inc(ins)
                    if bi == 1:
                        marks["vfm_free"] = mk
                @blk.vector
                def _(v, j0=j0, j1=j1, bi=bi):
                    ctx.pe.wait(v)
                    mk = ctx.dve.inc(v.tensor_copy(
                        vone_v[:, j0:j1, 0:128],
                        ptr[bi % 2][:, 0:(j1 - j0) * 128].rearrange("p (j c) -> p j c", c=128)))
                    marks["ptr_free"][bi % 2] = ("dve", mk)
                    marks["vone_dve"] = mk

            for lc in range(4):
                qs = slice(lc * 256, (lc + 1) * 256)
                s_marks, pv_marks = {}, {}
                p_ready = {}

                def emit_pv(t, j, lc):
                    if j == 0:
                        for sub in range(2):
                            wait_mark(t, marks["ctx_free"][sub])
                    for sub in range(2):
                        ins = t.matmul(pctx[sub][:, 0:129],
                                       ptil[:, j * 256 + sub * 128:j * 256 + (sub + 1) * 128],
                                       vone_v[:, 2 * lc + j, :],
                                       start=(j == 0), stop=(j == 3))
                    pv_marks[j] = ctx.pe.inc(ins)

                for j in range(4):
                    @blk.tensor
                    def _(t, j=j, lc=lc, qs=qs):
                        if j >= 2:
                            h2, n2 = p_ready[j - 2]
                            t.wait_ge(h2, n2)
                        else:
                            wait_mark(t, marks["pk_free"][j % 2])
                        if j == 0 and lc == 0:
                            t.wait_ge(ctx.dve.h, marks["vone_dve"])
                            wait_mark(t, marks["kq_ready"])
                        ins = t.matmul(pk[j % 2][:, 0:256],
                                       kfm[:, (2 * lc + j) * 128:(2 * lc + j + 1) * 128],
                                       qfm[:, qs], start=True, stop=True)
                        s_marks[j] = ctx.pe.inc(ins)
                        if j >= 1:
                            h1, n1 = p_ready[j - 1]
                            t.wait_ge(h1, n1)
                            emit_pv(t, j - 1, lc)
                    @blk.scalar
                    def _(sc, j=j, lc=lc):
                        sc.wait_ge(ctx.pe.h, s_marks[j])
                        sc.wait_ge(ctx.pe.h, marks["pv_last"])
                        mk = ctx.act.inc(
                            sc.activation(ptil[:, j * 256:(j + 1) * 256],
                                          pk[j % 2][:, 0:256], AF.Exp, scale=ATTN_SCALE))
                        p_ready[j] = (ctx.act.h, mk)
                        if j >= 2:
                            marks["pk_free"][j % 2] = ("act", mk)
                    @blk.vector
                    def _(v, j=j, lc=lc):
                        h0, n0 = p_ready[j]
                        v.wait_ge(h0, n0)
                        pt = ptil[:, j * 256:(j + 1) * 256]
                        if j >= 2:
                            m = maskA if j == 2 else maskB
                            mk = ctx.dve.inc(v.tensor_mul(pt, pt, m[:, :]))
                            p_ready[j] = (ctx.dve.h, mk)
                        elif lc == 0:
                            mk = ctx.dve.inc(
                                v.tensor_scalar(pt, pt, prevflag[:, 0:1], None, ALU.mult))
                            p_ready[j] = (ctx.dve.h, mk)
                @blk.tensor
                def _(t, lc=lc):
                    h3, n3 = p_ready[3]
                    t.wait_ge(h3, n3)
                    emit_pv(t, 3, lc)
                marks["pv_last"] = ctx.pe.n
                @blk.vector
                def _(v, lc=lc):
                    v.wait_ge(ctx.pe.h, pv_marks[3])
                    v.wait_ge(ctx.pe.h, marks["ctxtm_free"])
                    ctx.dve.wait(v)
                    for sub in range(2):
                        ctx.dve.inc(v.reciprocal(recips[:, sub:sub + 1],
                                                 pctx[sub][:, 128:129]))
                    ctx.dve.wait(v)
                    for sub in range(2):
                        mk = ctx.dve.inc(v.tensor_scalar(ctx_tm[:, sub * 128:(sub + 1) * 128],
                                                         pctx[sub][:, 0:128],
                                                         recips[:, sub:sub + 1], None, ALU.mult))
                        marks["ctx_free"][sub] = ("dve", mk)
                @blk.tensor
                def _(t, lc=lc):
                    ctx.dve.wait(t)
                    wait_mark(t, marks["ptr_free"][0])
                    for sub in range(2):
                        ins = t.transpose(ptr[0][:, sub * 128:(sub + 1) * 128],
                                          ctx_tm[:, sub * 128:(sub + 1) * 128], ident[:, :])
                    marks["ctxtm_free"] = ctx.pe.inc(ins)
                @blk.vector
                def _(v, lc=lc, h=h):
                    ctx.pe.wait(v)
                    mk = ctx.dve.inc(v.tensor_copy(
                        ctx_fm[:, h * TOWN + lc * 256:h * TOWN + (lc + 1) * 256],
                        ptr[0][:, 0:256]))
                    marks["ptr_free"][0] = ("dve", mk)
            marks["kfm_free"] = marks["pv_last"]
            marks["qfm_free"] = marks["pv_last"]

    ctx_v = ctx_fm[:, :].rearrange("p (i c) -> p i c", c=TOWN)
    ld_oc = [Trk(nc, "c_ldoc%d" % i) for i in range(2)]
    opj_done = {0: 0, 1: 0}
    for oc in range(4):
        with nc.named_scope("c_oproj"), blk_new() as blk:
            @blk.gpsimd
            def _(g, oc=oc):
                g.wait_ge(ctx.pe.h, opj_done[oc % 2])
                ld_oc[oc % 2].inc(g.dma_start(
                    woTd[oc % 2], io["WoT_r"][oc].rearrange("(j p) d -> p j d", p=128)), 16)
        with blk_new() as blk:
            t_marks = {}
            for tt in range(NT):
                bank = tt % 4
                @blk.sync
                def _(s, tt=tt, oc=oc):
                    if tt >= 2:
                        s.wait_ge(ctx.dve.h, t_marks[tt - 2])
                    ld_xr[tt % 2].inc(s.dma_start(xres[tt % 2][:, :],
                                                  io["x_own"][tt * 128:(tt + 1) * 128,
                                                              oc * 512:(oc + 1) * 512]), 16)
                @blk.tensor
                def _(t, tt=tt, bank=bank, oc=oc):
                    if tt == 0:
                        ld_oc[oc % 2].wait(t)
                    if tt >= 4:
                        t.wait_ge(ctx.dve.h, t_marks[tt - 4])
                    for j in range(NI):
                        ins = t.matmul(pctx[bank][:, :], ctx_v[:, j, tt * 128:(tt + 1) * 128],
                                       woTd[oc % 2][:, j, :], start=(j == 0), stop=(j == NI - 1))
                    mk = ctx.pe.inc(ins)
                    if tt == NT - 1:
                        opj_done[oc % 2] = mk
                @blk.vector
                def _(v, tt=tt, bank=bank):
                    ctx.pe.wait(v)
                    ld_xr[tt % 2].wait(v)
                    v.wait_ge(ctx.ld2.h, ctx.ld2.n)
                    t_marks[tt] = ctx.dve.inc(v.tensor_add(ysb[tt % 2][:, :],
                                                           xres[tt % 2][:, :],
                                                           pctx[bank][:, :]))
                @blk.scalar
                def _(sc, tt=tt, oc=oc):
                    sc.wait_ge(ctx.dve.h, t_marks[tt])
                    ctx.ld2.inc(sc.dma_start(io["y"][tt * 128:(tt + 1) * 128,
                                                     oc * 512:(oc + 1) * 512],
                                             ysb[tt % 2][:, :]), 16)
    with blk_new() as blk:
        @blk.scalar
        def _(sc):
            ctx.ld2.wait(sc)


def emit_l0_ffn(ctx, blk_new, io, ncores=8):
    nc = ctx.nc
    DK = float(2 ** 20 + 1)
    pb = [ctx.ps("b_p%d" % i, [128, 512], F32) for i in range(8)]
    ld_x = Trk(nc, "b_ld_x")
    ld_w = Trk(nc, "b_ld_w")
    ld_wq = Trk(nc, "b_ld_wq")
    ld_xr = [Trk(nc, "b_ld_xr%d" % i) for i in range(2)]
    ld_cc = Trk(nc, "b_ld_cc")
    cc_sem = Trk(nc, "b_cc")

    identf = ctx.sb("b_identf", [128, 128], F32)
    ctx.eps_col = ctx.sb("b_eps", [128, 1], F32)
    ones_f = ctx.sb("b_onesf", [1, 128], F32)
    xln = ctx.sb("b_xln", [128, NI * TOWN], F32)
    xq = ctx.sb("b_xq", [128, NI * TOWN], BF16)
    scr = ctx.sb("b_scr", [128, D], F32)
    xln_tm = ctx.sb("b_xlntm", [128, D], F32)
    qs1 = ctx.sb("b_qs1", [128, D], F32)
    qs2 = ctx.sb("b_qs2", [128, D], F32)
    wstage = [ctx.sb("b_wst%d" % i, [128, D], BF16) for i in range(2)]
    wqT2 = [ctx.sb("b_wqT%d" % i, [128, NI * 512], BF16) for i in range(2)]
    stats = ctx.sb("b_stats", [128, 8], F32)
    amx = ctx.sb("b_amx", [128, 2], F32)
    sca = ctx.sb("b_sca", [1, 8], F32)
    scol = ctx.sb("b_scol", [128, 4], F32)
    otmp = ctx.sb("b_otmp", [128, 4 * 512], BF16)
    xres = [ctx.sb("b_xres%d" % i, [128, 512], F32) for i in range(2)]
    ysb = [ctx.sb("b_ysb%d" % i, [128, 512], F32) for i in range(2)]

    xln_v = xln[:, :].rearrange("p (i c) -> p i c", c=TOWN)
    xq_v = xq[:, :].rearrange("p (i c) -> p i c", c=TOWN)
    wqT_v2 = [w[:, :].rearrange("p (j c) -> p j c", c=512) for w in wqT2]

    def tr_f32(blk, src_f32_t, dst_v, j0_dst, col_off, ncol):
        for b4 in range(4):
            @blk.tensor
            def _(t, b4=b4):
                ctx.act.wait(t)
                ctx.dve.wait(t)
                for j in range(4):
                    jj = b4 * 4 + j
                    ins = t.transpose(pb[6 + b4 % 2][:, j * 128:(j + 1) * 128],
                                      src_f32_t[:, jj * 128:(jj + 1) * 128], identf[:, :])
                ctx.pe.inc(ins)
            @blk.vector
            def _(v, b4=b4):
                ctx.pe.wait(v)
                ctx.dve.inc(v.tensor_copy(
                    dst_v[:, j0_dst + b4 * 4:j0_dst + b4 * 4 + 4, col_off:col_off + 128],
                    pb[6 + b4 % 2][:, 0:512].rearrange("p (j c) -> p j c", c=128)))

    with blk_new() as blk:
        @blk.gpsimd
        def _(g):
            ctx.pool.inc(g.memset(identf[:, :], 1.0))
            g.memset(ctx.eps_col[:, :], 1e-5)
            g.memset(ones_f[:, :], 1.0)
            ctx.pool.wait(g)
            g.affine_select(identf[:, :], identf[:, :], [[1, 128]], ALU.is_equal, 0.0,
                            base=0, channel_multiplier=-1)
        @blk.vector
        def _(v):
            ctx.dve.inc(v.memset(amx[:, :], 0.0))

    def b_amax_cb(tt, blk, xtm_ap, marks):
        @blk.vector
        def _(v, tt=tt, xtm_ap=xtm_ap):
            v.wait_ge(ctx.dve.h, marks["xlnw"][tt])
            ctx.dve.inc(v.tensor_reduce(stats[:, 0:1], xtm_ap[:, :], AX.X, ALU.max,
                                        apply_absolute_value=True))
            ctx.dve.wait(v)
            ctx.dve.inc(v.tensor_max(amx[:, 0:1], amx[:, 0:1], stats[:, 0:1]))

    tilesB = [(io["x_own"][t * 128:(t + 1) * 128, :], t * 128) for t in range(NT)]
    emit_ln_pipe(ctx, blk_new, tilesB, xln, TOWN, F32, [pb[6], pb[7]], 4, identf,
                 ld_x, post_cb=b_amax_cb, scope="b_ln", xs=[qs1, qs2],
                 tms=[scr, xln_tm])
    for t in range(NI):
        with blk_new() as blk:
            @blk.sync
            def _(s, t=t):
                s.wait_ge(ctx.dve.h, ctx.dve.n)
                ld_w.inc(s.dma_start(wstage[t % 2][:, :], io["w"][t * 128:(t + 1) * 128, :]), 16)
                ld_w.wait(s)
            @blk.vector
            def _(v, t=t):
                ld_w.wait(v)
                ctx.dve.inc(v.tensor_reduce(stats[:, 0:1], wstage[t % 2][:, :], AX.X,
                                            ALU.max, apply_absolute_value=True))
                ctx.dve.wait(v)
                ctx.dve.inc(v.tensor_max(amx[:, 1:2], amx[:, 1:2], stats[:, 0:1]))

    with blk_new() as blk:
        @blk.tensor
        def _(t):
            ctx.dve.wait(t)
            t.transpose(pb[6][0:1, 0:128], amx[:, 0:1], identf[:, :])
            ctx.pe.inc(t.transpose(pb[6][0:1, 128:256], amx[:, 1:2], identf[:, :]))
        @blk.vector
        def _(v):
            ctx.pe.wait(v)
            ctx.dve.inc(v.reduce_max(sca[0:1, 0:1], pb[6][0:1, 0:128], axis=AX.X))
            ctx.dve.inc(v.reduce_max(sca[0:1, 3:4], pb[6][0:1, 128:256], axis=AX.X))
    with blk_new() as blk:
        @blk.gpsimd
        def _(g):
            ctx.dve.wait(g)
            ld_cc.inc(g.dma_start(io["cc_in"], sca[0:1, 0:1]), 16)
            ld_cc.wait(g)
            cc_sem.inc(g.collective_compute("AllReduce", ALU.max,
                                            [list(range(ncores))],
                                            [io["cc_in"].opt()], [io["cc_out"].opt()]))
            cc_sem.wait(g)
            ld_cc.inc(g.dma_start(sca[0:1, 0:1], io["cc_out"]), 16)
            ld_cc.wait(g)
    with blk_new() as blk:
        @blk.vector
        def _(v):
            ld_cc.wait(v)
            def step(ins):
                ctx.dve.inc(ins)
                ctx.dve.wait(v)
            step(v.tensor_scalar(sca[0:1, 1:2], sca[0:1, 0:1], 1.0 / 448.0, 1e-12,
                                 ALU.mult, ALU.max))
            step(v.reciprocal(sca[0:1, 2:3], sca[0:1, 1:2]))
            step(v.tensor_scalar(sca[0:1, 4:5], sca[0:1, 3:4], 1.0 / 448.0, 1e-12,
                                 ALU.mult, ALU.max))
            step(v.reciprocal(sca[0:1, 5:6], sca[0:1, 4:5]))
            step(v.tensor_tensor(sca[0:1, 6:7], sca[0:1, 1:2], sca[0:1, 4:5], ALU.mult))
        @blk.tensor
        def _(t):
            ctx.dve.wait(t)
            t.matmul(pb[5][:, 0:1], ones_f[:, :], sca[0:1, 2:3], start=True, stop=True)
            t.matmul(pb[5][:, 1:2], ones_f[:, :], sca[0:1, 5:6], start=True, stop=True)
            ctx.pe.inc(t.matmul(pb[5][:, 2:3], ones_f[:, :], sca[0:1, 6:7], start=True, stop=True))
        @blk.vector
        def _(v):
            ctx.pe.wait(v)
            ctx.dve.inc(v.tensor_copy(scol[:, 0:3], pb[5][:, 0:3]))

    for i in range(NI):
        with blk_new() as blk:
            @blk.vector
            def _(v, i=i):
                ctx.act.wait(v)
                ctx.pool.wait(v)
                ctx.dve.inc(v.tensor_scalar(qs1[:, 0:TOWN], xln_v[:, i, :],
                                            scol[:, 0:1], None, ALU.mult))
            @blk.scalar
            def _(sc, i=i):
                ctx.dve.wait(sc)
                ctx.act.inc(sc.activation(qs2[:, 0:TOWN], qs1[:, 0:TOWN], AF.Identity,
                                          scale=DK))
            @blk.gpsimd
            def _(g, i=i):
                ctx.act.wait(g)
                ctx.pool.inc(g.tensor_sub(qs2[:, TOWN:2 * TOWN], qs2[:, 0:TOWN],
                                          qs1[:, 0:TOWN]))
            @blk.vector
            def _(v, i=i):
                ctx.pool.wait(v)
                ctx.dve.inc(v.tensor_sub(xq_v[:, i, :], qs2[:, 0:TOWN],
                                         qs2[:, TOWN:2 * TOWN]))

    ld_oc = [Trk(nc, "b_ldoc%d" % i) for i in range(2)]
    opj_done = {0: 0, 1: 0}
    q_done = {0: ("dve", 0), 1: ("dve", 0)}
    for oc in range(4):
        with nc.named_scope("b_oproj"), blk_new() as blk:
            par = oc % 2
            @blk.sync
            def _(s, oc=oc, par=par):
                s.wait_ge(ctx.pe.h, opj_done[par])
                ld_oc[par].inc(s.dma_start(
                    wqT2[par][:, :], io["wT_r"][oc].rearrange("(j p) d -> p j d", p=128)), 16)
            for qp in range(4):
                sl = slice(qp * 2048, (qp + 1) * 2048)
                @blk.vector
                def _(v, qp=qp, sl=sl, par=par, oc=oc):
                    ld_oc[par].wait(v)
                    if oc >= 2 and qp == 0:
                        pass
                    ctx.dve.wait(v)
                    ctx.pool.wait(v)
                    ctx.dve.inc(v.tensor_scalar(qs1[:, :], wqT2[par][:, sl],
                                                scol[:, 1:2], None, ALU.mult))
                @blk.scalar
                def _(sc, qp=qp):
                    ctx.dve.wait(sc)
                    ctx.act.inc(sc.activation(qs2[:, :], qs1[:, :], AF.Identity, scale=DK))
                @blk.gpsimd
                def _(g, qp=qp):
                    ctx.act.wait(g)
                    ctx.pool.inc(g.tensor_sub(scr[:, :], qs2[:, :], qs1[:, :]))
                @blk.vector
                def _(v, qp=qp, sl=sl, par=par):
                    ctx.pool.wait(v)
                    mk = ctx.dve.inc(v.tensor_sub(wqT2[par][:, sl], qs2[:, :], scr[:, :]))
                    if qp == 3:
                        q_done[par] = ("dve", mk)
            t_marks = {}
            a_marks = {}
            for tt in range(NT):
                bank = tt % 4
                @blk.sync
                def _(s, tt=tt, oc=oc):
                    if tt >= 2:
                        s.wait_ge(ctx.dve.h, t_marks[tt - 2])
                    ld_xr[tt % 2].inc(s.dma_start(xres[tt % 2][:, :],
                                                  io["x_own"][tt * 128:(tt + 1) * 128,
                                                              oc * 512:(oc + 1) * 512]), 16)
                @blk.tensor
                def _(t, tt=tt, bank=bank, par=par):
                    if tt == 0:
                        nm, val = q_done[par]
                        t.wait_ge(getattr(ctx, nm).h, val)
                    if tt >= 4:
                        t.wait_ge(ctx.act.h, a_marks[tt - 4])
                    for j in range(NI):
                        ins = t.matmul(pb[bank][:, :], xq_v[:, j, tt * 128:(tt + 1) * 128],
                                       wqT_v2[par][:, j, :], start=(j == 0), stop=(j == NI - 1))
                    mk = ctx.pe.inc(ins)
                    if tt == NT - 1:
                        opj_done[par] = mk
                @blk.scalar
                def _(sc, tt=tt, bank=bank):
                    ctx.pe.wait(sc)
                    if tt >= 4:
                        sc.wait_ge(ctx.dve.h, t_marks[tt - 4])
                    a_marks[tt] = ctx.act.inc(
                        sc.activation(otmp[:, (tt % 4) * 512:(tt % 4 + 1) * 512],
                                      pb[bank][:, :], AF.Identity, scale=scol[:, 2:3]))
                @blk.vector
                def _(v, tt=tt):
                    v.wait_ge(ctx.act.h, a_marks[tt])
                    ld_xr[tt % 2].wait(v)
                    v.wait_ge(ctx.ld2.h, ctx.ld2.n)
                    t_marks[tt] = ctx.dve.inc(
                        v.tensor_add(ysb[tt % 2][:, :], xres[tt % 2][:, :],
                                     otmp[:, (tt % 4) * 512:(tt % 4 + 1) * 512]))
                @blk.scalar
                def _(sc, tt=tt, oc=oc):
                    sc.wait_ge(ctx.dve.h, t_marks[tt])
                    ctx.ld2.inc(sc.dma_start(io["y"][tt * 128:(tt + 1) * 128,
                                                     oc * 512:(oc + 1) * 512],
                                             ysb[tt % 2][:, :]), 16)
    with blk_new() as blk:
        @blk.scalar
        def _(sc):
            ctx.ld2.wait(sc)


def emit_l1_ffn(ctx, blk_new, io):
    nc = ctx.nc
    FS, FE, E = 1536, 768, 8
    pb = [ctx.ps("d_p%d" % i, [128, 512], F32) for i in range(8)]
    ld_x = Trk(nc, "d_ld_x")
    ld_r = Trk(nc, "d_ld_r")
    ld_w = [Trk(nc, "d_ld_w%d" % i) for i in range(4)]
    ld_xr = [Trk(nc, "d_ld_xr%d" % i) for i in range(2)]

    identf = ctx.sb("d_identf", [128, 128], F32)
    ctx.eps_col = ctx.sb("d_eps", [128, 1], F32)
    xln = ctx.sb("d_xln", [128, NI * TOWN], BF16)
    rtr = ctx.sb("d_rtr", [128, NI * 8], BF16)
    lgt = ctx.sb("d_lgt", [128, 8], F32)
    pstat = ctx.sb("d_pstat", [128, 8], F32)
    probs = ctx.sb("d_probs", [128, 8], F32)
    keep = ctx.sb("d_keep", [128, 8], F32)
    cw = ctx.sb("d_cw", [128, NT * 8], F32)
    gbuf = ctx.sb("d_gbuf", [128, NT * 512], F32)
    ubuf = ctx.sb("d_ubuf", [128, NT * 512], F32)
    xstage = ubuf[:, 0:D]
    scr = gbuf[:, 0:D]
    hsh = ctx.sb("d_hsh", [128, (FS // 128) * TOWN], BF16)
    hex_ = [ctx.sb("d_hex%d" % e, [128, (FE // 128) * TOWN], BF16) for e in range(E)]
    wslc = [ctx.sb("d_wslc%d" % i, [128, 512], BF16) for i in range(4)]
    xres = [ctx.sb("d_xres%d" % i, [128, 512], F32) for i in range(2)]
    ysb = [ctx.sb("d_ysb%d" % i, [128, 512], F32) for i in range(2)]

    xln_v = xln[:, :].rearrange("p (i c) -> p i c", c=TOWN)
    hsh_v = hsh[:, :].rearrange("p (f c) -> p f c", c=TOWN)
    hex_v = [h[:, :].rearrange("p (f c) -> p f c", c=TOWN) for h in hex_]

    with blk_new() as blk:
        @blk.gpsimd
        def _(g):
            ctx.pool.inc(g.memset(identf[:, :], 1.0))
            g.memset(ctx.eps_col[:, :], 1e-5)
            ctx.pool.wait(g)
            g.affine_select(identf[:, :], identf[:, :], [[1, 128]], ALU.is_equal, 0.0,
                            base=0, channel_multiplier=-1)
        @blk.gpsimd
        def _(g):
            ld_r.inc(g.dma_start(rtr[:, :].rearrange("p (i c) -> p i c", c=8),
                                 io["router"].rearrange("(i p) c -> p i c", p=128)), 16)
            ld_r.wait(g)

    tilesD = [(io["x_own"][t * 128:(t + 1) * 128, :], t * 128) for t in range(NT)]
    emit_ln_pipe(ctx, blk_new, tilesD, xln, TOWN, F32, [pb[6], pb[7]], 4, identf,
                 ld_x, scope="d_ln", xs=[ubuf[:, 0:D], ubuf[:, D:2 * D]],
                 tms=[gbuf[:, 0:D], gbuf[:, D:2 * D]])
    for t in range(NT):
        with blk_new() as blk:
            @blk.tensor
            def _(tt, t=t):
                ld_r.wait(tt)
                ctx.act.wait(tt)
                for j in range(NI):
                    ins = tt.matmul(pb[5][:, 0:8], xln_v[:, j, t * 128:(t + 1) * 128],
                                    rtr[:, j * 8:(j + 1) * 8],
                                    start=(j == 0), stop=(j == NI - 1))
                ctx.pe.inc(ins)
            @blk.vector
            def _(v, t=t):
                ctx.pe.wait(v)
                ctx.dve.wait(v)
                def step(ins):
                    ctx.dve.inc(ins)
                    ctx.dve.wait(v)
                step(v.tensor_copy(lgt[:, :], pb[5][:, 0:8]))
                step(v.reduce_max(pstat[:, 0:1], lgt[:, :], axis=AX.X))
                step(v.tensor_scalar(pstat[:, 1:2], pstat[:, 0:1], -1.0, None, ALU.mult))
            @blk.scalar
            def _(sc, t=t):
                ctx.dve.wait(sc)
                ctx.act.inc(sc.activation(probs[:, :], lgt[:, :], AF.Exp,
                                          bias=pstat[:, 1:2], accum_out=pstat[:, 2:3]))
            @blk.vector
            def _(v, t=t):
                ctx.act.wait(v)
                def step(ins):
                    ctx.dve.inc(ins)
                    ctx.dve.wait(v)
                step(v.reciprocal(pstat[:, 3:4], pstat[:, 2:3]))
                step(v.tensor_scalar(probs[:, :], probs[:, :], pstat[:, 3:4], None, ALU.mult))
                step(v.reduce_max(pstat[:, 4:5], probs[:, :], axis=AX.X))
                step(v.tensor_scalar(keep[:, :], probs[:, :], pstat[:, 4:5], None, ALU.is_ge))
                step(v.scalar_tensor_tensor(keep[:, :], keep[:, :], -1e30, probs[:, :],
                                            ALU.mult, ALU.add))
                step(v.reduce_max(pstat[:, 5:6], keep[:, :], axis=AX.X))
                step(v.tensor_scalar(keep[:, :], probs[:, :], pstat[:, 5:6], None, ALU.is_ge))
                step(v.tensor_mul(cw[:, t * 8:(t + 1) * 8], probs[:, :], keep[:, :]))

    def gated_block(w_gate_sl, w_up_sl, cw_col, hdst_v, f0_tiles, fcw, fc_off, tag):
        with nc.named_scope("d_gated"), blk_new() as blk:
            pem = {}
            for mat, (wsl, buf) in enumerate([(w_gate_sl, gbuf), (w_up_sl, ubuf)]):
                for d in range(NI):
                    si = mat * NI + d
                    @blk.gpsimd
                    def _(g, d=d, mat=mat, wsl=wsl, si=si):
                        par = si % 4
                        if si >= 4:
                            g.wait_ge(ctx.pe.h, pem[si - 4])
                        ld_w[par].inc(g.dma_start(wslc[par][:, 0:fcw], wsl(d)), 16)
                    @blk.tensor
                    def _(t, d=d, mat=mat, si=si):
                        par = si % 4
                        ld_w[par].wait(t)
                        if d == 0:
                            ctx.act.wait(t)
                        for tt in range(NT):
                            ins = t.matmul(pb[tt][:, 0:fcw], xln_v[:, d, tt * 128:(tt + 1) * 128],
                                           wslc[par][:, 0:fcw],
                                           start=(d == 0), stop=(d == NI - 1))
                        pem[si] = ctx.pe.inc(ins)
                @blk.scalar
                def _(sc, mat=mat, buf=buf):
                    ctx.pe.wait(sc)
                    ctx.dve.wait(sc)
                    for tt in range(NT):
                        ctx.act.inc(sc.activation(buf[:, tt * 512:tt * 512 + fcw],
                                                  pb[tt][:, 0:fcw],
                                                  AF.Silu if mat == 0 else AF.Identity))
            @blk.vector
            def _(v):
                ctx.act.wait(v)
                for tt in range(NT):
                    gsl = gbuf[:, tt * 512:tt * 512 + fcw]
                    usl = ubuf[:, tt * 512:tt * 512 + fcw]
                    if cw_col is None:
                        ctx.dve.inc(v.tensor_mul(gsl, gsl, usl))
                    else:
                        ctx.dve.inc(v.scalar_tensor_tensor(gsl, gsl, cw_col(tt), usl,
                                                           ALU.mult, ALU.mult))
            nfs = fcw // 128
            for tt in range(NT):
                @blk.tensor
                def _(t, tt=tt):
                    ctx.dve.wait(t)
                    if tt >= 2:
                        ctx.dve.wait(t)
                    for fsub in range(nfs):
                        ins = t.transpose(pb[6 + tt % 2][:, fsub * 128:(fsub + 1) * 128],
                                          gbuf[:, tt * 512 + fsub * 128:tt * 512 + (fsub + 1) * 128],
                                          identf[:, :])
                    ctx.pe.inc(ins)
                @blk.vector
                def _(v, tt=tt):
                    ctx.pe.wait(v)
                    ctx.dve.inc(v.tensor_copy(
                        hdst_v[:, f0_tiles:f0_tiles + nfs, tt * 128:(tt + 1) * 128],
                        pb[6 + tt % 2][:, 0:nfs * 128].rearrange("p (j c) -> p j c", c=128)))

    for fc in range(3):
        gated_block(lambda d, fc=fc: io["shared_gate"][d * 128:(d + 1) * 128,
                                                       fc * 512:(fc + 1) * 512],
                    lambda d, fc=fc: io["shared_up"][d * 128:(d + 1) * 128,
                                                     fc * 512:(fc + 1) * 512],
                    None, hsh_v, fc * 4, 512, fc, "sh%d" % fc)
    for e in range(E):
        for fc, fcw in [(0, 512), (1, 256)]:
            gated_block(lambda d, e=e, fc=fc, fcw=fcw: io["w_gate"][e][d * 128:(d + 1) * 128,
                                                                      fc * 512:fc * 512 + fcw],
                        lambda d, e=e, fc=fc, fcw=fcw: io["w_up"][e][d * 128:(d + 1) * 128,
                                                                    fc * 512:fc * 512 + fcw],
                        (lambda tt, e=e: cw[:, tt * 8 + e:tt * 8 + e + 1]),
                        hex_v[e], fc * 4, fcw, fc, "e%d_%d" % (e, fc))

    for oc in range(4):
        with nc.named_scope("d_down"), blk_new() as blk:
            slices = [("shared_down", None, f) for f in range(FS // 128)]
            for e in range(E):
                slices += [("w_down", e, f) for f in range(FE // 128)]
            pem = {}
            for si, (nm, e, f) in enumerate(slices):
                @blk.gpsimd
                def _(g, si=si, nm=nm, e=e, f=f, oc=oc):
                    par = si % 4
                    if si >= 4:
                        g.wait_ge(ctx.pe.h, pem[si - 4])
                    src = (io[nm][f * 128:(f + 1) * 128, oc * 512:(oc + 1) * 512]
                           if e is None else
                           io[nm][e][f * 128:(f + 1) * 128, oc * 512:(oc + 1) * 512])
                    ld_w[par].inc(g.dma_start(wslc[par][:, :], src), 16)
                @blk.tensor
                def _(t, si=si, e=e, f=f):
                    par = si % 4
                    ld_w[par].wait(t)
                    if si == 0:
                        ctx.dve.wait(t)
                    hsrc = hsh_v if e is None else hex_v[e]
                    for tt in range(NT):
                        ins = t.matmul(pb[tt][:, :], hsrc[:, f, tt * 128:(tt + 1) * 128],
                                       wslc[par][:, :],
                                       start=(si == 0), stop=(si == len(slices) - 1))
                    pem[si] = ctx.pe.inc(ins)
            pe_done = ctx.pe.n
            t_marks = {}
            for tt in range(NT):
                @blk.sync
                def _(s, tt=tt, oc=oc):
                    if tt >= 2:
                        s.wait_ge(ctx.dve.h, t_marks[tt - 2])
                    ld_xr[tt % 2].inc(s.dma_start(xres[tt % 2][:, :],
                                                  io["x_own"][tt * 128:(tt + 1) * 128,
                                                              oc * 512:(oc + 1) * 512]), 16)
                @blk.vector
                def _(v, tt=tt, oc=oc):
                    v.wait_ge(ctx.pe.h, pe_done)
                    ld_xr[tt % 2].wait(v)
                    v.wait_ge(ctx.ld2.h, ctx.ld2.n)
                    t_marks[tt] = ctx.dve.inc(v.tensor_add(ysb[tt % 2][:, :],
                                                           xres[tt % 2][:, :],
                                                           pb[tt][:, :]))
                @blk.scalar
                def _(sc, tt=tt, oc=oc):
                    sc.wait_ge(ctx.dve.h, t_marks[tt])
                    ctx.ld2.inc(sc.dma_start(io["y"][tt * 128:(tt + 1) * 128,
                                                     oc * 512:(oc + 1) * 512],
                                             ysb[tt % 2][:, :]), 16)
    with blk_new() as blk:
        @blk.scalar
        def _(sc):
            ctx.ld2.wait(sc)


def emit_ln_core(ctx, blk, src_f32, scr_f32, out_tm, ld, out_dt):
    if not hasattr(ctx, "_ln_st2"):
        ctx._ln_st2 = ctx.sb("ln_stats2", [128, 8], F32)
    st = ctx._ln_st2
    rs, mean, sumsq, var, rstd, nb = (st[:, 0:1], st[:, 1:2], st[:, 2:3], st[:, 3:4],
                                      st[:, 4:5], st[:, 5:6])

    @blk.scalar
    def _(sc):
        ld.wait(sc)
        ctx.dve.wait(sc)
        ctx.act.wait(sc)
        ctx.act.inc(sc.activation(scr_f32[:, :], src_f32[:, :], AF.Square,
                                  accum_out=sumsq))

    @blk.vector
    def _(v):
        ld.wait(v)
        ctx.dve.inc(v.reduce_sum(rs, src_f32[:, :], axis=AX.X))
        ctx.dve.wait(v)
        ctx.dve.inc(v.tensor_scalar(mean[:, :], rs, 1.0 / D, None, ALU.mult))
        ctx.act.wait(v)
        ctx.dve.wait(v)
        ctx.dve.inc(v.tensor_tensor(var[:, :], mean, mean, ALU.mult))
        ctx.dve.wait(v)
        ctx.dve.inc(v.scalar_tensor_tensor(var[:, :], sumsq, 1.0 / D, var,
                                           ALU.mult, ALU.subtract))

    @blk.scalar
    def _(sc):
        ctx.dve.wait(sc)
        ctx.act.inc(sc.activation(var[:, :], var[:, :], AF.Sqrt, bias=ctx.eps_col[:, :]))

    @blk.vector
    def _(v):
        ctx.act.wait(v)
        ctx.dve.inc(v.reciprocal(rstd[:, :], var[:, :]))
        ctx.dve.wait(v)
        ctx.dve.inc(v.scalar_tensor_tensor(nb[:, :], mean, -1.0, rstd, ALU.mult, ALU.mult))

    @blk.scalar
    def _(sc):
        ctx.dve.wait(sc)
        ctx.act.inc(sc.activation(out_tm[:, :], src_f32[:, :], AF.Identity,
                                  bias=nb, scale=rstd))



def _theta_split():
    th64 = 10000.0 ** (-np.arange(0, 128, 2, dtype=np.float64) / 128.0)
    th64 = np.concatenate([th64, th64])
    hi = th64.astype(np.float32)
    lo = (th64 - hi.astype(np.float64)).astype(np.float32)
    return hi.reshape(1, 128), lo.reshape(1, 128)


def _csub():
    c = np.zeros((128, 64), dtype=ml_dtypes.bfloat16)
    for j in range(64):
        c[j, j] = 1.0
        c[64 + j, j] = -1.0
    return c


def _bias_cols(b):
    return np.ascontiguousarray(np.asarray(b, dtype=np.float32).reshape(16, 128).T)


def _wT_heads(w):
    wt = np.asarray(w, dtype=np.float32).T
    return np.ascontiguousarray(wt.reshape(2048, 16, 128).transpose(1, 0, 2))


def _wT_oc(w):
    wt = np.asarray(w, dtype=np.float32).T
    return np.ascontiguousarray(wt.reshape(2048, 4, 512).transpose(1, 0, 2))


def _perm_rows(w):
    w = np.asarray(w)
    v = w.reshape(16, 64, 2, *w.shape[1:])
    return np.ascontiguousarray(v.transpose(0, 2, 1, *range(3, v.ndim)).reshape(w.shape))


def build_l0_attn():
    nc = bass.Bass("TRN2", target_bir_lowering=False)
    io = {}
    io["x_own"] = nc.dram_tensor("x_own", [TOWN, D], F32, kind="ExternalInput").ap()
    io["x_oth"] = nc.dram_tensor("x_oth", [TOWN, D], F32, kind="ExternalInput").ap()
    for nm in ["wqT_p", "wkT_p", "wvT"]:
        io[nm] = nc.dram_tensor(nm, [H, D, 128], F32, kind="ExternalInput").ap()
    io["woT_r"] = nc.dram_tensor("woT_r", [4, D, 512], F32, kind="ExternalInput").ap()
    for nm in ["bq_c", "bk_c", "bv_c"]:
        io[nm] = nc.dram_tensor(nm, [128, H], F32, kind="ExternalInput").ap()
    io["bo"] = nc.dram_tensor("bo", [D], F32, kind="ExternalInput").ap()
    io["pos_all"] = nc.dram_tensor("pos_all", [1, TSEQ], F32, kind="ExternalInput").ap()
    io["th_hi"] = nc.dram_tensor("th_hi", [1, 128], F32, kind="ExternalInput").ap()
    io["th_lo"] = nc.dram_tensor("th_lo", [1, 128], F32, kind="ExternalInput").ap()
    io["csub"] = nc.dram_tensor("csub", [128, 64], BF16, kind="ExternalInput").ap()
    io["y"] = nc.dram_tensor("y", [TOWN, D], F32, kind="ExternalOutput").ap()
    ctx = Ctx(nc)
    emit_l0_attn(ctx, nc.Block, io)
    return nc


def run_l0_attn(x_full, wq, bq, wk, bk, wv, bv, wo, bo):
    th_hi, th_lo = _theta_split()
    wqT_p, wkT_p = _wT_heads(_perm_rows(wq)), _wT_heads(_perm_rows(wk))
    wvT, woT_r = _wT_heads(wv), _wT_oc(wo)
    bq_c = _bias_cols(_perm_rows(bq))
    bk_c = _bias_cols(_perm_rows(bk))
    bv_c = _bias_cols(np.asarray(bv))
    nc = build_l0_attn()
    in_maps = []
    for c in range(8):
        own = x_full[c * 1024:(c + 1) * 1024]
        oth = x_full[(c ^ 1) * 1024:((c ^ 1) + 1) * 1024]
        pos_own = (c % 2) * 1024 + np.arange(1024, dtype=np.float32)
        pos_oth = ((c ^ 1) % 2) * 1024 + np.arange(1024, dtype=np.float32)
        pos_all = np.concatenate([pos_own, pos_oth]).reshape(1, 2048)
        in_maps.append(dict(x_own=own, x_oth=oth, wqT_p=wqT_p, wkT_p=wkT_p,
                            wvT=wvT, woT_r=woT_r,
                            bq_c=bq_c, bk_c=bk_c, bv_c=bv_c, bo=bo, pos_all=pos_all,
                            th_hi=th_hi, th_lo=th_lo, csub=_csub()))
    res = run_bass_kernel_spmd(nc, in_maps, core_ids=list(range(8)))
    return np.concatenate([r["y"] for r in res.results], axis=0)


def build_l0_ffn(ncores=8):
    nc = bass.Bass("TRN2", target_bir_lowering=False)
    io = {}
    io["x_own"] = nc.dram_tensor("x_own", [TOWN, D], F32, kind="ExternalInput").ap()
    io["w"] = nc.dram_tensor("w", [D, D], BF16, kind="ExternalInput").ap()
    io["wT_r"] = nc.dram_tensor("wT_r", [4, D, 512], BF16, kind="ExternalInput").ap()
    io["y"] = nc.dram_tensor("y", [TOWN, D], F32, kind="ExternalOutput").ap()
    io["cc_in"] = nc.dram_tensor("cc_in", [1, 1], F32).ap()
    io["cc_out"] = nc.dram_tensor("cc_out", [1, 1], F32).ap()
    ctx = Ctx(nc)
    emit_l0_ffn(ctx, nc.Block, io, ncores=ncores)
    return nc


def _wT_oc_bf(w):
    wt = np.asarray(w).T
    return np.ascontiguousarray(wt.reshape(2048, 4, 512).transpose(1, 0, 2))


def run_l0_ffn(x_full, w_bf16):
    nc = build_l0_ffn(8)
    in_maps = [dict(x_own=x_full[c * 1024:(c + 1) * 1024], w=w_bf16,
                    wT_r=_wT_oc_bf(w_bf16)) for c in range(8)]
    res = run_bass_kernel_spmd(nc, in_maps, core_ids=list(range(8)))
    return np.concatenate([r["y"] for r in res.results], axis=0)


def _cadd():
    c = np.zeros((128, 64), dtype=ml_dtypes.bfloat16)
    for j in range(64):
        c[j, j] = 1.0
        c[64 + j, j] = 1.0
    return c


def build_l1_attn():
    nc = bass.Bass("TRN2", target_bir_lowering=False)
    io = {}
    io["x_own"] = nc.dram_tensor("x_own", [TOWN, D], F32, kind="ExternalInput").ap()
    io["x_prev"] = nc.dram_tensor("x_prev", [256, D], F32, kind="ExternalInput").ap()
    for nm in ["WqT_p", "WkT_p", "WvT1"]:
        io[nm] = nc.dram_tensor(nm, [H, D, 128], F32, kind="ExternalInput").ap()
    io["WoT_r"] = nc.dram_tensor("WoT_r", [4, D, 512], F32, kind="ExternalInput").ap()
    io["pos_all"] = nc.dram_tensor("pos_all", [1, 1536], F32, kind="ExternalInput").ap()
    io["th_hi"] = nc.dram_tensor("th_hi", [1, 128], F32, kind="ExternalInput").ap()
    io["th_lo"] = nc.dram_tensor("th_lo", [1, 128], F32, kind="ExternalInput").ap()
    io["csub"] = nc.dram_tensor("csub", [128, 64], BF16, kind="ExternalInput").ap()
    io["cadd"] = nc.dram_tensor("cadd", [128, 64], BF16, kind="ExternalInput").ap()
    io["prevflag"] = nc.dram_tensor("prevflag", [128, 1], F32, kind="ExternalInput").ap()
    io["y"] = nc.dram_tensor("y", [TOWN, D], F32, kind="ExternalOutput").ap()
    ctx = Ctx(nc)
    emit_l1_attn(ctx, nc.Block, io)
    return nc


def _l1_in_map(x_full, c, WqT_p, WkT_p, WvT1, WoT_r, th_hi, th_lo):
    own = x_full[c * 1024:(c + 1) * 1024]
    if c % 2 == 1:
        x_prev = x_full[c * 1024 - 256:c * 1024]
        pos_prev = 768.0 + np.arange(256, dtype=np.float32)
        flag = np.ones((128, 1), np.float32)
    else:
        x_prev = np.zeros((256, D), np.float32)
        pos_prev = np.arange(256, dtype=np.float32)
        flag = np.zeros((128, 1), np.float32)
    pos_own = (c % 2) * 1024 + np.arange(1024, dtype=np.float32)
    pos_all = np.concatenate([pos_prev, pos_own,
                              np.zeros(256, np.float32)]).reshape(1, 1536)
    return dict(x_own=own, x_prev=x_prev, WqT_p=WqT_p, WkT_p=WkT_p, WvT1=WvT1,
                WoT_r=WoT_r, pos_all=pos_all, th_hi=th_hi, th_lo=th_lo, csub=_csub(),
                cadd=_cadd(), prevflag=flag)


def run_l1_attn(x_full, Wq, Wk, Wv, Wo):
    th_hi, th_lo = _theta_split()
    WqT_p, WkT_p = _wT_heads(_perm_rows(Wq)), _wT_heads(_perm_rows(Wk))
    WvT1, WoT_r = _wT_heads(Wv), _wT_oc(Wo)
    nc = build_l1_attn()
    in_maps = [_l1_in_map(x_full, c, WqT_p, WkT_p, WvT1, WoT_r, th_hi, th_lo)
               for c in range(8)]
    res = run_bass_kernel_spmd(nc, in_maps, core_ids=list(range(8)))
    return np.concatenate([r["y"] for r in res.results], axis=0)


def build_l1_ffn():
    nc = bass.Bass("TRN2", target_bir_lowering=False)
    io = {}
    io["x_own"] = nc.dram_tensor("x_own", [TOWN, D], F32, kind="ExternalInput").ap()
    io["router"] = nc.dram_tensor("router", [D, 8], F32, kind="ExternalInput").ap()
    io["shared_gate"] = nc.dram_tensor("shared_gate", [D, 1536], F32, kind="ExternalInput").ap()
    io["shared_up"] = nc.dram_tensor("shared_up", [D, 1536], F32, kind="ExternalInput").ap()
    io["shared_down"] = nc.dram_tensor("shared_down", [1536, D], F32, kind="ExternalInput").ap()
    io["w_gate"] = nc.dram_tensor("w_gate", [8, D, 768], F32, kind="ExternalInput").ap()
    io["w_up"] = nc.dram_tensor("w_up", [8, D, 768], F32, kind="ExternalInput").ap()
    io["w_down"] = nc.dram_tensor("w_down", [8, 768, D], F32, kind="ExternalInput").ap()
    io["y"] = nc.dram_tensor("y", [TOWN, D], F32, kind="ExternalOutput").ap()
    ctx = Ctx(nc)
    emit_l1_ffn(ctx, nc.Block, io)
    return nc


def run_l1_ffn(x_full, router, sg, su, sd, wg, wu, wd):
    nc = build_l1_ffn()
    in_maps = [dict(x_own=x_full[c * 1024:(c + 1) * 1024], router=router,
                    shared_gate=sg, shared_up=su, shared_down=sd,
                    w_gate=wg, w_up=wu, w_down=wd) for c in range(8)]
    res = run_bass_kernel_spmd(nc, in_maps, core_ids=list(range(8)))
    return np.concatenate([r["y"] for r in res.results], axis=0)


def build_fused(ncores=8):
    nc = bass.Bass("TRN2", target_bir_lowering=False)
    EI = lambda nm, sh, dt=F32: nc.dram_tensor(nm, sh, dt, kind="ExternalInput").ap()
    x_own = EI("x_own", [TOWN, D])
    x_oth = EI("x_oth", [TOWN, D])
    y1 = nc.dram_tensor("y1", [TOWN, D], F32).ap()
    y2 = nc.dram_tensor("y2", [TOWN, D], F32).ap()
    y3 = nc.dram_tensor("y3", [TOWN, D], F32).ap()
    prevg = nc.dram_tensor("prevg", [512, D], F32).ap()
    yout = nc.dram_tensor("y", [TOWN, D], F32, kind="ExternalOutput").ap()
    cc_in = nc.dram_tensor("cc_in", [1, 1], F32).ap()
    cc_out = nc.dram_tensor("cc_out", [1, 1], F32).ap()

    ioA = dict(x_own=x_own, x_oth=x_oth, y=y1,
               wqT_p=EI("wqT_p", [H, D, 128]), wkT_p=EI("wkT_p", [H, D, 128]),
               wvT=EI("wvT", [H, D, 128]), woT_r=EI("woT_r", [4, D, 512]),
               bq_c=EI("bq_c", [128, H]), bk_c=EI("bk_c", [128, H]),
               bv_c=EI("bv_c", [128, H]), bo=EI("bo", [D]),
               pos_all=EI("pos_allA", [1, TSEQ]),
               th_hi=EI("th_hiA", [1, 128]), th_lo=EI("th_loA", [1, 128]),
               csub=EI("csubA", [128, 64], BF16))
    ioB = dict(x_own=y1, w=EI("w8", [D, D], BF16),
               wT_r=EI("w8T_r", [4, D, 512], BF16), y=y2, cc_in=cc_in, cc_out=cc_out)
    ioC = dict(x_own=y2, x_prev=prevg[0:256, :], y=y3,
               WqT_p=EI("WqT_p", [H, D, 128]), WkT_p=EI("WkT_p", [H, D, 128]),
               WvT1=EI("WvT1", [H, D, 128]), WoT_r=EI("WoT_r", [4, D, 512]),
               pos_all=EI("pos_allC", [1, 1536]),
               th_hi=EI("th_hiC", [1, 128]), th_lo=EI("th_loC", [1, 128]),
               csub=EI("csubC", [128, 64], BF16), cadd=EI("caddC", [128, 64], BF16),
               prevflag=EI("prevflag", [128, 1]))
    ioD = dict(x_own=y3, y=yout, router=EI("router", [D, 8]),
               shared_gate=EI("shared_gate", [D, 1536]),
               shared_up=EI("shared_up", [D, 1536]),
               shared_down=EI("shared_down", [1536, D]),
               w_gate=EI("w_gate", [8, D, 768]), w_up=EI("w_up", [8, D, 768]),
               w_down=EI("w_down", [8, 768, D]))

    ctx = Ctx(nc)

    blkno = [0]

    def run_block(emit, io, **kw):
        ctx.prefix = "blk%d_" % blkno[0]
        blkno[0] += 1
        with ExitStack() as st:
            ctx.stack = st
            with nc.named_scope(ctx.prefix + emit.__name__):
                emit(ctx, nc.Block, io, **kw)
        ctx.stack = None
        ctx.prefix = ""
        for attr in ("_ln_st", "_ln_st2", "eps_col"):
            if hasattr(ctx, attr):
                delattr(ctx, attr)

    run_block(emit_l0_attn, ioA)
    run_block(emit_l0_ffn, ioB, ncores=ncores)
    ccg = Trk(nc, "ccg")
    groups = [[2 * i, 2 * i + 1] for i in range(ncores // 2)]
    with nc.Block() as blk:
        @blk.gpsimd
        def _(g):
            g.wait_ge(ctx.ld2.h, ctx.ld2.n)
            ccg.inc(g.collective_compute("AllGather", ALU.bypass, groups,
                                         [y2[768:1024, :].opt()], [prevg.opt()]))
            ccg.wait(g)
    run_block(emit_l1_attn, ioC)
    run_block(emit_l1_ffn, ioD)
    return nc


def kernel(L0_attn_x, L0_attn_wq, L0_attn_bq, L0_attn_wk, L0_attn_bk, L0_attn_wv,
           L0_attn_bv, L0_attn_wo, L0_attn_bo, L0_ffn_weight, L1_attn_Wq, L1_attn_Wk,
           L1_attn_Wv, L1_attn_Wo, L1_ffn_router, L1_ffn_shared_gate, L1_ffn_shared_up,
           L1_ffn_shared_down, L1_ffn_w_gate, L1_ffn_w_up, L1_ffn_w_down):
    x = np.ascontiguousarray(np.asarray(L0_attn_x, dtype=np.float32).reshape(8192, 2048))
    th_hi, th_lo = _theta_split()
    wqT_p, wkT_p = _wT_heads(_perm_rows(L0_attn_wq)), _wT_heads(_perm_rows(L0_attn_wk))
    wvT, woT_r = _wT_heads(L0_attn_wv), _wT_oc(L0_attn_wo)
    bq_c = _bias_cols(_perm_rows(L0_attn_bq))
    bk_c = _bias_cols(_perm_rows(L0_attn_bk))
    bv_c = _bias_cols(L0_attn_bv)
    WqT_p = _wT_heads(_perm_rows(L1_attn_Wq))
    WkT_p = _wT_heads(_perm_rows(L1_attn_Wk))
    WvT1, WoT_r = _wT_heads(L1_attn_Wv), _wT_oc(L1_attn_Wo)
    w8T_r = _wT_oc_bf(np.asarray(L0_ffn_weight))
    nc = build_fused(8)
    in_maps = []
    for c in range(8):
        pos_own = (c % 2) * 1024 + np.arange(1024, dtype=np.float32)
        pos_oth = ((c ^ 1) % 2) * 1024 + np.arange(1024, dtype=np.float32)
        posA = np.concatenate([pos_own, pos_oth]).reshape(1, 2048)
        if c % 2 == 1:
            pos_prev = 768.0 + np.arange(256, dtype=np.float32)
            flag = np.ones((128, 1), np.float32)
        else:
            pos_prev = np.arange(256, dtype=np.float32)
            flag = np.zeros((128, 1), np.float32)
        posC = np.concatenate([pos_prev, pos_own,
                               np.zeros(256, np.float32)]).reshape(1, 1536)
        in_maps.append(dict(
            x_own=x[c * 1024:(c + 1) * 1024],
            x_oth=x[(c ^ 1) * 1024:((c ^ 1) + 1) * 1024],
            wqT_p=wqT_p, wkT_p=wkT_p, wvT=wvT, woT_r=woT_r,
            bq_c=bq_c, bk_c=bk_c, bv_c=bv_c, bo=np.asarray(L0_attn_bo),
            pos_allA=posA, th_hiA=th_hi, th_loA=th_lo, csubA=_csub(),
            w8=np.asarray(L0_ffn_weight), w8T_r=w8T_r,
            WqT_p=WqT_p, WkT_p=WkT_p, WvT1=WvT1, WoT_r=WoT_r,
            pos_allC=posC, th_hiC=th_hi, th_loC=th_lo, csubC=_csub(), caddC=_cadd(),
            prevflag=flag,
            router=np.asarray(L1_ffn_router),
            shared_gate=np.asarray(L1_ffn_shared_gate),
            shared_up=np.asarray(L1_ffn_shared_up),
            shared_down=np.asarray(L1_ffn_shared_down),
            w_gate=np.asarray(L1_ffn_w_gate), w_up=np.asarray(L1_ffn_w_up),
            w_down=np.asarray(L1_ffn_w_down)))
    res = run_bass_kernel_spmd(nc, in_maps, core_ids=list(range(8)))
    y = np.concatenate([r["y"] for r in res.results], axis=0)
    return y.reshape(4, 2048, 2048)
